# version 13
# baseline (speedup 1.0000x reference)
import contextlib
import math
import numpy as np
import ml_dtypes
import concourse.bass as bass
import concourse.mybir as mybir
from concourse.bass import IndirectOffsetOnAxis
from concourse.bass_utils import run_bass_kernel_spmd

F32 = mybir.dt.float32
BF16 = mybir.dt.bfloat16
I32 = mybir.dt.int32
U32 = mybir.dt.uint32
AF = mybir.ActivationFunctionType
ALU = mybir.AluOpType

B, T, D = 4, 4096, 1024
NH, DS, HD = 16, 128, 64
NT, TH, NTH = 32, 2048, 16
HID = 3072
CIN = 2384
EPS = 1e-5
TWO_PI = 2.0 * math.pi
MAGIC = 12582912.0
HALF_PI = 0.5 * math.pi
DEQ = 1.0 / 127.0

DEBUG = False
PHASES = ("AI", "AII", "AV", "B", "C")
POLL_CAP = 3000000

CBLK = [(0, 512), (512, 1024), (1024, 1536), (1536, 2048), (2048, 2384)]


def build_nc():
    nc = bass.Bass("TRN2", target_bir_lowering=False)
    dbgk = "ExternalOutput" if DEBUG else "Internal"

    def inp(name, shape, dt):
        return nc.dram_tensor(name, shape, dt, kind="ExternalInput").ap()

    u_full = inp("u_full", [T, D], F32)
    u_half = inp("u_half", [TH, D], F32)
    w_in = inp("w_in", [D, CIN], F32)
    w_out = inp("w_out", [D, D], F32)
    wg_i = inp("wg_i", [HID, D], I32)
    wu_i = inp("wu_i", [HID, D], I32)
    wd_i = inp("wd_i", [D, HID], I32)
    bb_bc = inp("bb_bc", [128, NH * DS], F32)
    cb_bc = inp("cb_bc", [128, NH * DS], F32)
    bbt_in = inp("bbt_in", [DS, NH], F32)
    cbt_in = inp("cbt_in", [DS, NH], F32)
    bnw_bc = inp("bnw_bc", [128, DS], F32)
    cnw_bc = inp("cnw_bc", [128, DS], F32)
    dtb_bc = inp("dtb_bc", [128, NH], F32)
    d_bc = inp("d_bc", [128, NH], F32)
    ident_in = inp("ident_in", [128, 128], BF16)
    cum_in = inp("cum_in", [128, 128], F32)
    su_in = inp("su_in", [128, 128], F32)
    strict_in = inp("strict_in", [128, 128], BF16)
    sd_in = inp("sd_in", [128, 128], F32)
    oneh_in = inp("oneh_in", [1, 128], F32)
    onesc_in = inp("onesc_in", [128, 1], F32)
    aones_in = inp("aones_in", [1, 128], F32)
    scat_in = inp("scat_in", [128, NT], U32)
    gown_in = inp("gown_in", [128, NTH], U32)
    gpar_in = inp("gpar_in", [128, NTH], U32)
    fscat_in = inp("fscat_in", [128, 1], U32)
    nonce_in = inp("nonce_in", [1, 1], I32)
    noncet_in = inp("noncet_in", [128, 4], I32)

    def scr(name, shape, dt, dbg=False):
        return nc.dram_tensor(name, shape, dt, kind=(dbgk if dbg else "Internal")).ap()

    xn_dram = scr("xn_dram", [T, D], BF16, True)
    xnh_dram = scr("xnh_dram", [TH, D], BF16, True)
    x_dram = scr("x_dram", [T, NH * HD], BF16, True)
    z_dram = scr("z_dram", [T, NH * HD], BF16, True)
    k_dram = scr("k_dram", [T, NH * DS], BF16, True)
    q_dram = scr("q_dram", [T, NH * DS], BF16, True)
    yg_dram = scr("yg_dram", [T, NH * HD], BF16, True)
    g_dram = scr("g_dram", [TH, D], BF16, True)
    wg_bf = scr("wg_bf", [HID, D], BF16)
    wu_bf = scr("wu_bf", [HID, D], BF16)
    wd_bf = scr("wd_bf", [D, HID], BF16)
    wgt_dram = scr("wgt_dram", [24, 8, 128, 128], BF16)
    wdt_dram = scr("wdt_dram", [HID, 1024], BF16)
    wut_dram = scr("wut_dram", [24, 8, 128, 128], BF16)
    xchg = nc.dram_tensor("xchg", [2 * T, D], F32, addr_space="Shared", kind="Internal").ap()
    flags = nc.dram_tensor("flags", [2 * 128, 4], I32, addr_space="Shared", kind="Internal").ap()
    sdbg = scr("sdbg", [T, D], F32, True) if DEBUG else None
    y_out = nc.dram_tensor("y_out", [TH, D], F32, kind="ExternalOutput").ap()

    st = contextlib.ExitStack()

    def sb(name, shape, dt):
        return st.enter_context(nc.sbuf_tensor(name, shape, dt))

    ident_sb = sb("ident_sb", [128, 128], BF16)
    cum_sb = sb("cum_sb", [128, 128], F32)
    su_sb = sb("su_sb", [128, 128], F32)
    strict_sb = sb("strict_sb", [128, 128], BF16)
    sd_sb = sb("sd_sb", [128, 128], F32)
    oneh_sb = sb("oneh_sb", [1, 128], F32)
    onesc_sb = sb("onesc_sb", [128, 1], F32)
    aones_sb = sb("aones_sb", [1, 128], F32)
    aones_bf = sb("aones_bf", [1, 128], BF16)
    mhalf = sb("mhalf", [128, 1], F32)
    halfpi_sb = sb("halfpi_sb", [128, 1], F32)
    bnw_sb = sb("bnw_sb", [128, DS], F32)
    cnw_sb = sb("cnw_sb", [128, DS], F32)
    dtb_sb = sb("dtb_sb", [128, NH], F32)
    dbc_sb = sb("dbc_sb", [128, NH], F32)
    bbt_bf = sb("bbt_bf", [DS, NH], BF16)
    cbt_bf = sb("cbt_bf", [DS, NH], BF16)
    t4p_sb = sb("t4p_sb", [DS, NH], F32)
    t4row_f = sb("t4row_f", [1, NH], F32)
    t4row_bf = sb("t4row_bf", [1, NH], BF16)
    qk_sb = sb("qk_sb", [128, NT * NH], F32)
    coef_sb = sb("coef_sb", [128, NT * NH], F32)
    gcx_sb = sb("gcx_sb", [128, NT * NH], F32)
    wsg_sb = sb("wsg_sb", [128, NT * NH], F32)
    expell_sb = sb("expell_sb", [128, NT * NH], F32)
    cdb_sb = sb("cdb_sb", [128, NT * NH], F32)
    H_sb = sb("H_sb", [128, 1024], F32)
    Hbf_sb = sb("Hbf_sb", [128, 1024], BF16)
    Hs_sb = sb("Hs_sb", [128, 1024], F32)
    sidx_sb = sb("sidx_sb", [128, NT], U32)
    goidx_sb = sb("goidx_sb", [128, NTH], U32)
    gpidx_sb = sb("gpidx_sb", [128, NTH], U32)
    fidx_sb = sb("fidx_sb", [128, 1], U32)
    nonce_sb = sb("nonce_sb", [1, 1], I32)
    nonce_t_sb = sb("nonce_t_sb", [128, 4], I32)
    flagrd_sb = sb("flagrd_sb", [2, 4], I32)

    S = {}
    names = ["fence", "s_ld", "s_t4",
             "s_cast_wd", "s_cast_win",
             "s_prep_wd", "s_prep_wg", "s_prep_wu",
             "s_sq", "s_var", "s_rstd", "s_rstd2", "s_xn",
             "s_mm", "s_ev", "s_evx", "s_sp", "s_bnx", "s_bn", "s_tp", "s_bte",
             "s_qkm", "s_qke",
             "s_v2", "s_a2", "s_tm", "s_vb", "s_tg", "s_sg", "s_vg",
             "s_vp", "s_thm", "s_vc", "s_rrv", "s_sc3", "s_ropev", "s_ropeg",
             "s_fx", "s_fmm", "s_fev", "s_fg", "s_fge", "s_fdm",
             "s_smm", "s_sw", "s_ymm", "s_wmm",
             "s_bz", "s_xwg", "s_dxp", "s_t2", "s_hsg", "s_bev1", "s_bev",
             "s_hup",
             "s_yt", "s_om", "s_se", "s_fl", "s_bar",
             "s_fin", "s_wo"]
    quad_names = ["s_u", "s_xnw"]
    pair_names = ["s_xt", "s_xzw", "s_kqw", "s_kq", "s_bx",
                  "s_fw", "s_fgw", "s_ygw", "s_sc", "s_gat", "s_uh", "s_gh",
                  "s_out"]
    for n in names:
        S[n] = st.enter_context(nc.semaphore(n))
    for n in pair_names:
        S[n] = [st.enter_context(nc.semaphore(n + "0")),
                st.enter_context(nc.semaphore(n + "1"))]
    for n in quad_names:
        S[n] = [st.enter_context(nc.semaphore(n + str(q))) for q in range(4)]
    S["s_cast_wg"] = S["s_cast_wd"]
    S["s_cast_wu"] = S["s_cast_wd"]
    S["s_cast_misc"] = S["s_cast_wd"]
    S["s_cast_bbcb"] = S["s_cast_win"]

    for n in ["s_vself", "s_aself", "s_gself"]:
        S[n] = st.enter_context(nc.semaphore(n))
    VC = {"v": 0, "a": 0, "g": 0}

    def vch(instr):
        VC["v"] += 1
        instr.then_inc(S["s_vself"], 1)
        return instr

    def vsy(e):
        e.wait_ge(S["s_vself"], VC["v"])

    def ach(instr):
        VC["a"] += 1
        instr.then_inc(S["s_aself"], 1)
        return instr

    def asy(e):
        e.wait_ge(S["s_aself"], VC["a"])

    def gch(instr):
        VC["g"] += 1
        instr.then_inc(S["s_gself"], 1)
        return instr

    def gsy(e):
        e.wait_ge(S["s_gself"], VC["g"])

    def vinc(e, sem, k=1):
        vsy(e)
        e.sem_inc(sem, k)

    def ainc(e, sem, k=1):
        asy(e)
        e.sem_inc(sem, k)

    def ginc(e, sem, k=1):
        gsy(e)
        e.sem_inc(sem, k)

    FN = [0]

    def fence(e):
        e.sem_inc(S["fence"], 1)
        e.wait_ge(S["fence"], 5 * FN[0])

    with nc.psum_tensor("ps_t4", [1, NH], F32) as ps_t4:
        with nc.Block("startup") as blk:
            @blk.sync
            def _(e):
                for dst, src in [(bnw_sb, bnw_bc), (cnw_sb, cnw_bc), (dtb_sb, dtb_bc),
                                 (dbc_sb, d_bc), (ident_sb, ident_in), (cum_sb, cum_in),
                                 (su_sb, su_in), (strict_sb, strict_in), (sd_sb, sd_in),
                                 (oneh_sb, oneh_in), (onesc_sb, onesc_in),
                                 (aones_sb, aones_in), (sidx_sb, scat_in),
                                 (goidx_sb, gown_in), (gpidx_sb, gpar_in),
                                 (fidx_sb, fscat_in), (nonce_sb, nonce_in),
                                 (nonce_t_sb, noncet_in), (t4p_sb, bbt_in)]:
                    e.dma_start(out=dst[:], in_=src).then_inc(S["s_ld"], 16)
                fence(e)

            @blk.gpsimd
            def _(e):
                e.dma_start(out=bbt_bf[:], in_=bbt_in).then_inc(S["s_cast_misc"], 16)
                e.dma_start(out=cbt_bf[:], in_=cbt_in).then_inc(S["s_cast_misc"], 16)
                e.dma_start(out=aones_bf[:], in_=aones_in).then_inc(S["s_cast_misc"], 16)
                e.wait_ge(S["s_cast_misc"], 16 * 3)
                for (srcw, dstw, sw) in [(wd_i, wd_bf, "s_cast_wd"),
                                         (wg_i, wg_bf, "s_cast_wg"),
                                         (wu_i, wu_bf, "s_cast_wu")]:
                    nrow = srcw.shape[0]
                    q = nrow // 4
                    for j in range(4):
                        e.dma_start(out=dstw[j * q:(j + 1) * q, :],
                                    in_=srcw[j * q:(j + 1) * q, :]).then_inc(S[sw], 16)
                e.memset(mhalf[:], -0.5)
                e.memset(halfpi_sb[:], HALF_PI)
                e.memset(Hs_sb[:], 0.0)
                fence(e)

            @blk.vector
            def _(e):
                e.wait_ge(S["s_ld"], 16 * 19)
                e.wait_ge(S["s_cast_misc"], 16 * 3)
                e.scalar_tensor_tensor(out=t4p_sb[:], in0=t4p_sb[:], scalar=1.0,
                                       in1=cbt_bf[:], op0=ALU.mult,
                                       op1=ALU.mult).then_inc(S["s_t4"], 1)
                e.memset(H_sb[:], 0.0)
                e.memset(Hbf_sb[:], 0.0)
                e.wait_ge(S["s_t4"], 2)
                vch(e.tensor_copy(out=t4row_f[:], in_=ps_t4[:]))
                vsy(e)
                e.tensor_copy(out=t4row_bf[:], in_=t4row_f[:]).then_inc(S["s_t4"], 1)
                fence(e)

            @blk.tensor
            def _(e):
                e.wait_ge(S["s_t4"], 1)
                e.matmul(out=ps_t4[:], lhsT=onesc_sb[:], rhs=t4p_sb[:],
                         start=True, stop=True).then_inc(S["s_t4"], 1)
                fence(e)

            @blk.scalar
            def _(e):
                fence(e)
        FN[0] += 1

    with nc.sbuf_tensor("bn_sb", [128, NT * 256], BF16) as bn_sb, \
         nc.sbuf_tensor("smalls_sb", [128, NT * 80], F32) as smalls_sb:

      if "AI" in PHASES:
        with contextlib.ExitStack() as sAI:
            def a(name, shape, dt):
                return sAI.enter_context(nc.sbuf_tensor(name, shape, dt))

            win_sb = a("win_sb", [128, 8 * CIN], BF16)
            u_ring = a("u_ring", [128, 4 * D], F32)
            sq_scr = a("sq_scr", [128, D], F32)
            cp_scr = a("cp_scr", [128, D], F32)
            ssum_c = a("ssum_c", [128, 4], F32)
            ssq_c = a("ssq_c", [128, 4], F32)
            mean_c = a("mean_c", [128, 4], F32)
            m2_c = a("m2_c", [128, 4], F32)
            var_c = a("var_c", [128, 4], F32)
            rstd_c = a("rstd_c", [128, 4], F32)
            lnv_c = a("lnv_c", [128, 4], F32)
            lnv2_c = a("lnv2_c", [128, 2], F32)
            xn_ring = a("xn_ring", [128, 4 * D], BF16)
            xnt_ring = a("xnt_ring", [128, 2 * 8 * 512], BF16)
            x_ev = a("x_ev", [128, 2 * 1024], BF16)
            z_ev = a("z_ev", [128, 2 * 1024], BF16)
            bc_ev = a("bc_ev", [128, 256], F32)
            rstdbc = a("rstdbc", [128, 2], F32)
            tb_scr = a("tb_scr", [128, 256], F32)
            bt_sb = a("bt_sb", [128, 1024], BF16)
            junk_sb = a("junk_sb", [128, 128], F32)
            t1_col = a("t1_col", [128, 4], F32)
            ps_ip = [sAI.enter_context(nc.psum_tensor(f"ps_ip{i}", [128, 512], F32))
                     for i in range(4)]
            ps_ip.append(sAI.enter_context(nc.psum_tensor("ps_ip4", [128, 336], F32)))
            ps_bt = sAI.enter_context(nc.psum_tensor("ps_bt", [128, 1024], BF16))
            ps_qk = sAI.enter_context(nc.psum_tensor("ps_qk", [128, 4 * NH], F32))

            with nc.Block("AI") as blk:
                @blk.gpsimd
                def _(e):
                    for dc in range(8):
                        e.dma_start(out=win_sb[:, dc * CIN:(dc + 1) * CIN],
                                    in_=w_in[dc * 128:(dc + 1) * 128, :]
                                    ).then_inc(S["s_cast_win"], 16)
                    def xn_store(m):
                        e.wait_ge(S["s_xn"], m + 1)
                        dst = xn_dram if m < NT else xnh_dram
                        mm_ = m if m < NT else m - NT
                        e.dma_start(out=dst[mm_ * 128:(mm_ + 1) * 128, :],
                                    in_=xn_ring[:, (m % 4) * D:(m % 4 + 1) * D]
                                    ).then_inc(S["s_xnw"][m % 4], 16)

                    def xz_store(m):
                        e.wait_ge(S["s_evx"], m + 1)
                        e.dma_start(out=x_dram[m * 128:(m + 1) * 128, :],
                                    in_=x_ev[:, (m % 2) * 1024:(m % 2 + 1) * 1024]
                                    ).then_inc(S["s_xzw"][m % 2], 16)
                        e.wait_ge(S["s_ev"], m + 1)
                        e.dma_start(out=z_dram[m * 128:(m + 1) * 128, :],
                                    in_=z_ev[:, (m % 2) * 1024:(m % 2 + 1) * 1024]
                                    ).then_inc(S["s_xzw"][m % 2], 16)

                    for step in range(NT + 6):
                        if step >= 6:
                            xz_store(step - 6)
                        if step < NT:
                            xn_store(step)
                    for j in range(NTH):
                        xn_store(NT + j)
                    for q_ in range(4):
                        e.wait_ge(S["s_xnw"][q_], 16 * 12)
                    e.wait_ge(S["s_xzw"][0], 32 * 16)
                    e.wait_ge(S["s_xzw"][1], 32 * 16)
                    e.wait_ge(S["s_cast_win"], 16 * 8)
                    fence(e)

                @blk.sync
                def _(e):
                    for m in range(NT):
                        if m >= 4:
                            e.wait_ge(S["s_xn"], m - 3)
                        e.dma_start(out=u_ring[:, (m % 4) * D:(m % 4 + 1) * D],
                                    in_=u_full[m * 128:(m + 1) * 128, :]).then_inc(S["s_u"][m % 4], 16)
                        if m % 4 == 3:
                            s4 = m // 4
                            for q_ in range(4):
                                e.wait_ge(S["s_xnw"][q_], 16 * (s4 + 1))
                            if s4 >= 2:
                                e.wait_ge(S["s_mm"], 4 * (s4 - 1))
                            for dc in range(8):
                                base = ((s4 % 2) * 8 + dc) * 512
                                e.dma_start(
                                    out=xnt_ring[:, base:base + 512],
                                    in_=xn_dram[s4 * 512:(s4 + 1) * 512,
                                                dc * 128:(dc + 1) * 128],
                                    transpose=True).then_inc(S["s_xt"][s4 % 2], 16)
                    for j in range(NTH):
                        m = NT + j
                        e.wait_ge(S["s_xn"], m - 3)
                        e.dma_start(out=u_ring[:, (m % 4) * D:(m % 4 + 1) * D],
                                    in_=u_half[j * 128:(j + 1) * 128, :]).then_inc(S["s_u"][m % 4], 16)
                    for q_ in range(4):
                        e.wait_ge(S["s_u"][q_], 16 * 12)
                    e.wait_ge(S["s_xt"][0], 128 * 4)
                    e.wait_ge(S["s_xt"][1], 128 * 4)
                    fence(e)

                @blk.scalar
                def _(e):
                    def sq(m):
                        e.wait_ge(S["s_u"][m % 4], 16 * (m // 4 + 1))
                        if m >= 4:
                            e.wait_ge(S["s_var"], m - 3)
                        asy(e)
                        ach(e.activation(out=sq_scr[:],
                                         in_=u_ring[:, (m % 4) * D:(m % 4 + 1) * D],
                                         func=AF.Square,
                                         accum_out=ssq_c[:, m % 4:m % 4 + 1]))
                        ainc(e, S["s_sq"], 1)

                    def lnexp_stats(m):
                        r = m % 4
                        e.wait_ge(S["s_var"], m + 1)
                        asy(e)
                        ach(e.activation(out=lnv_c[:, r:r + 1],
                                         in_=var_c[:, r:r + 1], func=AF.Ln))
                        asy(e)
                        ach(e.activation(out=rstd_c[:, r:r + 1],
                                         in_=lnv_c[:, r:r + 1], func=AF.Exp,
                                         scale=-0.5))
                        ainc(e, S["s_rstd"], 1)

                    def act_bn(m):
                        e.wait_ge(S["s_mm"], m + 1)
                        if m >= 2:
                            e.wait_ge(S["s_xzw"][m % 2], 32 * (m // 2))
                        e.copy(out=x_ev[:, (m % 2) * 1024:(m % 2) * 1024 + 512],
                               in_=ps_ip[0][:])
                        asy(e)
                        ach(e.copy(out=x_ev[:, (m % 2) * 1024 + 512:(m % 2 + 1) * 1024],
                                   in_=ps_ip[1][:]))
                        ainc(e, S["s_evx"], 1)
                        e.copy(out=smalls_sb[:, m * 80:(m + 1) * 80],
                               in_=ps_ip[4][:, 256:336]).then_inc(S["s_sp"], 1)
                        e.wait_ge(S["s_bnx"], m + 1)
                        asy(e)
                        ach(e.activation(out=lnv2_c[:], in_=rstdbc[:, 0:2],
                                         func=AF.Ln))
                        asy(e)
                        ach(e.activation(out=rstdbc[:, 0:2], in_=lnv2_c[:],
                                         func=AF.Exp, scale=-0.5))
                        ainc(e, S["s_rstd2"], 1)

                    for step in range(NT + 6):
                        if 1 <= step <= NT:
                            lnexp_stats(step - 1)
                        if step < NT:
                            sq(step)
                        if step >= 6:
                            act_bn(step - 6)
                    for j in range(NTH):
                        m = NT + j
                        sq(m)
                        if j >= 1:
                            lnexp_stats(m - 1)
                    lnexp_stats(NT + NTH - 1)
                    fence(e)

                @blk.vector
                def _(e):
                    def stats_xn(m):
                        e.wait_ge(S["s_u"][m % 4], 16 * (m // 4 + 1))
                        if m >= 4:
                            e.wait_ge(S["s_xnw"][m % 4], 16 * (m // 4))
                        ur = u_ring[:, (m % 4) * D:(m % 4 + 1) * D]
                        r = m % 4
                        vsy(e)
                        vch(e.tensor_scalar(out=cp_scr[:], in0=ur, scalar1=1.0,
                                            scalar2=None, op0=ALU.mult, op1=ALU.add,
                                            accum_out=ssum_c[:, r:r + 1]))
                        e.wait_ge(S["s_sq"], m + 1)
                        vsy(e)
                        vch(e.tensor_scalar(out=mean_c[:, r:r + 1], in0=ssum_c[:, r:r + 1],
                                            scalar1=1.0 / D, scalar2=None, op0=ALU.mult))
                        vsy(e)
                        vch(e.tensor_tensor(out=m2_c[:, r:r + 1], in0=mean_c[:, r:r + 1],
                                            in1=mean_c[:, r:r + 1], op=ALU.mult))
                        vch(e.tensor_scalar(out=var_c[:, r:r + 1], in0=ssq_c[:, r:r + 1],
                                            scalar1=1.0 / D, scalar2=EPS,
                                            op0=ALU.mult, op1=ALU.add))
                        vsy(e)
                        e.tensor_tensor(out=var_c[:, r:r + 1], in0=var_c[:, r:r + 1],
                                        in1=m2_c[:, r:r + 1],
                                        op=ALU.subtract).then_inc(S["s_var"], 1)
                        e.wait_ge(S["s_rstd"], m + 1)
                        vch(e.tensor_scalar(out=xn_ring[:, r * D:(r + 1) * D],
                                            in0=ur, scalar1=mean_c[:, r:r + 1],
                                            scalar2=rstd_c[:, r:r + 1],
                                            op0=ALU.subtract, op1=ALU.mult))
                        vinc(e, S["s_xn"], 1)

                    def bn_item(m):
                        s4 = m // 4
                        jj = m % 4
                        r = m % 2
                        e.wait_ge(S["s_mm"], m + 1)
                        if m >= 2:
                            e.wait_ge(S["s_xzw"][m % 2], 32 * (m // 2))
                        e.tensor_copy(out=z_ev[:, r * 1024:r * 1024 + 512], in_=ps_ip[2][:])
                        e.tensor_copy(out=z_ev[:, r * 1024 + 512:r * 1024 + 1024], in_=ps_ip[3][:])
                        vch(e.tensor_copy(out=bc_ev[:], in_=ps_ip[4][:, 0:256]))
                        vinc(e, S["s_ev"], 1)
                        vsy(e)
                        vch(e.scalar_tensor_tensor(out=junk_sb[:], in0=bc_ev[:, 0:128],
                                                   scalar=1.0 / DS, in1=bc_ev[:, 0:128],
                                                   op0=ALU.mult, op1=ALU.mult,
                                                   accum_out=rstdbc[:, 0:1]))
                        vsy(e)
                        vch(e.scalar_tensor_tensor(out=junk_sb[:], in0=bc_ev[:, 128:256],
                                                   scalar=1.0 / DS, in1=bc_ev[:, 128:256],
                                                   op0=ALU.mult, op1=ALU.mult,
                                                   accum_out=rstdbc[:, 1:2]))
                        vsy(e)
                        e.tensor_scalar(out=rstdbc[:, 0:2], in0=rstdbc[:, 0:2],
                                        scalar1=EPS, scalar2=None,
                                        op0=ALU.add).then_inc(S["s_bnx"], 1)
                        e.wait_ge(S["s_rstd2"], m + 1)
                        vch(e.tensor_scalar(out=tb_scr[:, 0:128], in0=bc_ev[:, 0:128],
                                            scalar1=rstdbc[:, 0:1], scalar2=None,
                                            op0=ALU.mult))
                        vch(e.tensor_scalar(out=tb_scr[:, 128:256], in0=bc_ev[:, 128:256],
                                            scalar1=rstdbc[:, 1:2], scalar2=None,
                                            op0=ALU.mult))
                        vsy(e)
                        vch(e.tensor_tensor(out=bn_sb[:, m * 256:m * 256 + 128],
                                            in0=tb_scr[:, 0:128], in1=bnw_sb[:],
                                            op=ALU.mult))
                        vch(e.tensor_tensor(out=bn_sb[:, m * 256 + 128:(m + 1) * 256],
                                            in0=tb_scr[:, 128:256], in1=cnw_sb[:],
                                            op=ALU.mult))
                        vinc(e, S["s_bn"], 1)
                        vsy(e)
                        vch(e.scalar_tensor_tensor(out=junk_sb[:],
                                                   in0=bn_sb[:, m * 256:m * 256 + 128],
                                                   scalar=1.0,
                                                   in1=bn_sb[:, m * 256 + 128:(m + 1) * 256],
                                                   op0=ALU.mult, op1=ALU.mult,
                                                   accum_out=t1_col[:, jj:jj + 1]))
                        if jj == 3:
                            e.wait_ge(S["s_tp"], s4 + 1)
                            vch(e.tensor_copy(out=bt_sb[:], in_=ps_bt[:]))
                            vinc(e, S["s_bte"], 1)
                            e.wait_ge(S["s_qkm"], s4 + 1)
                            vsy(e)
                            for j2 in range(4):
                                m2 = 4 * s4 + j2
                                ts = e.tensor_scalar(out=qk_sb[:, m2 * NH:(m2 + 1) * NH],
                                                     in0=ps_qk[:, j2 * NH:(j2 + 1) * NH],
                                                     scalar1=t1_col[:, j2:j2 + 1],
                                                     scalar2=None, op0=ALU.add)
                                if j2 == 3:
                                    ts.then_inc(S["s_qke"], 1)

                    for step in range(NT + 6):
                        if step >= 6:
                            bn_item(step - 6)
                        if step < NT:
                            stats_xn(step)
                    for j in range(NTH):
                        stats_xn(NT + j)
                    fence(e)

                @blk.tensor
                def _(e):
                    e.wait_ge(S["s_cast_win"], 16 * 8)
                    e.wait_ge(S["s_cast_misc"], 16 * 3)
                    for m in range(NT):
                        s4 = m // 4
                        e.wait_ge(S["s_xt"][s4 % 2], 128 * (s4 // 2 + 1))
                        if m > 0:
                            e.wait_ge(S["s_ev"], m)
                            e.wait_ge(S["s_evx"], m)
                            e.wait_ge(S["s_sp"], m)
                        base = ((s4 % 2) * 8)
                        for dc in range(8):
                            lt = xnt_ring[:, (base + dc) * 512 + (m % 4) * 128:
                                          (base + dc) * 512 + (m % 4 + 1) * 128]
                            for ci, (c0, c1) in enumerate(CBLK):
                                mm = e.matmul(out=ps_ip[ci][:, 0:(c1 - c0)], lhsT=lt,
                                              rhs=win_sb[:, dc * CIN + c0:dc * CIN + c1],
                                              start=(dc == 0), stop=(dc == 7))
                                if dc == 7 and ci == 4:
                                    mm.then_inc(S["s_mm"], 1)
                        if m % 4 == 3:
                            e.wait_ge(S["s_bn"], m + 1)
                            if s4 > 0:
                                e.wait_ge(S["s_bte"], s4)
                            for jj in range(4):
                                mj = 4 * s4 + jj
                                e.transpose(out=ps_bt[:, jj * 256:jj * 256 + 128],
                                            in_=bn_sb[:, mj * 256:mj * 256 + 128],
                                            identity=ident_sb[:])
                                tp = e.transpose(out=ps_bt[:, jj * 256 + 128:(jj + 1) * 256],
                                                 in_=bn_sb[:, mj * 256 + 128:(mj + 1) * 256],
                                                 identity=ident_sb[:])
                                if jj == 3:
                                    tp.then_inc(S["s_tp"], 1)
                            e.wait_ge(S["s_bte"], s4 + 1)
                            if s4 > 0:
                                e.wait_ge(S["s_qke"], s4)
                            e.wait_ge(S["s_t4"], 3)
                            for jj in range(4):
                                e.matmul(out=ps_qk[:, jj * NH:(jj + 1) * NH],
                                         lhsT=bt_sb[:, jj * 256:jj * 256 + 128],
                                         rhs=cbt_bf[:], start=True, stop=False)
                                e.matmul(out=ps_qk[:, jj * NH:(jj + 1) * NH],
                                         lhsT=bt_sb[:, jj * 256 + 128:(jj + 1) * 256],
                                         rhs=bbt_bf[:], start=False, stop=False)
                                mmq = e.matmul(out=ps_qk[:, jj * NH:(jj + 1) * NH],
                                               lhsT=aones_bf[:], rhs=t4row_bf[:],
                                               start=False, stop=True)
                                if jj == 3:
                                    mmq.then_inc(S["s_qkm"], 1)
                    fence(e)
        FN[0] += 1

      if "AII" in PHASES:
        with contextlib.ExitStack() as sA2:
            def a(name, shape, dt):
                return sA2.enter_context(nc.sbuf_tensor(name, shape, dt))

            dt_sb = a("dt_sb", [128, NT * NH], F32)
            adt_sb = a("adt_sb", [128, NT * NH], F32)
            beta_sb = a("beta_sb", [128, NT * NH], F32)
            betan_sb = a("betan_sb", [128, NT * NH], F32)
            gc_sb = a("gc_sb", [128, NT * NH], F32)
            g1_sb = a("g1_sb", [128, NT * NH], F32)
            dtin_sb = a("dtin_sb", [128, NT * NH], F32)
            spa_sb = a("spa_sb", [128, NT * NH], F32)
            ax_scr = a("ax_scr", [128, NT * NH], F32)
            ex_scr = a("ex_scr", [128, NT * NH], F32)
            ln_scr = a("ln_scr", [128, NT * NH], F32)
            mx_scr = a("mx_scr", [128, NT * NH], F32)
            eneg_scr = a("eneg_scr", [128, NH], F32)
            dec_scr = a("dec_scr", [128, NH], F32)
            sm3 = smalls_sb[:].rearrange("p (m c) -> p m c", c=80)
            dt3 = dt_sb[:].rearrange("p (m h) -> p m h", h=NH)

            with contextlib.ExitStack() as sG:
                ps_bn = [sG.enter_context(nc.psum_tensor(f"ps_bn{i}", [128, NH], F32))
                         for i in range(2)]
                ps_e = [sG.enter_context(nc.psum_tensor(f"ps_e{i}", [128, NH], F32))
                        for i in range(2)]
                ps_d = [sG.enter_context(nc.psum_tensor(f"ps_d{i}", [128, NH], F32))
                        for i in range(2)]

                with nc.Block("AII_gates") as blk:
                    @blk.vector
                    def _(e):
                        e.wait_ge(S["s_sp"], NT)
                        e.wait_ge(S["s_qke"], NT // 4)
                        e.tensor_tensor(out=dtin_sb[:].rearrange("p (m h) -> p m h", h=NH),
                                        in0=sm3[:, :, 0:16],
                                        in1=dtb_sb[:].unsqueeze(1).broadcast_to([128, NT, NH]),
                                        op=ALU.add).then_inc(S["s_v2"], 1)
                        e.wait_ge(S["s_a2"], 3)
                        vch(e.tensor_scalar_max(out=mx_scr[:], in0=dtin_sb[:], scalar1=0.0))
                        vsy(e)
                        vch(e.tensor_tensor(out=dt_sb[:], in0=mx_scr[:], in1=ln_scr[:],
                                            op=ALU.add))
                        vinc(e, S["s_v2"], 1)
                        e.wait_ge(S["s_a2"], 6)
                        vsy(e)
                        vch(e.tensor_tensor(out=mx_scr[:].rearrange("p (m h) -> p m h", h=NH),
                                            in0=sm3[:, :, 16:32], in1=sm3[:, :, 16:32],
                                            op=ALU.max))
                        vsy(e)
                        vch(e.tensor_scalar_max(out=mx_scr[:], in0=mx_scr[:], scalar1=0.0))
                        vsy(e)
                        vch(e.tensor_tensor(out=spa_sb[:], in0=mx_scr[:], in1=ln_scr[:],
                                            op=ALU.add))
                        vsy(e)
                        vch(e.tensor_scalar_max(out=spa_sb[:], in0=spa_sb[:], scalar1=1e-4))
                        vsy(e)
                        vch(e.scalar_tensor_tensor(out=adt_sb[:], in0=spa_sb[:], scalar=-1.0,
                                                   in1=dt_sb[:], op0=ALU.mult,
                                                   op1=ALU.mult))
                        vinc(e, S["s_v2"], 1)
                        e.wait_ge(S["s_a2"], 7)
                        vch(e.tensor_scalar(out=ex_scr[:], in0=ex_scr[:], scalar1=1.0,
                                            scalar2=None, op0=ALU.add))
                        vsy(e)
                        vch(e.reciprocal(out=beta_sb[:], in_=ex_scr[:]))
                        vinc(e, S["s_v2"], 1)
                        for m in range(NT):
                            e.wait_ge(S["s_tm"], m + 1)
                            e.tensor_copy(out=betan_sb[:, m * NH:(m + 1) * NH],
                                          in_=ps_bn[m % 2][:]).then_inc(S["s_vb"], 1)
                        vsy(e)
                        e.wait_ge(S["s_vb"], NT)
                        vch(e.scalar_tensor_tensor(out=mx_scr[:], in0=beta_sb[:], scalar=1.0,
                                                   in1=betan_sb[:], op0=ALU.add,
                                                   op1=ALU.subtract))
                        vsy(e)
                        vch(e.tensor_tensor(out=gc_sb[:], in0=mx_scr[:], in1=dt_sb[:],
                                            op=ALU.mult))
                        vch(e.tensor_tensor(out=g1_sb[:], in0=beta_sb[:], in1=dt_sb[:],
                                            op=ALU.mult))
                        vsy(e)
                        vch(e.tensor_tensor(out=mx_scr[:], in0=g1_sb[:], in1=qk_sb[:],
                                            op=ALU.mult))
                        vsy(e)
                        vch(e.tensor_tensor(out=coef_sb[:].rearrange("p (m h) -> p m h", h=NH),
                                            in0=mx_scr[:].rearrange("p (m h) -> p m h", h=NH),
                                            in1=dbc_sb[:].unsqueeze(1).broadcast_to([128, NT, NH]),
                                            op=ALU.add))
                        vinc(e, S["s_v2"], 1)
                        vsy(e)
                        for m in range(NT):
                            e.wait_ge(S["s_sg"], m + 1)
                            e.tensor_tensor(out=gcx_sb[:, m * NH:(m + 1) * NH],
                                            in0=gc_sb[:, m * NH:(m + 1) * NH],
                                            in1=eneg_scr[:], op=ALU.mult)
                            e.tensor_tensor(out=wsg_sb[:, m * NH:(m + 1) * NH],
                                            in0=gc_sb[:, m * NH:(m + 1) * NH],
                                            in1=dec_scr[:], op=ALU.mult)
                            e.tensor_tensor(out=cdb_sb[:, m * NH:(m + 1) * NH],
                                            in0=dec_scr[:],
                                            in1=expell_sb[:, m * NH:(m + 1) * NH],
                                            op=ALU.mult).then_inc(S["s_vg"], 1)
                        fence(e)

                    @blk.scalar
                    def _(e):
                        e.wait_ge(S["s_v2"], 1)
                        ach(e.activation(out=ax_scr[:], in_=dtin_sb[:],
                                         func=AF.Abs))
                        ainc(e, S["s_a2"], 1)
                        asy(e)
                        ach(e.activation(out=ex_scr[:], in_=ax_scr[:], func=AF.Exp,
                                         scale=-1.0))
                        ainc(e, S["s_a2"], 1)
                        asy(e)
                        ach(e.activation(out=ln_scr[:], in_=ex_scr[:], func=AF.Ln,
                                         bias=1.0))
                        ainc(e, S["s_a2"], 1)
                        e.wait_ge(S["s_v2"], 2)
                        ach(e.activation(out=ax_scr[:].rearrange("p (m h) -> p m h", h=NH),
                                         in_=sm3[:, :, 16:32],
                                         func=AF.Abs))
                        ainc(e, S["s_a2"], 1)
                        asy(e)
                        ach(e.activation(out=ex_scr[:], in_=ax_scr[:], func=AF.Exp,
                                         scale=-1.0))
                        ainc(e, S["s_a2"], 1)
                        asy(e)
                        ach(e.activation(out=ln_scr[:], in_=ex_scr[:], func=AF.Ln,
                                         bias=1.0))
                        ainc(e, S["s_a2"], 1)
                        e.wait_ge(S["s_v2"], 3)
                        ach(e.activation(out=ex_scr[:].rearrange("p (m h) -> p m h", h=NH),
                                         in_=sm3[:, :, 32:48], func=AF.Exp,
                                         scale=-1.0))
                        ainc(e, S["s_a2"], 1)
                        for m in range(NT):
                            e.wait_ge(S["s_tg"], m + 1)
                            if m >= 1:
                                e.wait_ge(S["s_vg"], m)
                            e.activation(out=expell_sb[:, m * NH:(m + 1) * NH],
                                         in_=ps_e[m % 2][:], func=AF.Exp)
                            e.activation(out=eneg_scr[:], in_=ps_e[m % 2][:],
                                         func=AF.Exp, scale=-1.0)
                            e.activation(out=dec_scr[:], in_=ps_d[m % 2][:],
                                         func=AF.Exp).then_inc(S["s_sg"], 1)
                        fence(e)

                    @blk.tensor
                    def _(e):
                        e.wait_ge(S["s_v2"], 4)
                        for m in range(NT):
                            if m >= 2:
                                e.wait_ge(S["s_vb"], m - 1)
                            if m < NT - 1:
                                e.matmul(out=ps_bn[m % 2][:], lhsT=sd_sb[:],
                                         rhs=beta_sb[:, m * NH:(m + 1) * NH],
                                         start=True, stop=False)
                                e.matmul(out=ps_bn[m % 2][:], lhsT=oneh_sb[:],
                                         rhs=beta_sb[0:1, (m + 1) * NH:(m + 2) * NH],
                                         start=False, stop=True).then_inc(S["s_tm"], 1)
                            else:
                                e.matmul(out=ps_bn[m % 2][:], lhsT=sd_sb[:],
                                         rhs=beta_sb[:, m * NH:(m + 1) * NH],
                                         start=True, stop=True).then_inc(S["s_tm"], 1)
                        e.wait_ge(S["s_v2"], 5)
                        for m in range(NT):
                            if m >= 2:
                                e.wait_ge(S["s_vg"], m - 1)
                                e.wait_ge(S["s_sg"], m - 1)
                            e.matmul(out=ps_e[m % 2][:], lhsT=cum_sb[:],
                                     rhs=adt_sb[:, m * NH:(m + 1) * NH],
                                     start=True, stop=True)
                            e.matmul(out=ps_d[m % 2][:], lhsT=su_sb[:],
                                     rhs=adt_sb[:, m * NH:(m + 1) * NH],
                                     start=True, stop=True).then_inc(S["s_tg"], 1)
                        fence(e)

                    @blk.sync
                    def _(e):
                        fence(e)

                    @blk.gpsimd
                    def _(e):
                        fence(e)
                FN[0] += 1

            with contextlib.ExitStack() as sR:
                def r_(name, shape, dt):
                    return sR.enter_context(nc.sbuf_tensor(name, shape, dt))
                carry_sb = r_("carry_sb", [1, 512], F32)
                prod_sb = r_("prod_sb", [128, 2 * 512], F32)
                rs1_sb = r_("rs1_sb", [128, 512], F32)
                rsi_sb = r_("rsi_sb", [128, 512], F32)
                red_s = r_("red_s", [128, 512], F32)
                rci_sb = r_("rci_sb", [128, 512], F32)
                red_c = r_("red_c", [128, 512], F32)
                c_sb = r_("c_sb", [128, 2 * 512], BF16)
                s_sb = r_("s_sb", [128, 2 * 512], BF16)
                x1b_sb = r_("x1b_sb", [128, 512], BF16)
                x2b_sb = r_("x2b_sb", [128, 512], BF16)
                m1_sb = r_("m1_sb", [128, 512], BF16)
                m2_sb = r_("m2_sb", [128, 512], BF16)
                x1q_sb = r_("x1q_sb", [128, 512], BF16)
                x2q_sb = r_("x2q_sb", [128, 512], BF16)
                m3_sb = r_("m3_sb", [128, 512], BF16)
                m4_sb = r_("m4_sb", [128, 512], BF16)
                bb_sb = r_("bb_sb", [128, NH * DS], BF16)
                cb_sb = r_("cb_sb", [128, NH * DS], BF16)
                k_tile = r_("k_tile", [128, 2 * NH * DS], BF16)
                stage_sb = r_("stage_sb", [128, HID], BF16)
                q_tile = r_("q_tile", [128, 2 * NH * DS], BF16)
                ps_thp = [sR.enter_context(nc.psum_tensor(f"ps_th{i}", [128, 512], F32))
                          for i in range(2)]
                ps_carry = sR.enter_context(nc.psum_tensor("ps_carry", [1, 512], F32))

                with nc.Block("AII_theta") as blk:
                    @blk.vector
                    def _(e):
                        e.memset(carry_sb[:], 0.0).then_inc(S["s_vc"], 1)

                        def ropeK(m):
                            r = m % 2
                            e.wait_ge(S["s_sc3"], m + 1)
                            if m == 0:
                                e.wait_ge(S["s_cast_bbcb"], 16 * 10)
                            if m >= 2:
                                e.wait_ge(S["s_kqw"][m % 2], 32 * (m // 2))
                            kv = k_tile[:, r * 2048:(r + 1) * 2048].rearrange(
                                "p (h n) -> p h n", n=DS)
                            bb3 = bb_sb[:].rearrange("p (h n) -> p h n", n=DS)
                            bn1 = bn_sb[:, m * 256:m * 256 + 32].unsqueeze(1).broadcast_to([128, NH, 32])
                            bn2 = bn_sb[:, m * 256 + 32:m * 256 + 64].unsqueeze(1).broadcast_to([128, NH, 32])
                            bnr = bn_sb[:, m * 256 + 64:m * 256 + 128].unsqueeze(1).broadcast_to([128, NH, 64])
                            cs = c_sb[:, (m % 2) * 512:(m % 2 + 1) * 512]
                            ss = s_sb[:, (m % 2) * 512:(m % 2 + 1) * 512]
                            x1b3 = x1b_sb[:].rearrange("p (h a) -> p h a", a=32)
                            x2b3 = x2b_sb[:].rearrange("p (h a) -> p h a", a=32)
                            m13 = m1_sb[:].rearrange("p (h a) -> p h a", a=32)
                            m23 = m2_sb[:].rearrange("p (h a) -> p h a", a=32)
                            vch(e.tensor_tensor(out=x1b3, in0=bn1, in1=bb3[:, :, 0:32], op=ALU.add))
                            vch(e.tensor_tensor(out=x2b3, in0=bn2, in1=bb3[:, :, 32:64], op=ALU.add))
                            e.tensor_tensor(out=kv[:, :, 64:128], in0=bnr,
                                            in1=bb3[:, :, 64:128], op=ALU.add)
                            vsy(e)
                            vch(e.tensor_tensor(out=m1_sb[:], in0=x1b_sb[:], in1=cs, op=ALU.mult))
                            vch(e.tensor_tensor(out=m2_sb[:], in0=x2b_sb[:], in1=ss, op=ALU.mult))
                            vsy(e)
                            vch(e.tensor_tensor(out=kv[:, :, 0:32], in0=m13, in1=m23, op=ALU.subtract))
                            vsy(e)
                            vch(e.tensor_tensor(out=m1_sb[:], in0=x1b_sb[:], in1=ss, op=ALU.mult))
                            vch(e.tensor_tensor(out=m2_sb[:], in0=x2b_sb[:], in1=cs, op=ALU.mult))
                            vsy(e)
                            vch(e.tensor_tensor(out=kv[:, :, 32:64], in0=m13, in1=m23,
                                                op=ALU.add))
                            vinc(e, S["s_ropev"], 1)

                        for m in range(NT):
                            r = m % 2
                            if m >= 2:
                                e.wait_ge(S["s_thm"], m - 1)
                            e.tensor_tensor(
                                out=prod_sb[:, r * 512:(r + 1) * 512].rearrange(
                                    "p (h a) -> p h a", a=32),
                                in0=sm3[:, m, 48:80].unsqueeze(1).broadcast_to([128, NH, 32]),
                                in1=dt3[:, m, :].unsqueeze(2).broadcast_to([128, NH, 32]),
                                op=ALU.mult).then_inc(S["s_vp"], 1)
                            if m >= 1:
                                ropeK(m - 1)
                            e.wait_ge(S["s_thm"], m + 1)
                            vch(e.tensor_tensor(out=carry_sb[:], in0=carry_sb[:],
                                                in1=ps_carry[:], op=ALU.add))
                            vinc(e, S["s_vc"], 1)
                            pt = ps_thp[m % 2]
                            vch(e.tensor_scalar(out=rs1_sb[:], in0=pt[:],
                                                scalar1=1.0 / TWO_PI, scalar2=MAGIC,
                                                op0=ALU.mult, op1=ALU.add))
                            vsy(e)
                            vch(e.tensor_scalar(out=rsi_sb[:], in0=rs1_sb[:],
                                                scalar1=-MAGIC, scalar2=None,
                                                op0=ALU.add))
                            vsy(e)
                            vch(e.scalar_tensor_tensor(out=red_s[:], in0=rsi_sb[:],
                                                       scalar=-TWO_PI, in1=pt[:],
                                                       op0=ALU.mult, op1=ALU.add))
                            vch(e.tensor_scalar(out=rs1_sb[:], in0=pt[:],
                                                scalar1=1.0 / TWO_PI, scalar2=0.25,
                                                op0=ALU.mult, op1=ALU.add))
                            vsy(e)
                            vch(e.tensor_scalar(out=rs1_sb[:], in0=rs1_sb[:],
                                                scalar1=MAGIC, scalar2=None,
                                                op0=ALU.add))
                            vsy(e)
                            vch(e.tensor_scalar(out=rci_sb[:], in0=rs1_sb[:],
                                                scalar1=-MAGIC, scalar2=None,
                                                op0=ALU.add))
                            vsy(e)
                            vch(e.scalar_tensor_tensor(out=red_c[:], in0=rci_sb[:],
                                                       scalar=-TWO_PI, in1=pt[:],
                                                       op0=ALU.mult, op1=ALU.add))
                            vinc(e, S["s_rrv"], 1)
                        ropeK(NT - 1)
                        fence(e)

                    @blk.tensor
                    def _(e):
                        for m in range(NT):
                            e.wait_ge(S["s_vp"], m + 1)
                            e.wait_ge(S["s_vc"], m + 1)
                            if m >= 2:
                                e.wait_ge(S["s_rrv"], m - 1)
                            r = m % 2
                            pt = ps_thp[m % 2]
                            e.matmul(out=pt[:], lhsT=cum_sb[:],
                                     rhs=prod_sb[:, r * 512:(r + 1) * 512],
                                     start=True, stop=False)
                            e.matmul(out=pt[:], lhsT=aones_sb[:], rhs=carry_sb[:],
                                     start=False, stop=True)
                            e.matmul(out=ps_carry[:], lhsT=onesc_sb[:],
                                     rhs=prod_sb[:, r * 512:(r + 1) * 512],
                                     start=True, stop=True).then_inc(S["s_thm"], 1)
                        fence(e)

                    @blk.scalar
                    def _(e):
                        for m in range(NT):
                            e.wait_ge(S["s_rrv"], m + 1)
                            if m >= 2:
                                e.wait_ge(S["s_ropev"], m - 1)
                                e.wait_ge(S["s_ropeg"], m - 1)
                            sl = (m % 2) * 512
                            asy(e)
                            ach(e.activation(out=s_sb[:, sl:sl + 512], in_=red_s[:],
                                             func=AF.Sin))
                            ach(e.activation(out=c_sb[:, sl:sl + 512], in_=red_c[:],
                                             func=AF.Sin, bias=halfpi_sb[:]))
                            ainc(e, S["s_sc3"], 1)
                        fence(e)

                    @blk.gpsimd
                    def _(e):
                        e.dma_start(out=bb_sb[:], in_=bb_bc).then_inc(S["s_cast_bbcb"], 16)
                        e.dma_start(out=cb_sb[:], in_=cb_bc).then_inc(S["s_cast_bbcb"], 16)
                        e.wait_ge(S["s_cast_bbcb"], 16 * 10)
                        for m in range(NT):
                            r = m % 2
                            e.wait_ge(S["s_sc3"], m + 1)
                            if m >= 2:
                                e.wait_ge(S["s_kqw"][m % 2], 32 * (m // 2))
                            qv = q_tile[:, r * 2048:(r + 1) * 2048].rearrange(
                                "p (h n) -> p h n", n=DS)
                            cb3 = cb_sb[:].rearrange("p (h n) -> p h n", n=DS)
                            cn1 = bn_sb[:, m * 256 + 128:m * 256 + 160].unsqueeze(1).broadcast_to([128, NH, 32])
                            cn2 = bn_sb[:, m * 256 + 160:m * 256 + 192].unsqueeze(1).broadcast_to([128, NH, 32])
                            cnr = bn_sb[:, m * 256 + 192:m * 256 + 256].unsqueeze(1).broadcast_to([128, NH, 64])
                            cs = c_sb[:, (m % 2) * 512:(m % 2 + 1) * 512]
                            ss = s_sb[:, (m % 2) * 512:(m % 2 + 1) * 512]
                            x1q3 = x1q_sb[:].rearrange("p (h a) -> p h a", a=32)
                            x2q3 = x2q_sb[:].rearrange("p (h a) -> p h a", a=32)
                            m33 = m3_sb[:].rearrange("p (h a) -> p h a", a=32)
                            m43 = m4_sb[:].rearrange("p (h a) -> p h a", a=32)
                            gch(e.tensor_tensor(out=x1q3, in0=cn1, in1=cb3[:, :, 0:32], op=ALU.add))
                            gch(e.tensor_tensor(out=x2q3, in0=cn2, in1=cb3[:, :, 32:64], op=ALU.add))
                            e.tensor_tensor(out=qv[:, :, 64:128], in0=cnr,
                                            in1=cb3[:, :, 64:128], op=ALU.add)
                            gsy(e)
                            gch(e.tensor_tensor(out=m3_sb[:], in0=x1q_sb[:], in1=cs, op=ALU.mult))
                            gch(e.tensor_tensor(out=m4_sb[:], in0=x2q_sb[:], in1=ss, op=ALU.mult))
                            gsy(e)
                            gch(e.tensor_tensor(out=qv[:, :, 0:32], in0=m33, in1=m43, op=ALU.subtract))
                            gsy(e)
                            gch(e.tensor_tensor(out=m3_sb[:], in0=x1q_sb[:], in1=ss, op=ALU.mult))
                            gch(e.tensor_tensor(out=m4_sb[:], in0=x2q_sb[:], in1=cs, op=ALU.mult))
                            gsy(e)
                            gch(e.tensor_tensor(out=qv[:, :, 32:64], in0=m33, in1=m43,
                                                op=ALU.add))
                            ginc(e, S["s_ropeg"], 1)
                            gsy(e)
                            e.dma_start(out=q_dram[m * 128:(m + 1) * 128, :],
                                        in_=q_tile[:, r * 2048:(r + 1) * 2048]
                                        ).then_inc(S["s_kqw"][m % 2], 16)
                            e.wait_ge(S["s_ropev"], m + 1)
                            e.dma_start(out=k_dram[m * 128:(m + 1) * 128, :],
                                        in_=k_tile[:, r * 2048:(r + 1) * 2048]
                                        ).then_inc(S["s_kqw"][m % 2], 16)
                        e.wait_ge(S["s_kqw"][0], 32 * 16)
                        e.wait_ge(S["s_kqw"][1], 32 * 16)
                        fence(e)

                    @blk.sync
                    def _(e):
                        e.wait_ge(S["s_cast_wd"], 16 * 15)
                        for hc in range(24):
                            sl3 = (hc % 3) * 1024
                            if hc >= 3:
                                e.wait_ge(S["s_prep_wd"], 16 * (2 * hc - 4))
                            if hc >= 1:
                                e.wait_ge(S["s_prep_wd"], 16 * (2 * hc))
                            e.dma_start(out=stage_sb[:, sl3:sl3 + 1024],
                                        in_=wd_bf[:, hc * 128:(hc + 1) * 128],
                                        transpose=True).then_inc(S["s_prep_wd"], 16)
                            e.wait_ge(S["s_prep_wd"], 16 * (2 * hc + 1))
                            e.dma_start(out=wdt_dram[hc * 128:(hc + 1) * 128, :],
                                        in_=stage_sb[:, sl3:sl3 + 1024]
                                        ).then_inc(S["s_prep_wd"], 16)
                        e.wait_ge(S["s_cast_wg"], 16 * 15)
                        e.wait_ge(S["s_prep_wd"], 16 * 48)
                        for dc in range(8):
                            if dc >= 1:
                                e.wait_ge(S["s_prep_wg"], 16 * (2 * dc))
                            e.dma_start(out=stage_sb[:], in_=wg_bf[:, dc * 128:(dc + 1) * 128],
                                        transpose=True).then_inc(S["s_prep_wg"], 16)
                            e.wait_ge(S["s_prep_wg"], 16 * (2 * dc + 1))
                            e.dma_start(out=wgt_dram[:, dc, :, :].transpose([1, 0, 2]),
                                        in_=stage_sb[:].rearrange("p (hc h) -> p hc h", h=128)
                                        ).then_inc(S["s_prep_wg"], 16)
                        e.wait_ge(S["s_cast_wu"], 16 * 15)
                        e.wait_ge(S["s_prep_wg"], 16 * 16)
                        for dc in range(8):
                            if dc >= 1:
                                e.wait_ge(S["s_prep_wu"], 16 * (2 * dc))
                            e.dma_start(out=stage_sb[:], in_=wu_bf[:, dc * 128:(dc + 1) * 128],
                                        transpose=True).then_inc(S["s_prep_wu"], 16)
                            e.wait_ge(S["s_prep_wu"], 16 * (2 * dc + 1))
                            e.dma_start(out=wut_dram[:, dc, :, :].transpose([1, 0, 2]),
                                        in_=stage_sb[:].rearrange("p (hc h) -> p hc h", h=128)
                                        ).then_inc(S["s_prep_wu"], 16)
                        fence(e)
                FN[0] += 1

    if "AV" in PHASES:
        with contextlib.ExitStack() as sF:
            def f_(name, shape, dt):
                return sF.enter_context(nc.sbuf_tensor(name, shape, dt))
            wdt_sb = f_("wdt_sb", [128, 24 * 1024], BF16)
            xnth_ring = f_("xnth_ring", [128, 8 * 1024], BF16)
            wgu_ring = f_("wgu_ring", [128, 4 * 1024], BF16)
            h_sb = f_("h_sb", [128, 24 * 1024], BF16)
            sg_scr = f_("sg_scr", [128, 1024], BF16)
            g_tile = f_("g_tile", [128, 2 * 1024], BF16)
            ps_g = sF.enter_context(nc.psum_tensor("ps_g", [128, 1024], F32))
            ps_u2 = sF.enter_context(nc.psum_tensor("ps_u2", [128, 1024], F32))
            ps_dn = sF.enter_context(nc.psum_tensor("ps_dn", [128, 2 * 1024], F32))

            with nc.Block("AV") as blk:
                @blk.sync
                def _(e):
                    e.wait_ge(S["s_prep_wg"], 16 * 16)
                    e.wait_ge(S["s_prep_wu"], 16 * 16)
                    for sl in range(2):
                        if sl > 0:
                            e.wait_ge(S["s_prep_wd"], 16 * 48)
                            for hc in range(24):
                                e.dma_start(out=wdt_sb[:, hc * 1024:(hc + 1) * 1024],
                                            in_=wdt_dram[hc * 128:(hc + 1) * 128, :].rearrange(
                                                "(o p) d -> p o d", p=128),
                                            ).then_inc(S["s_prep_wd"], 16)
                            e.wait_ge(S["s_fmm"], 48 * sl)
                        for dc in range(8):
                            e.dma_start(out=xnth_ring[:, dc * 1024:(dc + 1) * 1024],
                                        in_=xnh_dram[sl * 1024:(sl + 1) * 1024,
                                                     dc * 128:(dc + 1) * 128],
                                        transpose=True).then_inc(S["s_fx"], 16)
                        for hc in range(24):
                            g_i = sl * 24 + hc
                            rr = hc % 2
                            if g_i >= 2:
                                e.wait_ge(S["s_fmm"], 2 * g_i - 2)
                            e.dma_start(
                                out=wgu_ring[:, (rr * 2) * 1024:(rr * 2 + 1) * 1024
                                             ].rearrange("p (dc h) -> p dc h", h=128),
                                in_=wgt_dram[hc, :, :, :].transpose([1, 0, 2])
                                ).then_inc(S["s_fw"][g_i % 2], 16)
                            e.dma_start(
                                out=wgu_ring[:, (rr * 2 + 1) * 1024:(rr * 2 + 2) * 1024
                                             ].rearrange("p (dc h) -> p dc h", h=128),
                                in_=wut_dram[hc, :, :, :].transpose([1, 0, 2])
                                ).then_inc(S["s_fw"][g_i % 2], 16)
                    e.wait_ge(S["s_fx"], 16 * 16)
                    e.wait_ge(S["s_fw"][0], 32 * 12)
                    e.wait_ge(S["s_fw"][1], 32 * 12)
                    e.wait_ge(S["s_prep_wd"], 16 * 72)
                    fence(e)

                @blk.tensor
                def _(e):
                    e.wait_ge(S["s_prep_wg"], 16 * 16)
                    e.wait_ge(S["s_prep_wu"], 16 * 16)
                    for sl in range(2):
                        e.wait_ge(S["s_fx"], 16 * 8 * (sl + 1))
                        for hc in range(24):
                            g_i = sl * 24 + hc
                            rr = hc % 2
                            e.wait_ge(S["s_fw"][g_i % 2], 32 * (g_i // 2 + 1))
                            if g_i >= 1:
                                e.wait_ge(S["s_fg"], g_i)
                                e.wait_ge(S["s_fev"], g_i)
                            for dc in range(8):
                                lt = wgu_ring[:, (rr * 2) * 1024 + dc * 128:
                                              (rr * 2) * 1024 + (dc + 1) * 128]
                                e.matmul(out=ps_g[:, 0:512], lhsT=lt,
                                         rhs=xnth_ring[:, dc * 1024:dc * 1024 + 512],
                                         start=(dc == 0), stop=(dc == 7))
                                mm = e.matmul(out=ps_g[:, 512:1024], lhsT=lt,
                                              rhs=xnth_ring[:, dc * 1024 + 512:(dc + 1) * 1024],
                                              start=(dc == 0), stop=(dc == 7))
                                if dc == 7:
                                    mm.then_inc(S["s_fmm"], 1)
                            for dc in range(8):
                                lt = wgu_ring[:, (rr * 2 + 1) * 1024 + dc * 128:
                                              (rr * 2 + 1) * 1024 + (dc + 1) * 128]
                                e.matmul(out=ps_u2[:, 0:512], lhsT=lt,
                                         rhs=xnth_ring[:, dc * 1024:dc * 1024 + 512],
                                         start=(dc == 0), stop=(dc == 7))
                                mm = e.matmul(out=ps_u2[:, 512:1024], lhsT=lt,
                                              rhs=xnth_ring[:, dc * 1024 + 512:(dc + 1) * 1024],
                                              start=(dc == 0), stop=(dc == 7))
                                if dc == 7:
                                    mm.then_inc(S["s_fmm"], 1)
                        e.wait_ge(S["s_fev"], 24 * (sl + 1))
                        e.wait_ge(S["s_prep_wd"], 16 * 72)
                        for tt in range(8):
                            ti = sl * 8 + tt
                            rr = tt % 2
                            if ti >= 2:
                                e.wait_ge(S["s_fge"], ti - 1)
                            for hc in range(24):
                                lt = h_sb[:, hc * 1024 + tt * 128:hc * 1024 + (tt + 1) * 128]
                                e.matmul(out=ps_dn[:, rr * 1024:rr * 1024 + 512], lhsT=lt,
                                         rhs=wdt_sb[:, hc * 1024:hc * 1024 + 512],
                                         start=(hc == 0), stop=(hc == 23))
                                mm = e.matmul(out=ps_dn[:, rr * 1024 + 512:(rr + 1) * 1024],
                                              lhsT=lt,
                                              rhs=wdt_sb[:, hc * 1024 + 512:(hc + 1) * 1024],
                                              start=(hc == 0), stop=(hc == 23))
                                if hc == 23:
                                    mm.then_inc(S["s_fdm"], 1)
                    fence(e)

                @blk.scalar
                def _(e):
                    for sl in range(2):
                        for hc in range(24):
                            g_i = sl * 24 + hc
                            e.wait_ge(S["s_fmm"], 2 * g_i + 1)
                            if g_i >= 1:
                                e.wait_ge(S["s_fev"], g_i)
                            e.activation(out=sg_scr[:], in_=ps_g[:], func=AF.Silu,
                                         scale=DEQ).then_inc(S["s_fg"], 1)
                        for tt in range(8):
                            ti = sl * 8 + tt
                            rr = tt % 2
                            e.wait_ge(S["s_fdm"], ti + 1)
                            if ti >= 2:
                                e.wait_ge(S["s_fgw"][ti % 2], 16 * (ti // 2))
                            e.activation(out=g_tile[:, rr * 1024:(rr + 1) * 1024],
                                         in_=ps_dn[:, rr * 1024:(rr + 1) * 1024],
                                         func=AF.Silu, scale=DEQ).then_inc(S["s_fge"], 1)
                        fence(e) if sl == 1 else None

                @blk.vector
                def _(e):
                    for sl in range(2):
                        if sl > 0:
                            e.wait_ge(S["s_fdm"], 8 * sl)
                        for hc in range(24):
                            g_i = sl * 24 + hc
                            e.wait_ge(S["s_fg"], g_i + 1)
                            e.wait_ge(S["s_fmm"], 2 * g_i + 2)
                            e.scalar_tensor_tensor(out=h_sb[:, hc * 1024:(hc + 1) * 1024],
                                                   in0=ps_u2[:], scalar=DEQ,
                                                   in1=sg_scr[:], op0=ALU.mult,
                                                   op1=ALU.mult).then_inc(S["s_fev"], 1)
                    fence(e)

                @blk.gpsimd
                def _(e):
                    for ti in range(16):
                        rr = ti % 2
                        e.wait_ge(S["s_fge"], ti + 1)
                        e.dma_start(out=g_dram[ti * 128:(ti + 1) * 128, :],
                                    in_=g_tile[:, rr * 1024:(rr + 1) * 1024]
                                    ).then_inc(S["s_fgw"][ti % 2], 16)
                    e.wait_ge(S["s_fgw"][0], 16 * 8)
                    e.wait_ge(S["s_fgw"][1], 16 * 8)
                    fence(e)
            FN[0] += 1

    if "B" in PHASES:
        with contextlib.ExitStack() as sB:
            def b_(name, shape, dt):
                return sB.enter_context(nc.sbuf_tensor(name, shape, dt))
            ktq_sb = b_("ktq_sb", [128, 2 * 2 * NH * 512], BF16)
            ktok_sb = b_("ktok_sb", [128, 2 * NH * DS], BF16)
            xc_sb = b_("xc_sb", [128, 2 * 1024], BF16)
            zc_sb = b_("zc_sb", [128, 2 * 1024], BF16)
            xw_sb = b_("xw_sb", [128, 1024], BF16)
            siluz_sb = b_("siluz_sb", [128, 1024], BF16)
            sw_sb = b_("sw_sb", [128, 16 * 128], BF16)
            t1_sb = b_("t1_sb", [128, 1024], F32)
            dxp_sb = b_("dxp_sb", [128, 1024], F32)
            t2_sb = b_("t2_sb", [128, 1024], F32)
            yg_tile = b_("yg_tile", [128, 2 * 1024], BF16)
            ps_S = [sB.enter_context(nc.psum_tensor(f"ps_S{i}", [128, 512], F32))
                    for i in range(2)]
            ps_y = sB.enter_context(nc.psum_tensor("ps_y", [128, 1024], F32))
            ps_w = sB.enter_context(nc.psum_tensor("ps_w", [128, 1024], F32))

            def kt_slice(sr, kq, h, col, width=128):
                base = ((sr * 2 + kq) * NH + h) * 512
                return ktq_sb[:, base + col:base + col + width]

            with nc.Block("B") as blk:
                @blk.sync
                def _(e):
                    for h in range(NH):
                        for kq, dram in ((0, k_dram), (1, q_dram)):
                            e.dma_start(out=kt_slice(0, kq, h, 0, 512),
                                        in_=dram[0:512, h * DS:(h + 1) * DS],
                                        transpose=True).then_inc(S["s_kq"][0], 16)
                    for s in range(1, 8):
                        e.wait_ge(S["s_wmm"], 4 * (s - 1))
                        for h in range(NH):
                            for kq, dram in ((0, k_dram), (1, q_dram)):
                                e.dma_start(out=kt_slice(s % 2, kq, h, 0, 512),
                                            in_=dram[s * 512:(s + 1) * 512,
                                                     h * DS:(h + 1) * DS],
                                            transpose=True).then_inc(S["s_kq"][s % 2], 16)
                    e.wait_ge(S["s_kq"][0], 512 * 4)
                    e.wait_ge(S["s_kq"][1], 512 * 4)
                    fence(e)

                @blk.scalar
                def _(e):
                    for c in range(NT):
                        rc = c % 2
                        if c >= 2:
                            e.wait_ge(S["s_wmm"], c - 1)
                            e.wait_ge(S["s_dxp"], c - 1)
                            e.wait_ge(S["s_bz"], c - 1)
                        e.dma_start(out=ktok_sb[:, rc * 2048:(rc + 1) * 2048],
                                    in_=k_dram[c * 128:(c + 1) * 128, :]).then_inc(S["s_bx"][c % 2], 16)
                        e.dma_start(out=xc_sb[:, rc * 1024:(rc + 1) * 1024],
                                    in_=x_dram[c * 128:(c + 1) * 128, :]).then_inc(S["s_bx"][c % 2], 16)
                        e.dma_start(out=zc_sb[:, rc * 1024:(rc + 1) * 1024],
                                    in_=z_dram[c * 128:(c + 1) * 128, :]).then_inc(S["s_bx"][c % 2], 16)
                        e.wait_ge(S["s_bx"][c % 2], 48 * (c // 2 + 1))
                        if c >= 1:
                            e.wait_ge(S["s_bev"], c)
                        e.activation(out=siluz_sb[:], in_=zc_sb[:, rc * 1024:(rc + 1) * 1024],
                                     func=AF.Silu).then_inc(S["s_bz"], 1)
                    e.wait_ge(S["s_bx"][0], 48 * 16)
                    e.wait_ge(S["s_bx"][1], 48 * 16)
                    fence(e)

                @blk.tensor
                def _(e):
                    for c in range(NT):
                        s = c // 4
                        sr = s % 2
                        rc = c % 2
                        wc = (c % 4) * 128
                        e.wait_ge(S["s_kq"][s % 2], 512 * (s // 2 + 1))
                        e.wait_ge(S["s_bx"][c % 2], 48 * (c // 2 + 1))
                        for g4 in range(4):
                            gi = 4 * c + g4
                            if gi >= 2:
                                e.wait_ge(S["s_sw"], gi - 1)
                            for hh in range(4):
                                h = 4 * g4 + hh
                                mm = e.matmul(out=ps_S[g4 % 2][:, hh * 128:(hh + 1) * 128],
                                              lhsT=kt_slice(sr, 0, h, wc),
                                              rhs=kt_slice(sr, 1, h, wc),
                                              start=True, stop=True)
                                if hh == 3:
                                    mm.then_inc(S["s_smm"], 1)
                        if c >= 1:
                            e.wait_ge(S["s_bev1"], c)
                        e.wait_ge(S["s_hup"], c + 1)
                        for h in range(NH):
                            e.wait_ge(S["s_sw"], 4 * c + h // 4 + 1)
                            e.matmul(out=ps_y[:, h * 64:(h + 1) * 64],
                                     lhsT=sw_sb[:, h * 128:(h + 1) * 128],
                                     rhs=xc_sb[:, rc * 1024 + h * 64:rc * 1024 + (h + 1) * 64],
                                     start=True, stop=False)
                            mm = e.matmul(out=ps_y[:, h * 64:(h + 1) * 64],
                                          lhsT=kt_slice(sr, 1, h, wc),
                                          rhs=Hbf_sb[:, h * 64:(h + 1) * 64],
                                          start=False, stop=True)
                            if h == NH - 1:
                                mm.then_inc(S["s_ymm"], 1)
                        e.wait_ge(S["s_xwg"], c + 1)
                        for h in range(NH):
                            mm = e.matmul(out=ps_w[:, h * 64:(h + 1) * 64],
                                          lhsT=ktok_sb[:, rc * 2048 + h * DS:rc * 2048 + (h + 1) * DS],
                                          rhs=xw_sb[:, h * 64:(h + 1) * 64],
                                          start=True, stop=True)
                            if h == NH - 1:
                                mm.then_inc(S["s_wmm"], 1)
                    fence(e)

                @blk.vector
                def _(e):
                    e.sem_inc(S["s_hup"], 1)
                    for c in range(NT):
                        rc = c % 2
                        for g4 in range(4):
                            gi = 4 * c + g4
                            e.wait_ge(S["s_smm"], gi + 1)
                            for hh in range(4):
                                h = 4 * g4 + hh
                                stt = e.scalar_tensor_tensor(
                                    out=sw_sb[:, h * 128:(h + 1) * 128],
                                    in0=ps_S[g4 % 2][:, hh * 128:(hh + 1) * 128],
                                    scalar=gcx_sb[:, c * NH + h:c * NH + h + 1],
                                    in1=strict_sb[:], op0=ALU.mult, op1=ALU.mult)
                                if hh == 3:
                                    stt.then_inc(S["s_sw"], 1)
                        e.wait_ge(S["s_ymm"], c + 1)
                        if c >= 1:
                            e.wait_ge(S["s_t2"], c)
                        e.tensor_tensor(
                            out=t1_sb[:].rearrange("p (h q) -> p h q", q=64),
                            in0=ps_y[:].rearrange("p (h q) -> p h q", q=64),
                            in1=expell_sb[:, c * NH:(c + 1) * NH].unsqueeze(2
                                ).broadcast_to([128, NH, 64]),
                            op=ALU.mult).then_inc(S["s_bev1"], 1)
                        e.wait_ge(S["s_t2"], c + 1)
                        e.wait_ge(S["s_bz"], c + 1)
                        if c >= 2:
                            e.wait_ge(S["s_ygw"][c % 2], 16 * (c // 2))
                        e.tensor_tensor(out=yg_tile[:, rc * 1024:(rc + 1) * 1024],
                                        in0=t2_sb[:], in1=siluz_sb[:],
                                        op=ALU.mult).then_inc(S["s_bev"], 1)
                        e.wait_ge(S["s_wmm"], c + 1)
                        e.wait_ge(S["s_hsg"], c + 1)
                        vch(e.tensor_tensor(out=H_sb[:], in0=Hs_sb[:], in1=ps_w[:],
                                                op=ALU.add))
                        vsy(e)
                        e.tensor_copy(out=Hbf_sb[:], in_=H_sb[:]).then_inc(S["s_hup"], 1)
                    fence(e)

                @blk.gpsimd
                def _(e):
                    for c in range(NT):
                        rc = c % 2
                        e.wait_ge(S["s_bx"][c % 2], 48 * (c // 2 + 1))
                        if c >= 1:
                            e.wait_ge(S["s_wmm"], c)
                        xc3 = xc_sb[:, rc * 1024:(rc + 1) * 1024].rearrange(
                            "p (h q) -> p h q", q=64)
                        e.tensor_tensor(
                            out=xw_sb[:].rearrange("p (h q) -> p h q", q=64),
                            in0=xc3,
                            in1=wsg_sb[:, c * NH:(c + 1) * NH].unsqueeze(2
                                ).broadcast_to([128, NH, 64]),
                            op=ALU.mult).then_inc(S["s_xwg"], 1)
                        gch(e.tensor_tensor(
                            out=dxp_sb[:].rearrange("p (h q) -> p h q", q=64),
                            in0=xc3,
                            in1=coef_sb[:, c * NH:(c + 1) * NH].unsqueeze(2
                                ).broadcast_to([128, NH, 64]),
                            op=ALU.mult))
                        ginc(e, S["s_dxp"], 1)
                        e.wait_ge(S["s_hup"], c + 1)
                        e.tensor_tensor(
                            out=Hs_sb[:].rearrange("p (h q) -> p h q", q=64),
                            in0=H_sb[:].rearrange("p (h q) -> p h q", q=64),
                            in1=cdb_sb[:, c * NH:(c + 1) * NH].unsqueeze(2
                                ).broadcast_to([128, NH, 64]),
                            op=ALU.mult).then_inc(S["s_hsg"], 1)
                        e.wait_ge(S["s_bev1"], c + 1)
                        gsy(e)
                        e.tensor_tensor(out=t2_sb[:], in0=t1_sb[:], in1=dxp_sb[:],
                                        op=ALU.add).then_inc(S["s_t2"], 1)
                        e.wait_ge(S["s_bev"], c + 1)
                        e.dma_start(out=yg_dram[c * 128:(c + 1) * 128, :],
                                    in_=yg_tile[:, rc * 1024:(rc + 1) * 1024]
                                    ).then_inc(S["s_ygw"][c % 2], 16)
                    e.wait_ge(S["s_ygw"][0], 16 * 16)
                    e.wait_ge(S["s_ygw"][1], 16 * 16)
                    fence(e)
            FN[0] += 1

    if "C" in PHASES:
        with contextlib.ExitStack() as sC:
            def c_(name, shape, dt):
                return sC.enter_context(nc.sbuf_tensor(name, shape, dt))
            ygT_sb = c_("ygT_sb", [128, 8 * 4096], BF16)
            wout_sb = c_("wout_sb", [128, 8 * 1024], BF16)
            s_ring = c_("s_ring", [128, 3 * 1024], F32)
            so_ring = c_("so_ring", [128, 2 * 1024], F32)
            sp_ring = c_("sp_ring", [128, 2 * 1024], F32)
            uh_ring = c_("uh_ring", [128, 2 * 1024], F32)
            gh_ring = c_("gh_ring", [128, 2 * 1024], BF16)
            stot_sb = c_("stot_sb", [128, 1024], F32)
            out_ring = c_("out_ring", [128, 2 * 1024], F32)
            ps_s = sC.enter_context(nc.psum_tensor("ps_s", [128, 2 * 1024], F32))

            with nc.Block("C") as blk:
                @blk.sync
                def _(e):
                    for pc in range(8):
                        e.dma_start(out=ygT_sb[:, pc * 4096:(pc + 1) * 4096],
                                    in_=yg_dram[:, pc * 128:(pc + 1) * 128],
                                    transpose=True).then_inc(S["s_yt"], 16)
                    for ft in range(NTH):
                        rf = ft % 2
                        if ft >= 2:
                            e.wait_ge(S["s_fin"], ft - 1)
                        e.dma_start(out=uh_ring[:, rf * 1024:(rf + 1) * 1024],
                                    in_=u_half[ft * 128:(ft + 1) * 128, :]
                                    ).then_inc(S["s_uh"][ft % 2], 16)
                        e.dma_start(out=gh_ring[:, rf * 1024:(rf + 1) * 1024],
                                    in_=g_dram[ft * 128:(ft + 1) * 128, :]
                                    ).then_inc(S["s_gh"][ft % 2], 16)
                        e.wait_ge(S["s_fin"], ft + 1)
                        e.dma_start(out=y_out[ft * 128:(ft + 1) * 128, :],
                                    in_=out_ring[:, rf * 1024:(rf + 1) * 1024]
                                    ).then_inc(S["s_out"][ft % 2], 16)
                    e.wait_ge(S["s_out"][0], 16 * (NTH // 2))
                    e.wait_ge(S["s_out"][1], 16 * (NTH // 2))
                    fence(e)

                @blk.gpsimd
                def _(e):
                    for pc in range(8):
                        e.dma_start(out=wout_sb[:, pc * 1024:(pc + 1) * 1024],
                                    in_=w_out[pc * 128:(pc + 1) * 128, :]
                                    ).then_inc(S["s_wo"], 16)
                    for tt in range(NT):
                        rt = tt % 3
                        e.wait_ge(S["s_se"], tt + 1)
                        e.indirect_dma_start(
                            out=xchg,
                            out_offset=IndirectOffsetOnAxis(ap=sidx_sb[:, tt:tt + 1], axis=0),
                            in_=s_ring[:, rt * 1024:(rt + 1) * 1024],
                            in_offset=None).then_inc(S["s_sc"][tt % 2], 16)
                    e.wait_ge(S["s_sc"][0], 16 * (NT // 2))
                    e.wait_ge(S["s_sc"][1], 16 * (NT // 2))
                    e.indirect_dma_start(
                        out=flags,
                        out_offset=IndirectOffsetOnAxis(ap=fidx_sb[:], axis=0),
                        in_=nonce_t_sb[:], in_offset=None).then_inc(S["s_fl"], 16)
                    e.wait_ge(S["s_fl"], 16)
                    nonce_r = e.alloc_register("nonce_r")
                    e.reg_load(nonce_r, nonce_sb[0:1, 0:1])
                    tmp_r = e.alloc_register("tmp_r")
                    nm_r = e.alloc_register("nm_r")
                    cnt_r = e.alloc_register("cnt_r")
                    cond_r = e.alloc_register("cond_r")
                    c1_r = e.alloc_register("c1_r")
                    c2_r = e.alloc_register("c2_r")
                    e.reg_mov(cnt_r, 0)
                    e.reg_mov(cond_r, 1)
                    flags_s = flags.rearrange("(k f) c -> k f c", f=128)[:, 0, :]
                    with e.While(cond_r):
                        e.dma_start(out=flagrd_sb[:], in_=flags_s).then_inc(S["s_fl"], 16)
                        e.reg_mov(nm_r, 0)
                        for kk in range(2):
                            e.reg_load(tmp_r, flagrd_sb[kk:kk + 1, 0:1])
                            e.reg_alu(tmp_r, tmp_r, nonce_r, ALU.is_equal)
                            e.reg_alu(nm_r, nm_r, tmp_r, ALU.add)
                        e.reg_add(cnt_r, cnt_r, 1)
                        e.reg_alu(c1_r, nm_r, 2, ALU.is_lt)
                        e.reg_alu(c2_r, cnt_r, POLL_CAP, ALU.is_lt)
                        e.reg_alu(cond_r, c1_r, c2_r, ALU.logical_and)
                    e.sem_inc(S["s_bar"], 1)
                    for ft in range(NTH):
                        rf = ft % 2
                        if ft >= 2:
                            e.wait_ge(S["s_fin"], ft - 1)
                        e.indirect_dma_start(
                            out=so_ring[:, rf * 1024:(rf + 1) * 1024], out_offset=None,
                            in_=xchg,
                            in_offset=IndirectOffsetOnAxis(ap=goidx_sb[:, ft:ft + 1], axis=0)
                            ).then_inc(S["s_gat"][ft % 2], 16)
                        e.indirect_dma_start(
                            out=sp_ring[:, rf * 1024:(rf + 1) * 1024], out_offset=None,
                            in_=xchg,
                            in_offset=IndirectOffsetOnAxis(ap=gpidx_sb[:, ft:ft + 1], axis=0)
                            ).then_inc(S["s_gat"][ft % 2], 16)
                    e.wait_ge(S["s_out"][0], 16 * (NTH // 2))
                    e.wait_ge(S["s_out"][1], 16 * (NTH // 2))
                    fence(e)

                @blk.tensor
                def _(e):
                    e.wait_ge(S["s_yt"], 16 * 8)
                    e.wait_ge(S["s_wo"], 16 * 8)
                    for tt in range(NT):
                        if tt >= 2:
                            e.wait_ge(S["s_se"], tt - 1)
                        ob = (tt % 2) * 1024
                        for pc in range(8):
                            lt = ygT_sb[:, pc * 4096 + tt * 128:pc * 4096 + (tt + 1) * 128]
                            e.matmul(out=ps_s[:, ob:ob + 512], lhsT=lt,
                                     rhs=wout_sb[:, pc * 1024:pc * 1024 + 512],
                                     start=(pc == 0), stop=(pc == 7))
                            mm = e.matmul(out=ps_s[:, ob + 512:ob + 1024], lhsT=lt,
                                          rhs=wout_sb[:, pc * 1024 + 512:(pc + 1) * 1024],
                                          start=(pc == 0), stop=(pc == 7))
                            if pc == 7:
                                mm.then_inc(S["s_om"], 1)
                    e.wait_ge(S["s_out"][0], 16 * (NTH // 2))
                    e.wait_ge(S["s_out"][1], 16 * (NTH // 2))
                    fence(e)

                @blk.vector
                def _(e):
                    for tt in range(NT):
                        rt = tt % 3
                        e.wait_ge(S["s_om"], tt + 1)
                        if tt >= 3:
                            e.wait_ge(S["s_sc"][(tt - 3) % 2], 16 * ((tt - 3) // 2 + 1))
                        e.tensor_copy(out=s_ring[:, rt * 1024:(rt + 1) * 1024],
                                      in_=ps_s[:, (tt % 2) * 1024:(tt % 2 + 1) * 1024]
                                      ).then_inc(S["s_se"], 1)
                    for ft in range(NTH):
                        rf = ft % 2
                        e.wait_ge(S["s_gat"][ft % 2], 32 * (ft // 2 + 1))
                        e.wait_ge(S["s_uh"][ft % 2], 16 * (ft // 2 + 1))
                        e.wait_ge(S["s_gh"][ft % 2], 16 * (ft // 2 + 1))
                        if ft >= 2:
                            e.wait_ge(S["s_out"][ft % 2], 16 * (ft // 2))
                        vch(e.tensor_tensor(out=stot_sb[:],
                                                in0=so_ring[:, rf * 1024:(rf + 1) * 1024],
                                                in1=sp_ring[:, rf * 1024:(rf + 1) * 1024],
                                                op=ALU.add))
                        vsy(e)
                        vch(e.tensor_tensor(out=stot_sb[:], in0=stot_sb[:],
                                            in1=gh_ring[:, rf * 1024:(rf + 1) * 1024],
                                            op=ALU.mult))
                        vsy(e)
                        vch(e.tensor_tensor(out=out_ring[:, rf * 1024:(rf + 1) * 1024],
                                            in0=uh_ring[:, rf * 1024:(rf + 1) * 1024],
                                            in1=stot_sb[:], op=ALU.add))
                        vinc(e, S["s_fin"], 1)
                    e.wait_ge(S["s_out"][0], 16 * (NTH // 2))
                    e.wait_ge(S["s_out"][1], 16 * (NTH // 2))
                    fence(e)

                @blk.scalar
                def _(e):
                    e.wait_ge(S["s_out"][0], 16 * (NTH // 2))
                    e.wait_ge(S["s_out"][1], 16 * (NTH // 2))
                    fence(e)
        FN[0] += 1
    else:
        with nc.Block("stub") as blk:
            @blk.sync
            def _(e):
                e.dma_start(out=y_out[:], in_=u_half[:]).then_inc(S["s_out"][0], 16)
                e.wait_ge(S["s_out"][0], 16)
                fence(e)
            for en in ("gpsimd", "scalar", "vector", "tensor"):
                getattr(blk, en)(fence)
        FN[0] += 1

    st.close()
    return nc



_NC_CACHE = [None]


def kernel(**inputs):
    u = np.asarray(inputs["L0_attn_u"])
    W_in = np.asarray(inputs["L0_attn_W_in"])
    W_out = np.asarray(inputs["L0_attn_W_out"])
    dt_bias = np.asarray(inputs["L0_attn_dt_bias"])
    B_bias = np.asarray(inputs["L0_attn_B_bias"])
    C_bias = np.asarray(inputs["L0_attn_C_bias"])
    B_norm_w = np.asarray(inputs["L0_attn_B_norm_w"])
    C_norm_w = np.asarray(inputs["L0_attn_C_norm_w"])
    Dv = np.asarray(inputs["L0_attn_D"])
    w_gate = np.asarray(inputs["L0_ffn_w_gate"])
    w_up = np.asarray(inputs["L0_ffn_w_up"])
    w_down = np.asarray(inputs["L0_ffn_w_down"])

    if _NC_CACHE[0] is None:
        _NC_CACHE[0] = build_nc()
    nc = _NC_CACHE[0]

    f32 = np.float32
    consts = dict(
        ident_in=np.eye(128, dtype=ml_dtypes.bfloat16),
        cum_in=np.triu(np.ones((128, 128), f32)),
        su_in=np.tril(np.ones((128, 128), f32), -1),
        strict_in=np.triu(np.ones((128, 128)), 1).astype(ml_dtypes.bfloat16),
        sd_in=np.eye(128, k=-1, dtype=f32),
        oneh_in=np.eye(1, 128, k=127, dtype=f32),
        onesc_in=np.ones((128, 1), f32),
        aones_in=np.ones((1, 128), f32),
    )
    nonce_val = int(np.random.randint(1, 2 ** 31 - 1))
    consts["nonce_in"] = np.array([[nonce_val]], np.int32)
    consts["noncet_in"] = np.full((128, 4), nonce_val, np.int32)

    in_maps = []
    for k in range(8):
        b, g = k // 2, k % 2
        me = k & 1
        zc = W_in[:, g * 1024:(g + 1) * 1024]
        xc = W_in[:, 2048 + g * 1024:2048 + (g + 1) * 1024]
        Bc = W_in[:, 4096:4224]
        Cc = W_in[:, 4224:4352]
        dtc = W_in[:, 4352 + g * NH:4352 + (g + 1) * NH]
        Ac = W_in[:, 4384 + g * NH:4384 + (g + 1) * NH]
        trc = W_in[:, 4416 + g * NH:4416 + (g + 1) * NH]
        angc = W_in[:, 4448:4480]
        w_in_k = np.ascontiguousarray(
            np.concatenate([xc, zc, Bc, Cc, dtc, Ac, trc, angc], axis=1))
        bb = np.ascontiguousarray(B_bias[g * NH:(g + 1) * NH])
        cb = np.ascontiguousarray(C_bias[g * NH:(g + 1) * NH])
        scat = np.zeros((128, NT), np.uint32)
        gown = np.zeros((128, NTH), np.uint32)
        gpar = np.zeros((128, NTH), np.uint32)
        p_ = np.arange(128, dtype=np.uint32)
        for tt in range(NT):
            scat[:, tt] = me * T + tt * 128 + p_
        for ft in range(NTH):
            gown[:, ft] = me * T + g * TH + ft * 128 + p_
            gpar[:, ft] = (1 - me) * T + g * TH + ft * 128 + p_
        im = dict(
            u_full=np.ascontiguousarray(u[b]),
            u_half=np.ascontiguousarray(u[b, g * TH:(g + 1) * TH]),
            w_in=w_in_k,
            w_out=np.ascontiguousarray(W_out[g * 1024:(g + 1) * 1024]),
            wg_i=w_gate, wu_i=w_up, wd_i=w_down,
            bb_bc=np.ascontiguousarray(
                np.broadcast_to(bb.reshape(1, NH * DS), (128, NH * DS))),
            cb_bc=np.ascontiguousarray(
                np.broadcast_to(cb.reshape(1, NH * DS), (128, NH * DS))),
            bbt_in=np.ascontiguousarray(bb.T),
            cbt_in=np.ascontiguousarray(cb.T),
            bnw_bc=np.ascontiguousarray(np.broadcast_to(B_norm_w.reshape(1, DS), (128, DS))),
            cnw_bc=np.ascontiguousarray(np.broadcast_to(C_norm_w.reshape(1, DS), (128, DS))),
            dtb_bc=np.ascontiguousarray(
                np.broadcast_to(dt_bias[g * NH:(g + 1) * NH].reshape(1, NH), (128, NH))),
            d_bc=np.ascontiguousarray(
                np.broadcast_to(Dv[g * NH:(g + 1) * NH].reshape(1, NH), (128, NH))),
            scat_in=scat, gown_in=gown, gpar_in=gpar,
            fscat_in=(me * 128 + p_).reshape(128, 1).astype(np.uint32),
            **consts,
        )
        in_maps.append(im)

    res = run_bass_kernel_spmd(nc, in_maps, core_ids=list(range(8)))
    kernel._last_results = res.results
    out = np.zeros((B, T, D), np.float32)
    for k in range(8):
        b, g = k // 2, k % 2
        out[b, g * TH:(g + 1) * TH] = res.results[k]["y_out"]
    return out


# version 14
# speedup vs baseline: 1.1929x; 1.1929x over previous
import contextlib
import math
import numpy as np
import ml_dtypes
import concourse.bass as bass
import concourse.mybir as mybir
from concourse.bass import IndirectOffsetOnAxis
from concourse.bass_utils import run_bass_kernel_spmd

F32 = mybir.dt.float32
BF16 = mybir.dt.bfloat16
I32 = mybir.dt.int32
U32 = mybir.dt.uint32
AF = mybir.ActivationFunctionType
ALU = mybir.AluOpType

B, T, D = 4, 4096, 1024
NH, DS, HD = 16, 128, 64
NT, TH, NTH = 32, 2048, 16
HID = 3072
CIN = 2384
EPS = 1e-5
TWO_PI = 2.0 * math.pi
MAGIC = 12582912.0
HALF_PI = 0.5 * math.pi
DEQ = 1.0 / 127.0

DEBUG = False
PHASES = ("AI", "AII", "AV", "B", "C")
POLL_CAP = 3000000

CBLK = [(0, 512), (512, 1024), (1024, 1536), (1536, 2048), (2048, 2384)]


def build_nc():
    nc = bass.Bass("TRN2", target_bir_lowering=False)
    dbgk = "ExternalOutput" if DEBUG else "Internal"

    def inp(name, shape, dt):
        return nc.dram_tensor(name, shape, dt, kind="ExternalInput").ap()

    u_full = inp("u_full", [T, D], F32)
    u_half = inp("u_half", [TH, D], F32)
    w_in = inp("w_in", [D, CIN], F32)
    w_out = inp("w_out", [D, D], F32)
    wg_i = inp("wg_i", [HID, D], I32)
    wu_i = inp("wu_i", [HID, D], I32)
    wd_i = inp("wd_i", [D, HID], I32)
    bb_bc = inp("bb_bc", [128, NH * DS], F32)
    cb_bc = inp("cb_bc", [128, NH * DS], F32)
    bbt_in = inp("bbt_in", [DS, NH], F32)
    cbt_in = inp("cbt_in", [DS, NH], F32)
    bnw_bc = inp("bnw_bc", [128, DS], F32)
    cnw_bc = inp("cnw_bc", [128, DS], F32)
    dtb_bc = inp("dtb_bc", [128, NH], F32)
    d_bc = inp("d_bc", [128, NH], F32)
    ident_in = inp("ident_in", [128, 128], BF16)
    cum_in = inp("cum_in", [128, 128], F32)
    su_in = inp("su_in", [128, 128], F32)
    strict_in = inp("strict_in", [128, 128], BF16)
    sd_in = inp("sd_in", [128, 128], F32)
    oneh_in = inp("oneh_in", [1, 128], F32)
    onesc_in = inp("onesc_in", [128, 1], F32)
    aones_in = inp("aones_in", [1, 128], F32)
    scat_in = inp("scat_in", [128, NT], U32)
    gown_in = inp("gown_in", [128, NTH], U32)
    gpar_in = inp("gpar_in", [128, NTH], U32)
    fscat_in = inp("fscat_in", [128, 1], U32)
    nonce_in = inp("nonce_in", [1, 1], I32)
    noncet_in = inp("noncet_in", [128, 4], I32)

    def scr(name, shape, dt, dbg=False):
        return nc.dram_tensor(name, shape, dt, kind=(dbgk if dbg else "Internal")).ap()

    xn_dram = scr("xn_dram", [T, D], BF16, True)
    xnh_dram = scr("xnh_dram", [TH, D], BF16, True)
    x_dram = scr("x_dram", [T, NH * HD], BF16, True)
    z_dram = scr("z_dram", [T, NH * HD], BF16, True)
    k_dram = scr("k_dram", [T, NH * DS], BF16, True)
    q_dram = scr("q_dram", [T, NH * DS], BF16, True)
    yg_dram = scr("yg_dram", [T, NH * HD], BF16, True)
    g_dram = scr("g_dram", [TH, D], BF16, True)
    wg_bf = scr("wg_bf", [HID, D], BF16)
    wu_bf = scr("wu_bf", [HID, D], BF16)
    wd_bf = scr("wd_bf", [D, HID], BF16)
    wgt_dram = scr("wgt_dram", [24, 8, 128, 128], BF16)
    wdt_dram = scr("wdt_dram", [HID, 1024], BF16)
    wut_dram = scr("wut_dram", [24, 8, 128, 128], BF16)
    xchg = nc.dram_tensor("xchg", [2 * T, D], F32, addr_space="Shared", kind="Internal").ap()
    flags = nc.dram_tensor("flags", [2 * 128, 4], I32, addr_space="Shared", kind="Internal").ap()
    sdbg = scr("sdbg", [T, D], F32, True) if DEBUG else None
    y_out = nc.dram_tensor("y_out", [TH, D], F32, kind="ExternalOutput").ap()

    st = contextlib.ExitStack()

    def sb(name, shape, dt):
        return st.enter_context(nc.sbuf_tensor(name, shape, dt))

    ident_sb = sb("ident_sb", [128, 128], BF16)
    cum_sb = sb("cum_sb", [128, 128], F32)
    su_sb = sb("su_sb", [128, 128], F32)
    strict_sb = sb("strict_sb", [128, 128], BF16)
    sd_sb = sb("sd_sb", [128, 128], F32)
    oneh_sb = sb("oneh_sb", [1, 128], F32)
    onesc_sb = sb("onesc_sb", [128, 1], F32)
    aones_sb = sb("aones_sb", [1, 128], F32)
    aones_bf = sb("aones_bf", [1, 128], BF16)
    mhalf = sb("mhalf", [128, 1], F32)
    halfpi_sb = sb("halfpi_sb", [128, 1], F32)
    bnw_sb = sb("bnw_sb", [128, DS], F32)
    cnw_sb = sb("cnw_sb", [128, DS], F32)
    dtb_sb = sb("dtb_sb", [128, NH], F32)
    dbc_sb = sb("dbc_sb", [128, NH], F32)
    bbt_bf = sb("bbt_bf", [DS, NH], BF16)
    cbt_bf = sb("cbt_bf", [DS, NH], BF16)
    t4p_sb = sb("t4p_sb", [DS, NH], F32)
    t4row_f = sb("t4row_f", [1, NH], F32)
    t4row_bf = sb("t4row_bf", [1, NH], BF16)
    qk_sb = sb("qk_sb", [128, NT * NH], F32)
    coef_sb = sb("coef_sb", [128, NT * NH], F32)
    gcx_sb = sb("gcx_sb", [128, NT * NH], F32)
    wsg_sb = sb("wsg_sb", [128, NT * NH], F32)
    expell_sb = sb("expell_sb", [128, NT * NH], F32)
    cdb_sb = sb("cdb_sb", [128, NT * NH], F32)
    H_sb = sb("H_sb", [128, 1024], F32)
    Hbf_sb = sb("Hbf_sb", [128, 1024], BF16)
    Hs_sb = sb("Hs_sb", [128, 1024], F32)
    sidx_sb = sb("sidx_sb", [128, NT], U32)
    goidx_sb = sb("goidx_sb", [128, NTH], U32)
    gpidx_sb = sb("gpidx_sb", [128, NTH], U32)
    fidx_sb = sb("fidx_sb", [128, 1], U32)
    nonce_sb = sb("nonce_sb", [1, 1], I32)
    nonce_t_sb = sb("nonce_t_sb", [128, 4], I32)
    flagrd_sb = sb("flagrd_sb", [2, 4], I32)

    S = {}
    names = ["fence", "s_ld", "s_t4",
             "s_cast_wd", "s_cast_win",
             "s_prep_wd", "s_prep_wg", "s_prep_wu",
             "s_sq", "s_var", "s_rstd", "s_rstd2", "s_xn",
             "s_mm", "s_ev", "s_evx", "s_sp", "s_bnx", "s_bn", "s_tp", "s_bte",
             "s_qkm", "s_qke",
             "s_v2", "s_a2", "s_tm", "s_vb", "s_tg", "s_sg", "s_vg",
             "s_vp", "s_thm", "s_vc", "s_rrv", "s_sc3", "s_ropev", "s_ropeg",
             "s_fx", "s_fmm", "s_fev", "s_fg", "s_fge", "s_fdm",
             "s_smm", "s_sw", "s_ymm", "s_wmm",
             "s_bz", "s_xwg", "s_dxp", "s_t2", "s_hsg", "s_bev1", "s_bev",
             "s_hup",
             "s_yt", "s_om", "s_se", "s_fl", "s_bar",
             "s_fin", "s_wo"]
    quad_names = ["s_u", "s_xnw"]
    pair_names = ["s_xt", "s_xzw", "s_kqw", "s_kq", "s_bx",
                  "s_fw", "s_fgw", "s_ygw", "s_sc", "s_gat", "s_uh", "s_gh",
                  "s_out"]
    for n in names:
        S[n] = st.enter_context(nc.semaphore(n))
    for n in pair_names:
        S[n] = [st.enter_context(nc.semaphore(n + "0")),
                st.enter_context(nc.semaphore(n + "1"))]
    for n in quad_names:
        S[n] = [st.enter_context(nc.semaphore(n + str(q))) for q in range(4)]
    S["s_cast_wg"] = S["s_cast_wd"]
    S["s_cast_wu"] = S["s_cast_wd"]
    S["s_cast_misc"] = S["s_cast_wd"]
    S["s_cast_bbcb"] = S["s_cast_win"]

    for n in ["s_vself", "s_aself", "s_gself"]:
        S[n] = st.enter_context(nc.semaphore(n))
    VC = {"v": 0, "a": 0, "g": 0}

    def vch(instr):
        VC["v"] += 1
        instr.then_inc(S["s_vself"], 1)
        return instr

    def vsy(e):
        e.wait_ge(S["s_vself"], VC["v"])

    def ach(instr):
        VC["a"] += 1
        instr.then_inc(S["s_aself"], 1)
        return instr

    def asy(e):
        e.wait_ge(S["s_aself"], VC["a"])

    def gch(instr):
        VC["g"] += 1
        instr.then_inc(S["s_gself"], 1)
        return instr

    def gsy(e):
        e.wait_ge(S["s_gself"], VC["g"])

    def vinc(e, sem, k=1):
        vsy(e)
        e.sem_inc(sem, k)

    def ainc(e, sem, k=1):
        asy(e)
        e.sem_inc(sem, k)

    def ginc(e, sem, k=1):
        gsy(e)
        e.sem_inc(sem, k)

    FN = [0]

    def fence(e):
        e.sem_inc(S["fence"], 1)
        e.wait_ge(S["fence"], 5 * FN[0])

    with nc.psum_tensor("ps_t4", [1, NH], F32) as ps_t4:
        with nc.Block("startup") as blk:
            @blk.sync
            def _(e):
                for dst, src in [(bnw_sb, bnw_bc), (cnw_sb, cnw_bc), (dtb_sb, dtb_bc),
                                 (dbc_sb, d_bc), (ident_sb, ident_in), (cum_sb, cum_in),
                                 (su_sb, su_in), (strict_sb, strict_in), (sd_sb, sd_in),
                                 (oneh_sb, oneh_in), (onesc_sb, onesc_in),
                                 (aones_sb, aones_in), (sidx_sb, scat_in),
                                 (goidx_sb, gown_in), (gpidx_sb, gpar_in),
                                 (fidx_sb, fscat_in), (nonce_sb, nonce_in),
                                 (nonce_t_sb, noncet_in), (t4p_sb, bbt_in)]:
                    e.dma_start(out=dst[:], in_=src).then_inc(S["s_ld"], 16)
                fence(e)

            @blk.gpsimd
            def _(e):
                e.dma_start(out=bbt_bf[:], in_=bbt_in).then_inc(S["s_cast_misc"], 16)
                e.dma_start(out=cbt_bf[:], in_=cbt_in).then_inc(S["s_cast_misc"], 16)
                e.dma_start(out=aones_bf[:], in_=aones_in).then_inc(S["s_cast_misc"], 16)
                e.wait_ge(S["s_cast_misc"], 16 * 3)
                for (srcw, dstw, sw) in [(wd_i, wd_bf, "s_cast_wd"),
                                         (wg_i, wg_bf, "s_cast_wg"),
                                         (wu_i, wu_bf, "s_cast_wu")]:
                    nrow = srcw.shape[0]
                    q = nrow // 4
                    for j in range(4):
                        e.dma_start(out=dstw[j * q:(j + 1) * q, :],
                                    in_=srcw[j * q:(j + 1) * q, :]).then_inc(S[sw], 16)
                e.memset(mhalf[:], -0.5)
                e.memset(halfpi_sb[:], HALF_PI)
                e.memset(Hs_sb[:], 0.0)
                fence(e)

            @blk.vector
            def _(e):
                e.wait_ge(S["s_ld"], 16 * 19)
                e.wait_ge(S["s_cast_misc"], 16 * 3)
                e.scalar_tensor_tensor(out=t4p_sb[:], in0=t4p_sb[:], scalar=1.0,
                                       in1=cbt_bf[:], op0=ALU.mult,
                                       op1=ALU.mult).then_inc(S["s_t4"], 1)
                e.memset(H_sb[:], 0.0)
                e.memset(Hbf_sb[:], 0.0)
                e.wait_ge(S["s_t4"], 2)
                vch(e.tensor_copy(out=t4row_f[:], in_=ps_t4[:]))
                vsy(e)
                e.tensor_copy(out=t4row_bf[:], in_=t4row_f[:]).then_inc(S["s_t4"], 1)
                fence(e)

            @blk.tensor
            def _(e):
                e.wait_ge(S["s_t4"], 1)
                e.matmul(out=ps_t4[:], lhsT=onesc_sb[:], rhs=t4p_sb[:],
                         start=True, stop=True).then_inc(S["s_t4"], 1)
                fence(e)

            @blk.scalar
            def _(e):
                fence(e)
        FN[0] += 1

    with nc.sbuf_tensor("bn_sb", [128, NT * 256], BF16) as bn_sb, \
         nc.sbuf_tensor("smalls_sb", [128, NT * 80], F32) as smalls_sb:

      if "AI" in PHASES:
        with contextlib.ExitStack() as sAI:
            def a(name, shape, dt):
                return sAI.enter_context(nc.sbuf_tensor(name, shape, dt))

            win_sb = a("win_sb", [128, 8 * CIN], BF16)
            u_ring = a("u_ring", [128, 4 * D], F32)
            sq_scr = a("sq_scr", [128, D], F32)
            cp_scr = a("cp_scr", [128, D], F32)
            ssum_c = a("ssum_c", [128, 4], F32)
            ssq_c = a("ssq_c", [128, 4], F32)
            mean_c = a("mean_c", [128, 4], F32)
            m2_c = a("m2_c", [128, 4], F32)
            var_c = a("var_c", [128, 4], F32)
            rstd_c = a("rstd_c", [128, 4], F32)
            lnv_c = a("lnv_c", [128, 4], F32)
            lnv2_c = a("lnv2_c", [128, 2], F32)
            xn_ring = a("xn_ring", [128, 4 * D], BF16)
            xnt_ring = a("xnt_ring", [128, 2 * 8 * 512], BF16)
            x_ev = a("x_ev", [128, 2 * 1024], BF16)
            z_ev = a("z_ev", [128, 2 * 1024], BF16)
            bc_ev = a("bc_ev", [128, 256], F32)
            rstdbc = a("rstdbc", [128, 2], F32)
            tb_scr = a("tb_scr", [128, 256], F32)
            bt_sb = a("bt_sb", [128, 1024], BF16)
            junk_sb = a("junk_sb", [128, 128], F32)
            t1_col = a("t1_col", [128, 4], F32)
            ps_ip = [sAI.enter_context(nc.psum_tensor(f"ps_ip{i}", [128, 512], F32))
                     for i in range(4)]
            ps_ip.append(sAI.enter_context(nc.psum_tensor("ps_ip4", [128, 336], F32)))
            ps_bt = sAI.enter_context(nc.psum_tensor("ps_bt", [128, 1024], BF16))
            ps_qk = sAI.enter_context(nc.psum_tensor("ps_qk", [128, 4 * NH], F32))

            with nc.Block("AI") as blk:
                @blk.gpsimd
                def _(e):
                    for dc in range(8):
                        e.dma_start(out=win_sb[:, dc * CIN:(dc + 1) * CIN],
                                    in_=w_in[dc * 128:(dc + 1) * 128, :]
                                    ).then_inc(S["s_cast_win"], 16)
                    def xn_store(m):
                        e.wait_ge(S["s_xn"], m + 1)
                        dst = xn_dram if m < NT else xnh_dram
                        mm_ = m if m < NT else m - NT
                        e.dma_start(out=dst[mm_ * 128:(mm_ + 1) * 128, :],
                                    in_=xn_ring[:, (m % 4) * D:(m % 4 + 1) * D]
                                    ).then_inc(S["s_xnw"][m % 4], 16)

                    def xz_store(m):
                        e.wait_ge(S["s_evx"], m + 1)
                        e.dma_start(out=x_dram[m * 128:(m + 1) * 128, :],
                                    in_=x_ev[:, (m % 2) * 1024:(m % 2 + 1) * 1024]
                                    ).then_inc(S["s_xzw"][m % 2], 16)
                        e.wait_ge(S["s_ev"], m + 1)
                        e.dma_start(out=z_dram[m * 128:(m + 1) * 128, :],
                                    in_=z_ev[:, (m % 2) * 1024:(m % 2 + 1) * 1024]
                                    ).then_inc(S["s_xzw"][m % 2], 16)

                    for step in range(NT + 4):
                        if step >= 4:
                            xz_store(step - 4)
                        if step < NT:
                            xn_store(step)
                    for j in range(NTH):
                        xn_store(NT + j)
                    for q_ in range(4):
                        e.wait_ge(S["s_xnw"][q_], 16 * 12)
                    e.wait_ge(S["s_xzw"][0], 32 * 16)
                    e.wait_ge(S["s_xzw"][1], 32 * 16)
                    e.wait_ge(S["s_cast_win"], 16 * 8)
                    fence(e)

                @blk.sync
                def _(e):
                    for m in range(NT):
                        if m >= 4:
                            e.wait_ge(S["s_xn"], m - 3)
                        e.dma_start(out=u_ring[:, (m % 4) * D:(m % 4 + 1) * D],
                                    in_=u_full[m * 128:(m + 1) * 128, :]).then_inc(S["s_u"][m % 4], 16)
                        if m % 4 == 3:
                            s4 = m // 4
                            for q_ in range(4):
                                e.wait_ge(S["s_xnw"][q_], 16 * (s4 + 1))
                            if s4 >= 2:
                                e.wait_ge(S["s_mm"], 4 * (s4 - 1))
                            for dc in range(8):
                                base = ((s4 % 2) * 8 + dc) * 512
                                e.dma_start(
                                    out=xnt_ring[:, base:base + 512],
                                    in_=xn_dram[s4 * 512:(s4 + 1) * 512,
                                                dc * 128:(dc + 1) * 128],
                                    transpose=True).then_inc(S["s_xt"][s4 % 2], 16)
                    for j in range(NTH):
                        m = NT + j
                        e.wait_ge(S["s_xn"], m - 3)
                        e.dma_start(out=u_ring[:, (m % 4) * D:(m % 4 + 1) * D],
                                    in_=u_half[j * 128:(j + 1) * 128, :]).then_inc(S["s_u"][m % 4], 16)
                    for q_ in range(4):
                        e.wait_ge(S["s_u"][q_], 16 * 12)
                    e.wait_ge(S["s_xt"][0], 128 * 4)
                    e.wait_ge(S["s_xt"][1], 128 * 4)
                    fence(e)

                @blk.scalar
                def _(e):
                    def sq(m):
                        e.wait_ge(S["s_u"][m % 4], 16 * (m // 4 + 1))
                        if m >= 4:
                            e.wait_ge(S["s_var"], m - 3)
                        asy(e)
                        ach(e.activation(out=sq_scr[:],
                                         in_=u_ring[:, (m % 4) * D:(m % 4 + 1) * D],
                                         func=AF.Square,
                                         accum_out=ssq_c[:, m % 4:m % 4 + 1]))
                        ainc(e, S["s_sq"], 1)

                    def lnexp_stats(m):
                        r = m % 4
                        e.wait_ge(S["s_var"], m + 1)
                        asy(e)
                        ach(e.activation(out=lnv_c[:, r:r + 1],
                                         in_=var_c[:, r:r + 1], func=AF.Ln))
                        asy(e)
                        ach(e.activation(out=rstd_c[:, r:r + 1],
                                         in_=lnv_c[:, r:r + 1], func=AF.Exp,
                                         scale=-0.5))
                        ainc(e, S["s_rstd"], 1)

                    def act_bn(m):
                        e.wait_ge(S["s_mm"], m + 1)
                        if m >= 2:
                            e.wait_ge(S["s_xzw"][m % 2], 32 * (m // 2))
                        e.copy(out=x_ev[:, (m % 2) * 1024:(m % 2) * 1024 + 512],
                               in_=ps_ip[0][:])
                        asy(e)
                        ach(e.copy(out=x_ev[:, (m % 2) * 1024 + 512:(m % 2 + 1) * 1024],
                                   in_=ps_ip[1][:]))
                        ainc(e, S["s_evx"], 1)
                        e.copy(out=smalls_sb[:, m * 80:(m + 1) * 80],
                               in_=ps_ip[4][:, 256:336]).then_inc(S["s_sp"], 1)
                        e.wait_ge(S["s_bnx"], m + 1)
                        asy(e)
                        ach(e.activation(out=lnv2_c[:], in_=rstdbc[:, 0:2],
                                         func=AF.Ln))
                        asy(e)
                        ach(e.activation(out=rstdbc[:, 0:2], in_=lnv2_c[:],
                                         func=AF.Exp, scale=-0.5))
                        ainc(e, S["s_rstd2"], 1)

                    for step in range(NT + 4):
                        if 1 <= step <= NT:
                            lnexp_stats(step - 1)
                        if step < NT:
                            sq(step)
                        if step >= 4:
                            act_bn(step - 4)
                    for j in range(NTH):
                        m = NT + j
                        sq(m)
                        if j >= 1:
                            lnexp_stats(m - 1)
                    lnexp_stats(NT + NTH - 1)
                    fence(e)

                @blk.vector
                def _(e):
                    def stats_xn(m):
                        e.wait_ge(S["s_u"][m % 4], 16 * (m // 4 + 1))
                        if m >= 4:
                            e.wait_ge(S["s_xnw"][m % 4], 16 * (m // 4))
                        ur = u_ring[:, (m % 4) * D:(m % 4 + 1) * D]
                        r = m % 4
                        vsy(e)
                        vch(e.tensor_scalar(out=cp_scr[:], in0=ur, scalar1=1.0,
                                            scalar2=None, op0=ALU.mult, op1=ALU.add,
                                            accum_out=ssum_c[:, r:r + 1]))
                        e.wait_ge(S["s_sq"], m + 1)
                        vsy(e)
                        vch(e.tensor_scalar(out=mean_c[:, r:r + 1], in0=ssum_c[:, r:r + 1],
                                            scalar1=1.0 / D, scalar2=None, op0=ALU.mult))
                        vsy(e)
                        vch(e.tensor_tensor(out=m2_c[:, r:r + 1], in0=mean_c[:, r:r + 1],
                                            in1=mean_c[:, r:r + 1], op=ALU.mult))
                        vch(e.tensor_scalar(out=var_c[:, r:r + 1], in0=ssq_c[:, r:r + 1],
                                            scalar1=1.0 / D, scalar2=EPS,
                                            op0=ALU.mult, op1=ALU.add))
                        vsy(e)
                        e.tensor_tensor(out=var_c[:, r:r + 1], in0=var_c[:, r:r + 1],
                                        in1=m2_c[:, r:r + 1],
                                        op=ALU.subtract).then_inc(S["s_var"], 1)
                        e.wait_ge(S["s_rstd"], m + 1)
                        vch(e.tensor_scalar(out=xn_ring[:, r * D:(r + 1) * D],
                                            in0=ur, scalar1=mean_c[:, r:r + 1],
                                            scalar2=rstd_c[:, r:r + 1],
                                            op0=ALU.subtract, op1=ALU.mult))
                        vinc(e, S["s_xn"], 1)

                    def bn_item(m):
                        s4 = m // 4
                        jj = m % 4
                        r = m % 2
                        e.wait_ge(S["s_mm"], m + 1)
                        if m >= 2:
                            e.wait_ge(S["s_xzw"][m % 2], 32 * (m // 2))
                        e.tensor_copy(out=z_ev[:, r * 1024:r * 1024 + 512], in_=ps_ip[2][:])
                        e.tensor_copy(out=z_ev[:, r * 1024 + 512:r * 1024 + 1024], in_=ps_ip[3][:])
                        vch(e.tensor_copy(out=bc_ev[:], in_=ps_ip[4][:, 0:256]))
                        vinc(e, S["s_ev"], 1)
                        vsy(e)
                        vch(e.scalar_tensor_tensor(out=junk_sb[:], in0=bc_ev[:, 0:128],
                                                   scalar=1.0 / DS, in1=bc_ev[:, 0:128],
                                                   op0=ALU.mult, op1=ALU.mult,
                                                   accum_out=rstdbc[:, 0:1]))
                        vsy(e)
                        vch(e.scalar_tensor_tensor(out=junk_sb[:], in0=bc_ev[:, 128:256],
                                                   scalar=1.0 / DS, in1=bc_ev[:, 128:256],
                                                   op0=ALU.mult, op1=ALU.mult,
                                                   accum_out=rstdbc[:, 1:2]))
                        vsy(e)
                        e.tensor_scalar(out=rstdbc[:, 0:2], in0=rstdbc[:, 0:2],
                                        scalar1=EPS, scalar2=None,
                                        op0=ALU.add).then_inc(S["s_bnx"], 1)
                        e.wait_ge(S["s_rstd2"], m + 1)
                        vch(e.tensor_scalar(out=tb_scr[:, 0:128], in0=bc_ev[:, 0:128],
                                            scalar1=rstdbc[:, 0:1], scalar2=None,
                                            op0=ALU.mult))
                        vch(e.tensor_scalar(out=tb_scr[:, 128:256], in0=bc_ev[:, 128:256],
                                            scalar1=rstdbc[:, 1:2], scalar2=None,
                                            op0=ALU.mult))
                        vsy(e)
                        vch(e.tensor_tensor(out=bn_sb[:, m * 256:m * 256 + 128],
                                            in0=tb_scr[:, 0:128], in1=bnw_sb[:],
                                            op=ALU.mult))
                        vch(e.tensor_tensor(out=bn_sb[:, m * 256 + 128:(m + 1) * 256],
                                            in0=tb_scr[:, 128:256], in1=cnw_sb[:],
                                            op=ALU.mult))
                        vinc(e, S["s_bn"], 1)
                        vsy(e)
                        vch(e.scalar_tensor_tensor(out=junk_sb[:],
                                                   in0=bn_sb[:, m * 256:m * 256 + 128],
                                                   scalar=1.0,
                                                   in1=bn_sb[:, m * 256 + 128:(m + 1) * 256],
                                                   op0=ALU.mult, op1=ALU.mult,
                                                   accum_out=t1_col[:, jj:jj + 1]))
                        if jj == 3:
                            e.wait_ge(S["s_tp"], s4 + 1)
                            vch(e.tensor_copy(out=bt_sb[:], in_=ps_bt[:]))
                            vinc(e, S["s_bte"], 1)
                            e.wait_ge(S["s_qkm"], s4 + 1)
                            vsy(e)
                            for j2 in range(4):
                                m2 = 4 * s4 + j2
                                ts = e.tensor_scalar(out=qk_sb[:, m2 * NH:(m2 + 1) * NH],
                                                     in0=ps_qk[:, j2 * NH:(j2 + 1) * NH],
                                                     scalar1=t1_col[:, j2:j2 + 1],
                                                     scalar2=None, op0=ALU.add)
                                if j2 == 3:
                                    ts.then_inc(S["s_qke"], 1)

                    for step in range(NT + 4):
                        if step >= 4:
                            bn_item(step - 4)
                        if step < NT:
                            stats_xn(step)
                    for j in range(NTH):
                        stats_xn(NT + j)
                    fence(e)

                @blk.tensor
                def _(e):
                    e.wait_ge(S["s_cast_win"], 16 * 8)
                    e.wait_ge(S["s_cast_misc"], 16 * 3)
                    for m in range(NT):
                        s4 = m // 4
                        e.wait_ge(S["s_xt"][s4 % 2], 128 * (s4 // 2 + 1))
                        if m > 0:
                            e.wait_ge(S["s_ev"], m)
                            e.wait_ge(S["s_evx"], m)
                            e.wait_ge(S["s_sp"], m)
                        base = ((s4 % 2) * 8)
                        for dc in range(8):
                            lt = xnt_ring[:, (base + dc) * 512 + (m % 4) * 128:
                                          (base + dc) * 512 + (m % 4 + 1) * 128]
                            for ci, (c0, c1) in enumerate(CBLK):
                                mm = e.matmul(out=ps_ip[ci][:, 0:(c1 - c0)], lhsT=lt,
                                              rhs=win_sb[:, dc * CIN + c0:dc * CIN + c1],
                                              start=(dc == 0), stop=(dc == 7))
                                if dc == 7 and ci == 4:
                                    mm.then_inc(S["s_mm"], 1)
                        if m % 4 == 3:
                            e.wait_ge(S["s_bn"], m + 1)
                            if s4 > 0:
                                e.wait_ge(S["s_bte"], s4)
                            for jj in range(4):
                                mj = 4 * s4 + jj
                                e.transpose(out=ps_bt[:, jj * 256:jj * 256 + 128],
                                            in_=bn_sb[:, mj * 256:mj * 256 + 128],
                                            identity=ident_sb[:])
                                tp = e.transpose(out=ps_bt[:, jj * 256 + 128:(jj + 1) * 256],
                                                 in_=bn_sb[:, mj * 256 + 128:(mj + 1) * 256],
                                                 identity=ident_sb[:])
                                if jj == 3:
                                    tp.then_inc(S["s_tp"], 1)
                            e.wait_ge(S["s_bte"], s4 + 1)
                            if s4 > 0:
                                e.wait_ge(S["s_qke"], s4)
                            e.wait_ge(S["s_t4"], 3)
                            for jj in range(4):
                                e.matmul(out=ps_qk[:, jj * NH:(jj + 1) * NH],
                                         lhsT=bt_sb[:, jj * 256:jj * 256 + 128],
                                         rhs=cbt_bf[:], start=True, stop=False)
                                e.matmul(out=ps_qk[:, jj * NH:(jj + 1) * NH],
                                         lhsT=bt_sb[:, jj * 256 + 128:(jj + 1) * 256],
                                         rhs=bbt_bf[:], start=False, stop=False)
                                mmq = e.matmul(out=ps_qk[:, jj * NH:(jj + 1) * NH],
                                               lhsT=aones_bf[:], rhs=t4row_bf[:],
                                               start=False, stop=True)
                                if jj == 3:
                                    mmq.then_inc(S["s_qkm"], 1)
                    fence(e)
        FN[0] += 1

      if "AII" in PHASES:
        with contextlib.ExitStack() as sA2:
            def a(name, shape, dt):
                return sA2.enter_context(nc.sbuf_tensor(name, shape, dt))

            dt_sb = a("dt_sb", [128, NT * NH], F32)
            adt_sb = a("adt_sb", [128, NT * NH], F32)
            beta_sb = a("beta_sb", [128, NT * NH], F32)
            betan_sb = a("betan_sb", [128, NT * NH], F32)
            gc_sb = a("gc_sb", [128, NT * NH], F32)
            g1_sb = a("g1_sb", [128, NT * NH], F32)
            dtin_sb = a("dtin_sb", [128, NT * NH], F32)
            spa_sb = a("spa_sb", [128, NT * NH], F32)
            ax_scr = a("ax_scr", [128, NT * NH], F32)
            ex_scr = a("ex_scr", [128, NT * NH], F32)
            ln_scr = a("ln_scr", [128, NT * NH], F32)
            mx_scr = a("mx_scr", [128, NT * NH], F32)
            eneg_scr = a("eneg_scr", [128, NH], F32)
            dec_scr = a("dec_scr", [128, NH], F32)
            sm3 = smalls_sb[:].rearrange("p (m c) -> p m c", c=80)
            dt3 = dt_sb[:].rearrange("p (m h) -> p m h", h=NH)

            with contextlib.ExitStack() as sG:
                ps_bn = [sG.enter_context(nc.psum_tensor(f"ps_bn{i}", [128, NH], F32))
                         for i in range(2)]
                ps_e = [sG.enter_context(nc.psum_tensor(f"ps_e{i}", [128, NH], F32))
                        for i in range(2)]
                ps_d = [sG.enter_context(nc.psum_tensor(f"ps_d{i}", [128, NH], F32))
                        for i in range(2)]

                with nc.Block("AII_gates") as blk:
                    @blk.vector
                    def _(e):
                        e.wait_ge(S["s_sp"], NT)
                        e.wait_ge(S["s_qke"], NT // 4)
                        e.tensor_tensor(out=dtin_sb[:].rearrange("p (m h) -> p m h", h=NH),
                                        in0=sm3[:, :, 0:16],
                                        in1=dtb_sb[:].unsqueeze(1).broadcast_to([128, NT, NH]),
                                        op=ALU.add).then_inc(S["s_v2"], 1)
                        e.wait_ge(S["s_a2"], 3)
                        vch(e.tensor_scalar_max(out=mx_scr[:], in0=dtin_sb[:], scalar1=0.0))
                        vsy(e)
                        vch(e.tensor_tensor(out=dt_sb[:], in0=mx_scr[:], in1=ln_scr[:],
                                            op=ALU.add))
                        vinc(e, S["s_v2"], 1)
                        e.wait_ge(S["s_a2"], 6)
                        vsy(e)
                        vch(e.tensor_tensor(out=mx_scr[:].rearrange("p (m h) -> p m h", h=NH),
                                            in0=sm3[:, :, 16:32], in1=sm3[:, :, 16:32],
                                            op=ALU.max))
                        vsy(e)
                        vch(e.tensor_scalar_max(out=mx_scr[:], in0=mx_scr[:], scalar1=0.0))
                        vsy(e)
                        vch(e.tensor_tensor(out=spa_sb[:], in0=mx_scr[:], in1=ln_scr[:],
                                            op=ALU.add))
                        vsy(e)
                        vch(e.tensor_scalar_max(out=spa_sb[:], in0=spa_sb[:], scalar1=1e-4))
                        vsy(e)
                        vch(e.scalar_tensor_tensor(out=adt_sb[:], in0=spa_sb[:], scalar=-1.0,
                                                   in1=dt_sb[:], op0=ALU.mult,
                                                   op1=ALU.mult))
                        vinc(e, S["s_v2"], 1)
                        e.wait_ge(S["s_a2"], 7)
                        vch(e.tensor_scalar(out=ex_scr[:], in0=ex_scr[:], scalar1=1.0,
                                            scalar2=None, op0=ALU.add))
                        vsy(e)
                        vch(e.reciprocal(out=beta_sb[:], in_=ex_scr[:]))
                        vinc(e, S["s_v2"], 1)
                        for m in range(NT):
                            e.wait_ge(S["s_tm"], m + 1)
                            e.tensor_copy(out=betan_sb[:, m * NH:(m + 1) * NH],
                                          in_=ps_bn[m % 2][:]).then_inc(S["s_vb"], 1)
                        vsy(e)
                        e.wait_ge(S["s_vb"], NT)
                        vch(e.scalar_tensor_tensor(out=mx_scr[:], in0=beta_sb[:], scalar=1.0,
                                                   in1=betan_sb[:], op0=ALU.add,
                                                   op1=ALU.subtract))
                        vsy(e)
                        vch(e.tensor_tensor(out=gc_sb[:], in0=mx_scr[:], in1=dt_sb[:],
                                            op=ALU.mult))
                        vch(e.tensor_tensor(out=g1_sb[:], in0=beta_sb[:], in1=dt_sb[:],
                                            op=ALU.mult))
                        vsy(e)
                        vch(e.tensor_tensor(out=mx_scr[:], in0=g1_sb[:], in1=qk_sb[:],
                                            op=ALU.mult))
                        vsy(e)
                        vch(e.tensor_tensor(out=coef_sb[:].rearrange("p (m h) -> p m h", h=NH),
                                            in0=mx_scr[:].rearrange("p (m h) -> p m h", h=NH),
                                            in1=dbc_sb[:].unsqueeze(1).broadcast_to([128, NT, NH]),
                                            op=ALU.add))
                        vinc(e, S["s_v2"], 1)
                        vsy(e)
                        for m in range(NT):
                            e.wait_ge(S["s_sg"], m + 1)
                            e.tensor_tensor(out=gcx_sb[:, m * NH:(m + 1) * NH],
                                            in0=gc_sb[:, m * NH:(m + 1) * NH],
                                            in1=eneg_scr[:], op=ALU.mult)
                            e.tensor_tensor(out=wsg_sb[:, m * NH:(m + 1) * NH],
                                            in0=gc_sb[:, m * NH:(m + 1) * NH],
                                            in1=dec_scr[:], op=ALU.mult)
                            e.tensor_tensor(out=cdb_sb[:, m * NH:(m + 1) * NH],
                                            in0=dec_scr[:],
                                            in1=expell_sb[:, m * NH:(m + 1) * NH],
                                            op=ALU.mult).then_inc(S["s_vg"], 1)
                        fence(e)

                    @blk.scalar
                    def _(e):
                        e.wait_ge(S["s_v2"], 1)
                        ach(e.activation(out=ax_scr[:], in_=dtin_sb[:],
                                         func=AF.Abs))
                        ainc(e, S["s_a2"], 1)
                        asy(e)
                        ach(e.activation(out=ex_scr[:], in_=ax_scr[:], func=AF.Exp,
                                         scale=-1.0))
                        ainc(e, S["s_a2"], 1)
                        asy(e)
                        ach(e.activation(out=ln_scr[:], in_=ex_scr[:], func=AF.Ln,
                                         bias=1.0))
                        ainc(e, S["s_a2"], 1)
                        e.wait_ge(S["s_v2"], 2)
                        ach(e.activation(out=ax_scr[:].rearrange("p (m h) -> p m h", h=NH),
                                         in_=sm3[:, :, 16:32],
                                         func=AF.Abs))
                        ainc(e, S["s_a2"], 1)
                        asy(e)
                        ach(e.activation(out=ex_scr[:], in_=ax_scr[:], func=AF.Exp,
                                         scale=-1.0))
                        ainc(e, S["s_a2"], 1)
                        asy(e)
                        ach(e.activation(out=ln_scr[:], in_=ex_scr[:], func=AF.Ln,
                                         bias=1.0))
                        ainc(e, S["s_a2"], 1)
                        e.wait_ge(S["s_v2"], 3)
                        ach(e.activation(out=ex_scr[:].rearrange("p (m h) -> p m h", h=NH),
                                         in_=sm3[:, :, 32:48], func=AF.Exp,
                                         scale=-1.0))
                        ainc(e, S["s_a2"], 1)
                        for m in range(NT):
                            e.wait_ge(S["s_tg"], m + 1)
                            if m >= 1:
                                e.wait_ge(S["s_vg"], m)
                            e.activation(out=expell_sb[:, m * NH:(m + 1) * NH],
                                         in_=ps_e[m % 2][:], func=AF.Exp)
                            e.activation(out=eneg_scr[:], in_=ps_e[m % 2][:],
                                         func=AF.Exp, scale=-1.0)
                            e.activation(out=dec_scr[:], in_=ps_d[m % 2][:],
                                         func=AF.Exp).then_inc(S["s_sg"], 1)
                        fence(e)

                    @blk.tensor
                    def _(e):
                        e.wait_ge(S["s_v2"], 4)
                        for m in range(NT):
                            if m >= 2:
                                e.wait_ge(S["s_vb"], m - 1)
                            if m < NT - 1:
                                e.matmul(out=ps_bn[m % 2][:], lhsT=sd_sb[:],
                                         rhs=beta_sb[:, m * NH:(m + 1) * NH],
                                         start=True, stop=False)
                                e.matmul(out=ps_bn[m % 2][:], lhsT=oneh_sb[:],
                                         rhs=beta_sb[0:1, (m + 1) * NH:(m + 2) * NH],
                                         start=False, stop=True).then_inc(S["s_tm"], 1)
                            else:
                                e.matmul(out=ps_bn[m % 2][:], lhsT=sd_sb[:],
                                         rhs=beta_sb[:, m * NH:(m + 1) * NH],
                                         start=True, stop=True).then_inc(S["s_tm"], 1)
                        e.wait_ge(S["s_v2"], 5)
                        for m in range(NT):
                            if m >= 2:
                                e.wait_ge(S["s_vg"], m - 1)
                                e.wait_ge(S["s_sg"], m - 1)
                            e.matmul(out=ps_e[m % 2][:], lhsT=cum_sb[:],
                                     rhs=adt_sb[:, m * NH:(m + 1) * NH],
                                     start=True, stop=True)
                            e.matmul(out=ps_d[m % 2][:], lhsT=su_sb[:],
                                     rhs=adt_sb[:, m * NH:(m + 1) * NH],
                                     start=True, stop=True).then_inc(S["s_tg"], 1)
                        fence(e)

                    @blk.sync
                    def _(e):
                        fence(e)

                    @blk.gpsimd
                    def _(e):
                        fence(e)
                FN[0] += 1

            with contextlib.ExitStack() as sR:
                def r_(name, shape, dt):
                    return sR.enter_context(nc.sbuf_tensor(name, shape, dt))
                carry_sb = r_("carry_sb", [1, 512], F32)
                prod_sb = r_("prod_sb", [128, 2 * 512], F32)
                rs1_sb = r_("rs1_sb", [128, 512], F32)
                rsi_sb = r_("rsi_sb", [128, 512], F32)
                red_s = r_("red_s", [128, 512], F32)
                rci_sb = r_("rci_sb", [128, 512], F32)
                red_c = r_("red_c", [128, 512], F32)
                c_sb = r_("c_sb", [128, 2 * 512], BF16)
                s_sb = r_("s_sb", [128, 2 * 512], BF16)
                x1b_sb = r_("x1b_sb", [128, 512], BF16)
                x2b_sb = r_("x2b_sb", [128, 512], BF16)
                m1_sb = r_("m1_sb", [128, 512], BF16)
                m2_sb = r_("m2_sb", [128, 512], BF16)
                x1q_sb = r_("x1q_sb", [128, 512], BF16)
                x2q_sb = r_("x2q_sb", [128, 512], BF16)
                m3_sb = r_("m3_sb", [128, 512], BF16)
                m4_sb = r_("m4_sb", [128, 512], BF16)
                bb_sb = r_("bb_sb", [128, NH * DS], BF16)
                cb_sb = r_("cb_sb", [128, NH * DS], BF16)
                k_tile = r_("k_tile", [128, 2 * NH * DS], BF16)
                stage_sb = r_("stage_sb", [128, HID], BF16)
                q_tile = r_("q_tile", [128, 2 * NH * DS], BF16)
                ps_thp = [sR.enter_context(nc.psum_tensor(f"ps_th{i}", [128, 512], F32))
                          for i in range(2)]
                ps_carry = sR.enter_context(nc.psum_tensor("ps_carry", [1, 512], F32))

                with nc.Block("AII_theta") as blk:
                    @blk.vector
                    def _(e):
                        e.memset(carry_sb[:], 0.0).then_inc(S["s_vc"], 1)

                        def ropeK(m):
                            r = m % 2
                            e.wait_ge(S["s_sc3"], m + 1)
                            if m == 0:
                                e.wait_ge(S["s_cast_bbcb"], 16 * 10)
                            if m >= 2:
                                e.wait_ge(S["s_kqw"][m % 2], 32 * (m // 2))
                            kv = k_tile[:, r * 2048:(r + 1) * 2048].rearrange(
                                "p (h n) -> p h n", n=DS)
                            bb3 = bb_sb[:].rearrange("p (h n) -> p h n", n=DS)
                            bn1 = bn_sb[:, m * 256:m * 256 + 32].unsqueeze(1).broadcast_to([128, NH, 32])
                            bn2 = bn_sb[:, m * 256 + 32:m * 256 + 64].unsqueeze(1).broadcast_to([128, NH, 32])
                            bnr = bn_sb[:, m * 256 + 64:m * 256 + 128].unsqueeze(1).broadcast_to([128, NH, 64])
                            cs = c_sb[:, (m % 2) * 512:(m % 2 + 1) * 512]
                            ss = s_sb[:, (m % 2) * 512:(m % 2 + 1) * 512]
                            x1b3 = x1b_sb[:].rearrange("p (h a) -> p h a", a=32)
                            x2b3 = x2b_sb[:].rearrange("p (h a) -> p h a", a=32)
                            m13 = m1_sb[:].rearrange("p (h a) -> p h a", a=32)
                            m23 = m2_sb[:].rearrange("p (h a) -> p h a", a=32)
                            vch(e.tensor_tensor(out=x1b3, in0=bn1, in1=bb3[:, :, 0:32], op=ALU.add))
                            vch(e.tensor_tensor(out=x2b3, in0=bn2, in1=bb3[:, :, 32:64], op=ALU.add))
                            e.tensor_tensor(out=kv[:, :, 64:128], in0=bnr,
                                            in1=bb3[:, :, 64:128], op=ALU.add)
                            vsy(e)
                            vch(e.tensor_tensor(out=m1_sb[:], in0=x1b_sb[:], in1=cs, op=ALU.mult))
                            vch(e.tensor_tensor(out=m2_sb[:], in0=x2b_sb[:], in1=ss, op=ALU.mult))
                            vsy(e)
                            vch(e.tensor_tensor(out=kv[:, :, 0:32], in0=m13, in1=m23, op=ALU.subtract))
                            vsy(e)
                            vch(e.tensor_tensor(out=m1_sb[:], in0=x1b_sb[:], in1=ss, op=ALU.mult))
                            vch(e.tensor_tensor(out=m2_sb[:], in0=x2b_sb[:], in1=cs, op=ALU.mult))
                            vsy(e)
                            vch(e.tensor_tensor(out=kv[:, :, 32:64], in0=m13, in1=m23,
                                                op=ALU.add))
                            vinc(e, S["s_ropev"], 1)

                        for m in range(NT):
                            r = m % 2
                            if m >= 2:
                                e.wait_ge(S["s_thm"], m - 1)
                            e.tensor_tensor(
                                out=prod_sb[:, r * 512:(r + 1) * 512].rearrange(
                                    "p (h a) -> p h a", a=32),
                                in0=sm3[:, m, 48:80].unsqueeze(1).broadcast_to([128, NH, 32]),
                                in1=dt3[:, m, :].unsqueeze(2).broadcast_to([128, NH, 32]),
                                op=ALU.mult).then_inc(S["s_vp"], 1)
                            if m >= 1:
                                ropeK(m - 1)
                            e.wait_ge(S["s_thm"], m + 1)
                            vch(e.tensor_tensor(out=carry_sb[:], in0=carry_sb[:],
                                                in1=ps_carry[:], op=ALU.add))
                            vinc(e, S["s_vc"], 1)
                            pt = ps_thp[m % 2]
                            vch(e.tensor_scalar(out=rs1_sb[:], in0=pt[:],
                                                scalar1=1.0 / TWO_PI, scalar2=MAGIC,
                                                op0=ALU.mult, op1=ALU.add))
                            vsy(e)
                            vch(e.tensor_scalar(out=rsi_sb[:], in0=rs1_sb[:],
                                                scalar1=-MAGIC, scalar2=None,
                                                op0=ALU.add))
                            vsy(e)
                            vch(e.scalar_tensor_tensor(out=red_s[:], in0=rsi_sb[:],
                                                       scalar=-TWO_PI, in1=pt[:],
                                                       op0=ALU.mult, op1=ALU.add))
                            vch(e.tensor_scalar(out=rs1_sb[:], in0=pt[:],
                                                scalar1=1.0 / TWO_PI, scalar2=0.25,
                                                op0=ALU.mult, op1=ALU.add))
                            vsy(e)
                            vch(e.tensor_scalar(out=rs1_sb[:], in0=rs1_sb[:],
                                                scalar1=MAGIC, scalar2=None,
                                                op0=ALU.add))
                            vsy(e)
                            vch(e.tensor_scalar(out=rci_sb[:], in0=rs1_sb[:],
                                                scalar1=-MAGIC, scalar2=None,
                                                op0=ALU.add))
                            vsy(e)
                            vch(e.scalar_tensor_tensor(out=red_c[:], in0=rci_sb[:],
                                                       scalar=-TWO_PI, in1=pt[:],
                                                       op0=ALU.mult, op1=ALU.add))
                            vinc(e, S["s_rrv"], 1)
                        ropeK(NT - 1)
                        fence(e)

                    @blk.tensor
                    def _(e):
                        for m in range(NT):
                            e.wait_ge(S["s_vp"], m + 1)
                            e.wait_ge(S["s_vc"], m + 1)
                            if m >= 2:
                                e.wait_ge(S["s_rrv"], m - 1)
                            r = m % 2
                            pt = ps_thp[m % 2]
                            e.matmul(out=pt[:], lhsT=cum_sb[:],
                                     rhs=prod_sb[:, r * 512:(r + 1) * 512],
                                     start=True, stop=False)
                            e.matmul(out=pt[:], lhsT=aones_sb[:], rhs=carry_sb[:],
                                     start=False, stop=True)
                            e.matmul(out=ps_carry[:], lhsT=onesc_sb[:],
                                     rhs=prod_sb[:, r * 512:(r + 1) * 512],
                                     start=True, stop=True).then_inc(S["s_thm"], 1)
                        fence(e)

                    @blk.scalar
                    def _(e):
                        for m in range(NT):
                            e.wait_ge(S["s_rrv"], m + 1)
                            if m >= 2:
                                e.wait_ge(S["s_ropev"], m - 1)
                                e.wait_ge(S["s_ropeg"], m - 1)
                            sl = (m % 2) * 512
                            asy(e)
                            ach(e.activation(out=s_sb[:, sl:sl + 512], in_=red_s[:],
                                             func=AF.Sin, scale=1.0 - 2.0 ** -20))
                            ach(e.activation(out=c_sb[:, sl:sl + 512], in_=red_c[:],
                                             func=AF.Sin, scale=1.0 - 2.0 ** -20,
                                             bias=halfpi_sb[:]))
                            ainc(e, S["s_sc3"], 1)
                        fence(e)

                    @blk.gpsimd
                    def _(e):
                        e.dma_start(out=bb_sb[:], in_=bb_bc).then_inc(S["s_cast_bbcb"], 16)
                        e.dma_start(out=cb_sb[:], in_=cb_bc).then_inc(S["s_cast_bbcb"], 16)
                        e.wait_ge(S["s_cast_bbcb"], 16 * 10)
                        for m in range(NT):
                            r = m % 2
                            e.wait_ge(S["s_sc3"], m + 1)
                            if m >= 2:
                                e.wait_ge(S["s_kqw"][m % 2], 32 * (m // 2))
                            qv = q_tile[:, r * 2048:(r + 1) * 2048].rearrange(
                                "p (h n) -> p h n", n=DS)
                            cb3 = cb_sb[:].rearrange("p (h n) -> p h n", n=DS)
                            cn1 = bn_sb[:, m * 256 + 128:m * 256 + 160].unsqueeze(1).broadcast_to([128, NH, 32])
                            cn2 = bn_sb[:, m * 256 + 160:m * 256 + 192].unsqueeze(1).broadcast_to([128, NH, 32])
                            cnr = bn_sb[:, m * 256 + 192:m * 256 + 256].unsqueeze(1).broadcast_to([128, NH, 64])
                            cs = c_sb[:, (m % 2) * 512:(m % 2 + 1) * 512]
                            ss = s_sb[:, (m % 2) * 512:(m % 2 + 1) * 512]
                            x1q3 = x1q_sb[:].rearrange("p (h a) -> p h a", a=32)
                            x2q3 = x2q_sb[:].rearrange("p (h a) -> p h a", a=32)
                            m33 = m3_sb[:].rearrange("p (h a) -> p h a", a=32)
                            m43 = m4_sb[:].rearrange("p (h a) -> p h a", a=32)
                            gch(e.tensor_tensor(out=x1q3, in0=cn1, in1=cb3[:, :, 0:32], op=ALU.add))
                            gch(e.tensor_tensor(out=x2q3, in0=cn2, in1=cb3[:, :, 32:64], op=ALU.add))
                            e.tensor_tensor(out=qv[:, :, 64:128], in0=cnr,
                                            in1=cb3[:, :, 64:128], op=ALU.add)
                            gsy(e)
                            gch(e.tensor_tensor(out=m3_sb[:], in0=x1q_sb[:], in1=cs, op=ALU.mult))
                            gch(e.tensor_tensor(out=m4_sb[:], in0=x2q_sb[:], in1=ss, op=ALU.mult))
                            gsy(e)
                            gch(e.tensor_tensor(out=qv[:, :, 0:32], in0=m33, in1=m43, op=ALU.subtract))
                            gsy(e)
                            gch(e.tensor_tensor(out=m3_sb[:], in0=x1q_sb[:], in1=ss, op=ALU.mult))
                            gch(e.tensor_tensor(out=m4_sb[:], in0=x2q_sb[:], in1=cs, op=ALU.mult))
                            gsy(e)
                            gch(e.tensor_tensor(out=qv[:, :, 32:64], in0=m33, in1=m43,
                                                op=ALU.add))
                            ginc(e, S["s_ropeg"], 1)
                            gsy(e)
                            e.dma_start(out=q_dram[m * 128:(m + 1) * 128, :],
                                        in_=q_tile[:, r * 2048:(r + 1) * 2048]
                                        ).then_inc(S["s_kqw"][m % 2], 16)
                            e.wait_ge(S["s_ropev"], m + 1)
                            e.dma_start(out=k_dram[m * 128:(m + 1) * 128, :],
                                        in_=k_tile[:, r * 2048:(r + 1) * 2048]
                                        ).then_inc(S["s_kqw"][m % 2], 16)
                        e.wait_ge(S["s_kqw"][0], 32 * 16)
                        e.wait_ge(S["s_kqw"][1], 32 * 16)
                        fence(e)

                    @blk.sync
                    def _(e):
                        e.wait_ge(S["s_cast_wd"], 16 * 15)
                        for hc in range(24):
                            sl3 = (hc % 3) * 1024
                            if hc >= 3:
                                e.wait_ge(S["s_prep_wd"], 16 * (2 * hc - 4))
                            if hc >= 1:
                                e.wait_ge(S["s_prep_wd"], 16 * (2 * hc))
                            e.dma_start(out=stage_sb[:, sl3:sl3 + 1024],
                                        in_=wd_bf[:, hc * 128:(hc + 1) * 128],
                                        transpose=True).then_inc(S["s_prep_wd"], 16)
                            e.wait_ge(S["s_prep_wd"], 16 * (2 * hc + 1))
                            e.dma_start(out=wdt_dram[hc * 128:(hc + 1) * 128, :],
                                        in_=stage_sb[:, sl3:sl3 + 1024]
                                        ).then_inc(S["s_prep_wd"], 16)
                        e.wait_ge(S["s_cast_wg"], 16 * 15)
                        e.wait_ge(S["s_prep_wd"], 16 * 48)
                        for dc in range(8):
                            if dc >= 1:
                                e.wait_ge(S["s_prep_wg"], 16 * (2 * dc))
                            e.dma_start(out=stage_sb[:], in_=wg_bf[:, dc * 128:(dc + 1) * 128],
                                        transpose=True).then_inc(S["s_prep_wg"], 16)
                            e.wait_ge(S["s_prep_wg"], 16 * (2 * dc + 1))
                            e.dma_start(out=wgt_dram[:, dc, :, :].transpose([1, 0, 2]),
                                        in_=stage_sb[:].rearrange("p (hc h) -> p hc h", h=128)
                                        ).then_inc(S["s_prep_wg"], 16)
                        e.wait_ge(S["s_cast_wu"], 16 * 15)
                        e.wait_ge(S["s_prep_wg"], 16 * 16)
                        for dc in range(8):
                            if dc >= 1:
                                e.wait_ge(S["s_prep_wu"], 16 * (2 * dc))
                            e.dma_start(out=stage_sb[:], in_=wu_bf[:, dc * 128:(dc + 1) * 128],
                                        transpose=True).then_inc(S["s_prep_wu"], 16)
                            e.wait_ge(S["s_prep_wu"], 16 * (2 * dc + 1))
                            e.dma_start(out=wut_dram[:, dc, :, :].transpose([1, 0, 2]),
                                        in_=stage_sb[:].rearrange("p (hc h) -> p hc h", h=128)
                                        ).then_inc(S["s_prep_wu"], 16)
                        fence(e)
                FN[0] += 1

    if "AV" in PHASES:
        with contextlib.ExitStack() as sF:
            def f_(name, shape, dt):
                return sF.enter_context(nc.sbuf_tensor(name, shape, dt))
            wdt_sb = f_("wdt_sb", [128, 24 * 1024], BF16)
            xnth_ring = f_("xnth_ring", [128, 8 * 1024], BF16)
            wgu_ring = f_("wgu_ring", [128, 4 * 1024], BF16)
            h_sb = f_("h_sb", [128, 24 * 1024], BF16)
            sg_scr = f_("sg_scr", [128, 1024], BF16)
            g_tile = f_("g_tile", [128, 2 * 1024], BF16)
            ps_g = sF.enter_context(nc.psum_tensor("ps_g", [128, 1024], F32))
            ps_u2 = sF.enter_context(nc.psum_tensor("ps_u2", [128, 1024], F32))
            ps_dn = sF.enter_context(nc.psum_tensor("ps_dn", [128, 2 * 1024], F32))

            with nc.Block("AV") as blk:
                @blk.sync
                def _(e):
                    e.wait_ge(S["s_prep_wg"], 16 * 16)
                    e.wait_ge(S["s_prep_wu"], 16 * 16)
                    for sl in range(2):
                        if sl > 0:
                            e.wait_ge(S["s_prep_wd"], 16 * 48)
                            for hc in range(24):
                                e.dma_start(out=wdt_sb[:, hc * 1024:(hc + 1) * 1024],
                                            in_=wdt_dram[hc * 128:(hc + 1) * 128, :].rearrange(
                                                "(o p) d -> p o d", p=128),
                                            ).then_inc(S["s_prep_wd"], 16)
                            e.wait_ge(S["s_fmm"], 48 * sl)
                        for dc in range(8):
                            e.dma_start(out=xnth_ring[:, dc * 1024:(dc + 1) * 1024],
                                        in_=xnh_dram[sl * 1024:(sl + 1) * 1024,
                                                     dc * 128:(dc + 1) * 128],
                                        transpose=True).then_inc(S["s_fx"], 16)
                        for hc in range(24):
                            g_i = sl * 24 + hc
                            rr = hc % 2
                            if g_i >= 2:
                                e.wait_ge(S["s_fmm"], 2 * g_i - 2)
                            e.dma_start(
                                out=wgu_ring[:, (rr * 2) * 1024:(rr * 2 + 1) * 1024
                                             ].rearrange("p (dc h) -> p dc h", h=128),
                                in_=wgt_dram[hc, :, :, :].transpose([1, 0, 2])
                                ).then_inc(S["s_fw"][g_i % 2], 16)
                            e.dma_start(
                                out=wgu_ring[:, (rr * 2 + 1) * 1024:(rr * 2 + 2) * 1024
                                             ].rearrange("p (dc h) -> p dc h", h=128),
                                in_=wut_dram[hc, :, :, :].transpose([1, 0, 2])
                                ).then_inc(S["s_fw"][g_i % 2], 16)
                    e.wait_ge(S["s_fx"], 16 * 16)
                    e.wait_ge(S["s_fw"][0], 32 * 12)
                    e.wait_ge(S["s_fw"][1], 32 * 12)
                    e.wait_ge(S["s_prep_wd"], 16 * 72)
                    fence(e)

                @blk.tensor
                def _(e):
                    e.wait_ge(S["s_prep_wg"], 16 * 16)
                    e.wait_ge(S["s_prep_wu"], 16 * 16)
                    for sl in range(2):
                        e.wait_ge(S["s_fx"], 16 * 8 * (sl + 1))
                        for hc in range(24):
                            g_i = sl * 24 + hc
                            rr = hc % 2
                            e.wait_ge(S["s_fw"][g_i % 2], 32 * (g_i // 2 + 1))
                            if g_i >= 1:
                                e.wait_ge(S["s_fg"], g_i)
                                e.wait_ge(S["s_fev"], g_i)
                            for dc in range(8):
                                lt = wgu_ring[:, (rr * 2) * 1024 + dc * 128:
                                              (rr * 2) * 1024 + (dc + 1) * 128]
                                e.matmul(out=ps_g[:, 0:512], lhsT=lt,
                                         rhs=xnth_ring[:, dc * 1024:dc * 1024 + 512],
                                         start=(dc == 0), stop=(dc == 7))
                                mm = e.matmul(out=ps_g[:, 512:1024], lhsT=lt,
                                              rhs=xnth_ring[:, dc * 1024 + 512:(dc + 1) * 1024],
                                              start=(dc == 0), stop=(dc == 7))
                                if dc == 7:
                                    mm.then_inc(S["s_fmm"], 1)
                            for dc in range(8):
                                lt = wgu_ring[:, (rr * 2 + 1) * 1024 + dc * 128:
                                              (rr * 2 + 1) * 1024 + (dc + 1) * 128]
                                e.matmul(out=ps_u2[:, 0:512], lhsT=lt,
                                         rhs=xnth_ring[:, dc * 1024:dc * 1024 + 512],
                                         start=(dc == 0), stop=(dc == 7))
                                mm = e.matmul(out=ps_u2[:, 512:1024], lhsT=lt,
                                              rhs=xnth_ring[:, dc * 1024 + 512:(dc + 1) * 1024],
                                              start=(dc == 0), stop=(dc == 7))
                                if dc == 7:
                                    mm.then_inc(S["s_fmm"], 1)
                        e.wait_ge(S["s_fev"], 24 * (sl + 1))
                        e.wait_ge(S["s_prep_wd"], 16 * 72)
                        for tt in range(8):
                            ti = sl * 8 + tt
                            rr = tt % 2
                            if ti >= 2:
                                e.wait_ge(S["s_fge"], ti - 1)
                            for hc in range(24):
                                lt = h_sb[:, hc * 1024 + tt * 128:hc * 1024 + (tt + 1) * 128]
                                e.matmul(out=ps_dn[:, rr * 1024:rr * 1024 + 512], lhsT=lt,
                                         rhs=wdt_sb[:, hc * 1024:hc * 1024 + 512],
                                         start=(hc == 0), stop=(hc == 23))
                                mm = e.matmul(out=ps_dn[:, rr * 1024 + 512:(rr + 1) * 1024],
                                              lhsT=lt,
                                              rhs=wdt_sb[:, hc * 1024 + 512:(hc + 1) * 1024],
                                              start=(hc == 0), stop=(hc == 23))
                                if hc == 23:
                                    mm.then_inc(S["s_fdm"], 1)
                    fence(e)

                @blk.scalar
                def _(e):
                    for sl in range(2):
                        for hc in range(24):
                            g_i = sl * 24 + hc
                            e.wait_ge(S["s_fmm"], 2 * g_i + 1)
                            if g_i >= 1:
                                e.wait_ge(S["s_fev"], g_i)
                            e.activation(out=sg_scr[:], in_=ps_g[:], func=AF.Silu,
                                         scale=DEQ).then_inc(S["s_fg"], 1)
                        for tt in range(8):
                            ti = sl * 8 + tt
                            rr = tt % 2
                            e.wait_ge(S["s_fdm"], ti + 1)
                            if ti >= 2:
                                e.wait_ge(S["s_fgw"][ti % 2], 16 * (ti // 2))
                            e.activation(out=g_tile[:, rr * 1024:(rr + 1) * 1024],
                                         in_=ps_dn[:, rr * 1024:(rr + 1) * 1024],
                                         func=AF.Silu, scale=DEQ).then_inc(S["s_fge"], 1)
                        fence(e) if sl == 1 else None

                @blk.vector
                def _(e):
                    for sl in range(2):
                        if sl > 0:
                            e.wait_ge(S["s_fdm"], 8 * sl)
                        for hc in range(24):
                            g_i = sl * 24 + hc
                            e.wait_ge(S["s_fg"], g_i + 1)
                            e.wait_ge(S["s_fmm"], 2 * g_i + 2)
                            e.scalar_tensor_tensor(out=h_sb[:, hc * 1024:(hc + 1) * 1024],
                                                   in0=ps_u2[:], scalar=DEQ,
                                                   in1=sg_scr[:], op0=ALU.mult,
                                                   op1=ALU.mult).then_inc(S["s_fev"], 1)
                    fence(e)

                @blk.gpsimd
                def _(e):
                    for ti in range(16):
                        rr = ti % 2
                        e.wait_ge(S["s_fge"], ti + 1)
                        e.dma_start(out=g_dram[ti * 128:(ti + 1) * 128, :],
                                    in_=g_tile[:, rr * 1024:(rr + 1) * 1024]
                                    ).then_inc(S["s_fgw"][ti % 2], 16)
                    e.wait_ge(S["s_fgw"][0], 16 * 8)
                    e.wait_ge(S["s_fgw"][1], 16 * 8)
                    fence(e)
            FN[0] += 1

    if "B" in PHASES:
        with contextlib.ExitStack() as sB:
            def b_(name, shape, dt):
                return sB.enter_context(nc.sbuf_tensor(name, shape, dt))
            ktq_sb = b_("ktq_sb", [128, 2 * 2 * NH * 512], BF16)
            ktok_sb = b_("ktok_sb", [128, 2 * NH * DS], BF16)
            xc_sb = b_("xc_sb", [128, 2 * 1024], BF16)
            zc_sb = b_("zc_sb", [128, 2 * 1024], BF16)
            xw_sb = b_("xw_sb", [128, 1024], BF16)
            siluz_sb = b_("siluz_sb", [128, 1024], BF16)
            sw_sb = b_("sw_sb", [128, 16 * 128], BF16)
            t1_sb = b_("t1_sb", [128, 1024], F32)
            dxp_sb = b_("dxp_sb", [128, 1024], F32)
            t2_sb = b_("t2_sb", [128, 1024], F32)
            yg_tile = b_("yg_tile", [128, 2 * 1024], BF16)
            ps_S = [sB.enter_context(nc.psum_tensor(f"ps_S{i}", [128, 512], F32))
                    for i in range(2)]
            ps_y = sB.enter_context(nc.psum_tensor("ps_y", [128, 1024], F32))
            ps_w = sB.enter_context(nc.psum_tensor("ps_w", [128, 1024], F32))

            def kt_slice(sr, kq, h, col, width=128):
                base = ((sr * 2 + kq) * NH + h) * 512
                return ktq_sb[:, base + col:base + col + width]

            with nc.Block("B") as blk:
                @blk.sync
                def _(e):
                    for h in range(NH):
                        for kq, dram in ((0, k_dram), (1, q_dram)):
                            e.dma_start(out=kt_slice(0, kq, h, 0, 512),
                                        in_=dram[0:512, h * DS:(h + 1) * DS],
                                        transpose=True).then_inc(S["s_kq"][0], 16)
                    for s in range(1, 8):
                        e.wait_ge(S["s_wmm"], 4 * (s - 1))
                        for h in range(NH):
                            for kq, dram in ((0, k_dram), (1, q_dram)):
                                e.dma_start(out=kt_slice(s % 2, kq, h, 0, 512),
                                            in_=dram[s * 512:(s + 1) * 512,
                                                     h * DS:(h + 1) * DS],
                                            transpose=True).then_inc(S["s_kq"][s % 2], 16)
                    e.wait_ge(S["s_kq"][0], 512 * 4)
                    e.wait_ge(S["s_kq"][1], 512 * 4)
                    fence(e)

                @blk.scalar
                def _(e):
                    for c in range(NT):
                        rc = c % 2
                        if c >= 2:
                            e.wait_ge(S["s_wmm"], c - 1)
                            e.wait_ge(S["s_dxp"], c - 1)
                            e.wait_ge(S["s_bz"], c - 1)
                        e.dma_start(out=ktok_sb[:, rc * 2048:(rc + 1) * 2048],
                                    in_=k_dram[c * 128:(c + 1) * 128, :]).then_inc(S["s_bx"][c % 2], 16)
                        e.dma_start(out=xc_sb[:, rc * 1024:(rc + 1) * 1024],
                                    in_=x_dram[c * 128:(c + 1) * 128, :]).then_inc(S["s_bx"][c % 2], 16)
                        e.dma_start(out=zc_sb[:, rc * 1024:(rc + 1) * 1024],
                                    in_=z_dram[c * 128:(c + 1) * 128, :]).then_inc(S["s_bx"][c % 2], 16)
                        e.wait_ge(S["s_bx"][c % 2], 48 * (c // 2 + 1))
                        if c >= 1:
                            e.wait_ge(S["s_bev"], c)
                        e.activation(out=siluz_sb[:], in_=zc_sb[:, rc * 1024:(rc + 1) * 1024],
                                     func=AF.Silu).then_inc(S["s_bz"], 1)
                    e.wait_ge(S["s_bx"][0], 48 * 16)
                    e.wait_ge(S["s_bx"][1], 48 * 16)
                    fence(e)

                @blk.tensor
                def _(e):
                    for c in range(NT):
                        s = c // 4
                        sr = s % 2
                        rc = c % 2
                        wc = (c % 4) * 128
                        e.wait_ge(S["s_kq"][s % 2], 512 * (s // 2 + 1))
                        e.wait_ge(S["s_bx"][c % 2], 48 * (c // 2 + 1))
                        for g4 in range(4):
                            gi = 4 * c + g4
                            if gi >= 2:
                                e.wait_ge(S["s_sw"], gi - 1)
                            for hh in range(4):
                                h = 4 * g4 + hh
                                mm = e.matmul(out=ps_S[g4 % 2][:, hh * 128:(hh + 1) * 128],
                                              lhsT=kt_slice(sr, 0, h, wc),
                                              rhs=kt_slice(sr, 1, h, wc),
                                              start=True, stop=True)
                                if hh == 3:
                                    mm.then_inc(S["s_smm"], 1)
                        if c >= 1:
                            e.wait_ge(S["s_bev1"], c)
                        e.wait_ge(S["s_hup"], c + 1)
                        for h in range(NH):
                            e.wait_ge(S["s_sw"], 4 * c + h // 4 + 1)
                            e.matmul(out=ps_y[:, h * 64:(h + 1) * 64],
                                     lhsT=sw_sb[:, h * 128:(h + 1) * 128],
                                     rhs=xc_sb[:, rc * 1024 + h * 64:rc * 1024 + (h + 1) * 64],
                                     start=True, stop=False)
                            mm = e.matmul(out=ps_y[:, h * 64:(h + 1) * 64],
                                          lhsT=kt_slice(sr, 1, h, wc),
                                          rhs=Hbf_sb[:, h * 64:(h + 1) * 64],
                                          start=False, stop=True)
                            if h == NH - 1:
                                mm.then_inc(S["s_ymm"], 1)
                        e.wait_ge(S["s_xwg"], c + 1)
                        for h in range(NH):
                            mm = e.matmul(out=ps_w[:, h * 64:(h + 1) * 64],
                                          lhsT=ktok_sb[:, rc * 2048 + h * DS:rc * 2048 + (h + 1) * DS],
                                          rhs=xw_sb[:, h * 64:(h + 1) * 64],
                                          start=True, stop=True)
                            if h == NH - 1:
                                mm.then_inc(S["s_wmm"], 1)
                    fence(e)

                @blk.vector
                def _(e):
                    e.sem_inc(S["s_hup"], 1)
                    for c in range(NT):
                        rc = c % 2
                        for g4 in range(4):
                            gi = 4 * c + g4
                            e.wait_ge(S["s_smm"], gi + 1)
                            for hh in range(4):
                                h = 4 * g4 + hh
                                stt = e.scalar_tensor_tensor(
                                    out=sw_sb[:, h * 128:(h + 1) * 128],
                                    in0=ps_S[g4 % 2][:, hh * 128:(hh + 1) * 128],
                                    scalar=gcx_sb[:, c * NH + h:c * NH + h + 1],
                                    in1=strict_sb[:], op0=ALU.mult, op1=ALU.mult)
                                if hh == 3:
                                    stt.then_inc(S["s_sw"], 1)
                        e.wait_ge(S["s_ymm"], c + 1)
                        if c >= 1:
                            e.wait_ge(S["s_t2"], c)
                        e.tensor_tensor(
                            out=t1_sb[:].rearrange("p (h q) -> p h q", q=64),
                            in0=ps_y[:].rearrange("p (h q) -> p h q", q=64),
                            in1=expell_sb[:, c * NH:(c + 1) * NH].unsqueeze(2
                                ).broadcast_to([128, NH, 64]),
                            op=ALU.mult).then_inc(S["s_bev1"], 1)
                        e.wait_ge(S["s_t2"], c + 1)
                        e.wait_ge(S["s_bz"], c + 1)
                        if c >= 2:
                            e.wait_ge(S["s_ygw"][c % 2], 16 * (c // 2))
                        e.tensor_tensor(out=yg_tile[:, rc * 1024:(rc + 1) * 1024],
                                        in0=t2_sb[:], in1=siluz_sb[:],
                                        op=ALU.mult).then_inc(S["s_bev"], 1)
                        e.wait_ge(S["s_wmm"], c + 1)
                        e.wait_ge(S["s_hsg"], c + 1)
                        vch(e.tensor_tensor(out=H_sb[:], in0=Hs_sb[:], in1=ps_w[:],
                                                op=ALU.add))
                        vsy(e)
                        e.tensor_copy(out=Hbf_sb[:], in_=H_sb[:]).then_inc(S["s_hup"], 1)
                    fence(e)

                @blk.gpsimd
                def _(e):
                    for c in range(NT):
                        rc = c % 2
                        e.wait_ge(S["s_bx"][c % 2], 48 * (c // 2 + 1))
                        if c >= 1:
                            e.wait_ge(S["s_wmm"], c)
                        xc3 = xc_sb[:, rc * 1024:(rc + 1) * 1024].rearrange(
                            "p (h q) -> p h q", q=64)
                        e.tensor_tensor(
                            out=xw_sb[:].rearrange("p (h q) -> p h q", q=64),
                            in0=xc3,
                            in1=wsg_sb[:, c * NH:(c + 1) * NH].unsqueeze(2
                                ).broadcast_to([128, NH, 64]),
                            op=ALU.mult).then_inc(S["s_xwg"], 1)
                        gch(e.tensor_tensor(
                            out=dxp_sb[:].rearrange("p (h q) -> p h q", q=64),
                            in0=xc3,
                            in1=coef_sb[:, c * NH:(c + 1) * NH].unsqueeze(2
                                ).broadcast_to([128, NH, 64]),
                            op=ALU.mult))
                        ginc(e, S["s_dxp"], 1)
                        e.wait_ge(S["s_hup"], c + 1)
                        e.tensor_tensor(
                            out=Hs_sb[:].rearrange("p (h q) -> p h q", q=64),
                            in0=H_sb[:].rearrange("p (h q) -> p h q", q=64),
                            in1=cdb_sb[:, c * NH:(c + 1) * NH].unsqueeze(2
                                ).broadcast_to([128, NH, 64]),
                            op=ALU.mult).then_inc(S["s_hsg"], 1)
                        e.wait_ge(S["s_bev1"], c + 1)
                        gsy(e)
                        e.tensor_tensor(out=t2_sb[:], in0=t1_sb[:], in1=dxp_sb[:],
                                        op=ALU.add).then_inc(S["s_t2"], 1)
                        e.wait_ge(S["s_bev"], c + 1)
                        e.dma_start(out=yg_dram[c * 128:(c + 1) * 128, :],
                                    in_=yg_tile[:, rc * 1024:(rc + 1) * 1024]
                                    ).then_inc(S["s_ygw"][c % 2], 16)
                    e.wait_ge(S["s_ygw"][0], 16 * 16)
                    e.wait_ge(S["s_ygw"][1], 16 * 16)
                    fence(e)
            FN[0] += 1

    if "C" in PHASES:
        with contextlib.ExitStack() as sC:
            def c_(name, shape, dt):
                return sC.enter_context(nc.sbuf_tensor(name, shape, dt))
            ygT_sb = c_("ygT_sb", [128, 8 * 4096], BF16)
            wout_sb = c_("wout_sb", [128, 8 * 1024], BF16)
            s_ring = c_("s_ring", [128, 3 * 1024], F32)
            so_ring = c_("so_ring", [128, 2 * 1024], F32)
            sp_ring = c_("sp_ring", [128, 2 * 1024], F32)
            uh_ring = c_("uh_ring", [128, 2 * 1024], F32)
            gh_ring = c_("gh_ring", [128, 2 * 1024], BF16)
            stot_sb = c_("stot_sb", [128, 1024], F32)
            out_ring = c_("out_ring", [128, 2 * 1024], F32)
            ps_s = sC.enter_context(nc.psum_tensor("ps_s", [128, 2 * 1024], F32))

            with nc.Block("C") as blk:
                @blk.sync
                def _(e):
                    for pc in range(8):
                        e.dma_start(out=ygT_sb[:, pc * 4096:(pc + 1) * 4096],
                                    in_=yg_dram[:, pc * 128:(pc + 1) * 128],
                                    transpose=True).then_inc(S["s_yt"], 16)
                    for ft in range(NTH):
                        rf = ft % 2
                        if ft >= 2:
                            e.wait_ge(S["s_fin"], ft - 1)
                        e.dma_start(out=uh_ring[:, rf * 1024:(rf + 1) * 1024],
                                    in_=u_half[ft * 128:(ft + 1) * 128, :]
                                    ).then_inc(S["s_uh"][ft % 2], 16)
                        e.dma_start(out=gh_ring[:, rf * 1024:(rf + 1) * 1024],
                                    in_=g_dram[ft * 128:(ft + 1) * 128, :]
                                    ).then_inc(S["s_gh"][ft % 2], 16)
                        e.wait_ge(S["s_fin"], ft + 1)
                        e.dma_start(out=y_out[ft * 128:(ft + 1) * 128, :],
                                    in_=out_ring[:, rf * 1024:(rf + 1) * 1024]
                                    ).then_inc(S["s_out"][ft % 2], 16)
                    e.wait_ge(S["s_out"][0], 16 * (NTH // 2))
                    e.wait_ge(S["s_out"][1], 16 * (NTH // 2))
                    fence(e)

                @blk.gpsimd
                def _(e):
                    for pc in range(8):
                        e.dma_start(out=wout_sb[:, pc * 1024:(pc + 1) * 1024],
                                    in_=w_out[pc * 128:(pc + 1) * 128, :]
                                    ).then_inc(S["s_wo"], 16)
                    for tt in range(NT):
                        rt = tt % 3
                        e.wait_ge(S["s_se"], tt + 1)
                        e.indirect_dma_start(
                            out=xchg,
                            out_offset=IndirectOffsetOnAxis(ap=sidx_sb[:, tt:tt + 1], axis=0),
                            in_=s_ring[:, rt * 1024:(rt + 1) * 1024],
                            in_offset=None).then_inc(S["s_sc"][tt % 2], 16)
                    e.wait_ge(S["s_sc"][0], 16 * (NT // 2))
                    e.wait_ge(S["s_sc"][1], 16 * (NT // 2))
                    e.indirect_dma_start(
                        out=flags,
                        out_offset=IndirectOffsetOnAxis(ap=fidx_sb[:], axis=0),
                        in_=nonce_t_sb[:], in_offset=None).then_inc(S["s_fl"], 16)
                    e.wait_ge(S["s_fl"], 16)
                    nonce_r = e.alloc_register("nonce_r")
                    e.reg_load(nonce_r, nonce_sb[0:1, 0:1])
                    tmp_r = e.alloc_register("tmp_r")
                    nm_r = e.alloc_register("nm_r")
                    cnt_r = e.alloc_register("cnt_r")
                    cond_r = e.alloc_register("cond_r")
                    c1_r = e.alloc_register("c1_r")
                    c2_r = e.alloc_register("c2_r")
                    e.reg_mov(cnt_r, 0)
                    e.reg_mov(cond_r, 1)
                    flags_s = flags.rearrange("(k f) c -> k f c", f=128)[:, 0, :]
                    with e.While(cond_r):
                        e.dma_start(out=flagrd_sb[:], in_=flags_s).then_inc(S["s_fl"], 16)
                        e.reg_mov(nm_r, 0)
                        for kk in range(2):
                            e.reg_load(tmp_r, flagrd_sb[kk:kk + 1, 0:1])
                            e.reg_alu(tmp_r, tmp_r, nonce_r, ALU.is_equal)
                            e.reg_alu(nm_r, nm_r, tmp_r, ALU.add)
                        e.reg_add(cnt_r, cnt_r, 1)
                        e.reg_alu(c1_r, nm_r, 2, ALU.is_lt)
                        e.reg_alu(c2_r, cnt_r, POLL_CAP, ALU.is_lt)
                        e.reg_alu(cond_r, c1_r, c2_r, ALU.logical_and)
                    e.sem_inc(S["s_bar"], 1)
                    for ft in range(NTH):
                        rf = ft % 2
                        if ft >= 2:
                            e.wait_ge(S["s_fin"], ft - 1)
                        e.indirect_dma_start(
                            out=so_ring[:, rf * 1024:(rf + 1) * 1024], out_offset=None,
                            in_=xchg,
                            in_offset=IndirectOffsetOnAxis(ap=goidx_sb[:, ft:ft + 1], axis=0)
                            ).then_inc(S["s_gat"][ft % 2], 16)
                        e.indirect_dma_start(
                            out=sp_ring[:, rf * 1024:(rf + 1) * 1024], out_offset=None,
                            in_=xchg,
                            in_offset=IndirectOffsetOnAxis(ap=gpidx_sb[:, ft:ft + 1], axis=0)
                            ).then_inc(S["s_gat"][ft % 2], 16)
                    e.wait_ge(S["s_out"][0], 16 * (NTH // 2))
                    e.wait_ge(S["s_out"][1], 16 * (NTH // 2))
                    fence(e)

                @blk.tensor
                def _(e):
                    e.wait_ge(S["s_yt"], 16 * 8)
                    e.wait_ge(S["s_wo"], 16 * 8)
                    for tt in range(NT):
                        if tt >= 2:
                            e.wait_ge(S["s_se"], tt - 1)
                        ob = (tt % 2) * 1024
                        for pc in range(8):
                            lt = ygT_sb[:, pc * 4096 + tt * 128:pc * 4096 + (tt + 1) * 128]
                            e.matmul(out=ps_s[:, ob:ob + 512], lhsT=lt,
                                     rhs=wout_sb[:, pc * 1024:pc * 1024 + 512],
                                     start=(pc == 0), stop=(pc == 7))
                            mm = e.matmul(out=ps_s[:, ob + 512:ob + 1024], lhsT=lt,
                                          rhs=wout_sb[:, pc * 1024 + 512:(pc + 1) * 1024],
                                          start=(pc == 0), stop=(pc == 7))
                            if pc == 7:
                                mm.then_inc(S["s_om"], 1)
                    e.wait_ge(S["s_out"][0], 16 * (NTH // 2))
                    e.wait_ge(S["s_out"][1], 16 * (NTH // 2))
                    fence(e)

                @blk.vector
                def _(e):
                    for tt in range(NT):
                        rt = tt % 3
                        e.wait_ge(S["s_om"], tt + 1)
                        if tt >= 3:
                            e.wait_ge(S["s_sc"][(tt - 3) % 2], 16 * ((tt - 3) // 2 + 1))
                        e.tensor_copy(out=s_ring[:, rt * 1024:(rt + 1) * 1024],
                                      in_=ps_s[:, (tt % 2) * 1024:(tt % 2 + 1) * 1024]
                                      ).then_inc(S["s_se"], 1)
                    for ft in range(NTH):
                        rf = ft % 2
                        e.wait_ge(S["s_gat"][ft % 2], 32 * (ft // 2 + 1))
                        e.wait_ge(S["s_uh"][ft % 2], 16 * (ft // 2 + 1))
                        e.wait_ge(S["s_gh"][ft % 2], 16 * (ft // 2 + 1))
                        if ft >= 2:
                            e.wait_ge(S["s_out"][ft % 2], 16 * (ft // 2))
                        vch(e.tensor_tensor(out=stot_sb[:],
                                                in0=so_ring[:, rf * 1024:(rf + 1) * 1024],
                                                in1=sp_ring[:, rf * 1024:(rf + 1) * 1024],
                                                op=ALU.add))
                        vsy(e)
                        vch(e.tensor_tensor(out=stot_sb[:], in0=stot_sb[:],
                                            in1=gh_ring[:, rf * 1024:(rf + 1) * 1024],
                                            op=ALU.mult))
                        vsy(e)
                        vch(e.tensor_tensor(out=out_ring[:, rf * 1024:(rf + 1) * 1024],
                                            in0=uh_ring[:, rf * 1024:(rf + 1) * 1024],
                                            in1=stot_sb[:], op=ALU.add))
                        vinc(e, S["s_fin"], 1)
                    e.wait_ge(S["s_out"][0], 16 * (NTH // 2))
                    e.wait_ge(S["s_out"][1], 16 * (NTH // 2))
                    fence(e)

                @blk.scalar
                def _(e):
                    e.wait_ge(S["s_out"][0], 16 * (NTH // 2))
                    e.wait_ge(S["s_out"][1], 16 * (NTH // 2))
                    fence(e)
        FN[0] += 1
    else:
        with nc.Block("stub") as blk:
            @blk.sync
            def _(e):
                e.dma_start(out=y_out[:], in_=u_half[:]).then_inc(S["s_out"][0], 16)
                e.wait_ge(S["s_out"][0], 16)
                fence(e)
            for en in ("gpsimd", "scalar", "vector", "tensor"):
                getattr(blk, en)(fence)
        FN[0] += 1

    st.close()
    return nc



_NC_CACHE = [None]


def kernel(**inputs):
    u = np.asarray(inputs["L0_attn_u"])
    W_in = np.asarray(inputs["L0_attn_W_in"])
    W_out = np.asarray(inputs["L0_attn_W_out"])
    dt_bias = np.asarray(inputs["L0_attn_dt_bias"])
    B_bias = np.asarray(inputs["L0_attn_B_bias"])
    C_bias = np.asarray(inputs["L0_attn_C_bias"])
    B_norm_w = np.asarray(inputs["L0_attn_B_norm_w"])
    C_norm_w = np.asarray(inputs["L0_attn_C_norm_w"])
    Dv = np.asarray(inputs["L0_attn_D"])
    w_gate = np.asarray(inputs["L0_ffn_w_gate"])
    w_up = np.asarray(inputs["L0_ffn_w_up"])
    w_down = np.asarray(inputs["L0_ffn_w_down"])

    if _NC_CACHE[0] is None:
        _NC_CACHE[0] = build_nc()
    nc = _NC_CACHE[0]

    f32 = np.float32
    consts = dict(
        ident_in=np.eye(128, dtype=ml_dtypes.bfloat16),
        cum_in=np.triu(np.ones((128, 128), f32)),
        su_in=np.tril(np.ones((128, 128), f32), -1),
        strict_in=np.triu(np.ones((128, 128)), 1).astype(ml_dtypes.bfloat16),
        sd_in=np.eye(128, k=-1, dtype=f32),
        oneh_in=np.eye(1, 128, k=127, dtype=f32),
        onesc_in=np.ones((128, 1), f32),
        aones_in=np.ones((1, 128), f32),
    )
    nonce_val = int(np.random.randint(1, 2 ** 31 - 1))
    consts["nonce_in"] = np.array([[nonce_val]], np.int32)
    consts["noncet_in"] = np.full((128, 4), nonce_val, np.int32)

    in_maps = []
    for k in range(8):
        b, g = k // 2, k % 2
        me = k & 1
        zc = W_in[:, g * 1024:(g + 1) * 1024]
        xc = W_in[:, 2048 + g * 1024:2048 + (g + 1) * 1024]
        Bc = W_in[:, 4096:4224]
        Cc = W_in[:, 4224:4352]
        dtc = W_in[:, 4352 + g * NH:4352 + (g + 1) * NH]
        Ac = W_in[:, 4384 + g * NH:4384 + (g + 1) * NH]
        trc = W_in[:, 4416 + g * NH:4416 + (g + 1) * NH]
        angc = W_in[:, 4448:4480]
        w_in_k = np.ascontiguousarray(
            np.concatenate([xc, zc, Bc, Cc, dtc, Ac, trc, angc], axis=1))
        bb = np.ascontiguousarray(B_bias[g * NH:(g + 1) * NH])
        cb = np.ascontiguousarray(C_bias[g * NH:(g + 1) * NH])
        scat = np.zeros((128, NT), np.uint32)
        gown = np.zeros((128, NTH), np.uint32)
        gpar = np.zeros((128, NTH), np.uint32)
        p_ = np.arange(128, dtype=np.uint32)
        for tt in range(NT):
            scat[:, tt] = me * T + tt * 128 + p_
        for ft in range(NTH):
            gown[:, ft] = me * T + g * TH + ft * 128 + p_
            gpar[:, ft] = (1 - me) * T + g * TH + ft * 128 + p_
        im = dict(
            u_full=np.ascontiguousarray(u[b]),
            u_half=np.ascontiguousarray(u[b, g * TH:(g + 1) * TH]),
            w_in=w_in_k,
            w_out=np.ascontiguousarray(W_out[g * 1024:(g + 1) * 1024]),
            wg_i=w_gate, wu_i=w_up, wd_i=w_down,
            bb_bc=np.ascontiguousarray(
                np.broadcast_to(bb.reshape(1, NH * DS), (128, NH * DS))),
            cb_bc=np.ascontiguousarray(
                np.broadcast_to(cb.reshape(1, NH * DS), (128, NH * DS))),
            bbt_in=np.ascontiguousarray(bb.T),
            cbt_in=np.ascontiguousarray(cb.T),
            bnw_bc=np.ascontiguousarray(np.broadcast_to(B_norm_w.reshape(1, DS), (128, DS))),
            cnw_bc=np.ascontiguousarray(np.broadcast_to(C_norm_w.reshape(1, DS), (128, DS))),
            dtb_bc=np.ascontiguousarray(
                np.broadcast_to(dt_bias[g * NH:(g + 1) * NH].reshape(1, NH), (128, NH))),
            d_bc=np.ascontiguousarray(
                np.broadcast_to(Dv[g * NH:(g + 1) * NH].reshape(1, NH), (128, NH))),
            scat_in=scat, gown_in=gown, gpar_in=gpar,
            fscat_in=(me * 128 + p_).reshape(128, 1).astype(np.uint32),
            **consts,
        )
        in_maps.append(im)

    res = run_bass_kernel_spmd(nc, in_maps, core_ids=list(range(8)))
    kernel._last_results = res.results
    out = np.zeros((B, T, D), np.float32)
    for k in range(8):
        b, g = k // 2, k % 2
        out[b, g * TH:(g + 1) * TH] = res.results[k]["y_out"]
    return out


# version 15
# speedup vs baseline: 1.5937x; 1.3359x over previous
import contextlib
import math
import numpy as np
import ml_dtypes
import concourse.bass as bass
import concourse.mybir as mybir
from concourse.bass import IndirectOffsetOnAxis
from concourse.bass_utils import run_bass_kernel_spmd

F32 = mybir.dt.float32
BF16 = mybir.dt.bfloat16
I32 = mybir.dt.int32
U32 = mybir.dt.uint32
AF = mybir.ActivationFunctionType
ALU = mybir.AluOpType

B, T, D = 4, 4096, 1024
NH, DS, HD = 16, 128, 64
NT, TH, NTH = 32, 2048, 16
HID = 3072
CIN = 2384
EPS = 1e-5
TWO_PI = 2.0 * math.pi
MAGIC = 12582912.0
HALF_PI = 0.5 * math.pi
DEQ = 1.0 / 127.0

DEBUG = False
PHASES = ("AI", "AII", "AV", "B", "C")
POLL_CAP = 3000000

CBLK = [(0, 512), (512, 1024), (1024, 1536), (1536, 2048), (2048, 2384)]


def build_nc():
    nc = bass.Bass("TRN2", target_bir_lowering=False)
    dbgk = "ExternalOutput" if DEBUG else "Internal"

    def inp(name, shape, dt):
        return nc.dram_tensor(name, shape, dt, kind="ExternalInput").ap()

    u_full = inp("u_full", [T, D], F32)
    u_half = inp("u_half", [TH, D], F32)
    w_in = inp("w_in", [D, CIN], F32)
    w_out = inp("w_out", [D, D], F32)
    wg_i = inp("wg_i", [HID, D], I32)
    wu_i = inp("wu_i", [HID, D], I32)
    wd_i = inp("wd_i", [D, HID], I32)
    bb_bc = inp("bb_bc", [128, NH * DS], F32)
    cb_bc = inp("cb_bc", [128, NH * DS], F32)
    bbt_in = inp("bbt_in", [DS, NH], F32)
    cbt_in = inp("cbt_in", [DS, NH], F32)
    bnw_bc = inp("bnw_bc", [128, DS], F32)
    cnw_bc = inp("cnw_bc", [128, DS], F32)
    dtb_bc = inp("dtb_bc", [128, NH], F32)
    d_bc = inp("d_bc", [128, NH], F32)
    ident_in = inp("ident_in", [128, 128], BF16)
    cum_in = inp("cum_in", [128, 128], F32)
    su_in = inp("su_in", [128, 128], F32)
    strict_in = inp("strict_in", [128, 128], BF16)
    sd_in = inp("sd_in", [128, 128], F32)
    oneh_in = inp("oneh_in", [1, 128], F32)
    onesc_in = inp("onesc_in", [128, 1], F32)
    aones_in = inp("aones_in", [1, 128], F32)
    scat_in = inp("scat_in", [128, NT], U32)
    gown_in = inp("gown_in", [128, NTH], U32)
    gpar_in = inp("gpar_in", [128, NTH], U32)
    fscat_in = inp("fscat_in", [128, 1], U32)
    nonce_in = inp("nonce_in", [1, 1], I32)
    noncet_in = inp("noncet_in", [128, 4], I32)

    def scr(name, shape, dt, dbg=False):
        return nc.dram_tensor(name, shape, dt, kind=(dbgk if dbg else "Internal")).ap()

    xn_dram = scr("xn_dram", [T, D], BF16, True)
    xnh_dram = scr("xnh_dram", [TH, D], BF16, True)
    x_dram = scr("x_dram", [T, NH * HD], BF16, True)
    z_dram = scr("z_dram", [T, NH * HD], BF16, True)
    k_dram = scr("k_dram", [T, NH * DS], BF16, True)
    q_dram = scr("q_dram", [T, NH * DS], BF16, True)
    yg_dram = scr("yg_dram", [T, NH * HD], BF16, True)
    g_dram = scr("g_dram", [TH, D], BF16, True)
    wg_bf = scr("wg_bf", [HID, D], BF16)
    wu_bf = scr("wu_bf", [HID, D], BF16)
    wd_bf = scr("wd_bf", [D, HID], BF16)
    wgt_dram = scr("wgt_dram", [24, 8, 128, 128], BF16)
    wdt_dram = scr("wdt_dram", [HID, 1024], BF16)
    wut_dram = scr("wut_dram", [24, 8, 128, 128], BF16)
    xchg = nc.dram_tensor("xchg", [2 * T, D], F32, addr_space="Shared", kind="Internal").ap()
    flags = nc.dram_tensor("flags", [2 * 128, 4], I32, addr_space="Shared", kind="Internal").ap()
    sdbg = scr("sdbg", [T, D], F32, True) if DEBUG else None
    y_out = nc.dram_tensor("y_out", [TH, D], F32, kind="ExternalOutput").ap()

    st = contextlib.ExitStack()

    def sb(name, shape, dt):
        return st.enter_context(nc.sbuf_tensor(name, shape, dt))

    ident_sb = sb("ident_sb", [128, 128], BF16)
    cum_sb = sb("cum_sb", [128, 128], F32)
    su_sb = sb("su_sb", [128, 128], F32)
    strict_sb = sb("strict_sb", [128, 128], BF16)
    sd_sb = sb("sd_sb", [128, 128], F32)
    oneh_sb = sb("oneh_sb", [1, 128], F32)
    onesc_sb = sb("onesc_sb", [128, 1], F32)
    aones_sb = sb("aones_sb", [1, 128], F32)
    aones_bf = sb("aones_bf", [1, 128], BF16)
    mhalf = sb("mhalf", [128, 1], F32)
    halfpi_sb = sb("halfpi_sb", [128, 1], F32)
    bnw_sb = sb("bnw_sb", [128, DS], F32)
    cnw_sb = sb("cnw_sb", [128, DS], F32)
    dtb_sb = sb("dtb_sb", [128, NH], F32)
    dbc_sb = sb("dbc_sb", [128, NH], F32)
    bbt_bf = sb("bbt_bf", [DS, NH], BF16)
    cbt_bf = sb("cbt_bf", [DS, NH], BF16)
    t4p_sb = sb("t4p_sb", [DS, NH], F32)
    t4row_f = sb("t4row_f", [1, NH], F32)
    t4row_bf = sb("t4row_bf", [1, NH], BF16)
    qk_sb = sb("qk_sb", [128, NT * NH], F32)
    coef_sb = sb("coef_sb", [128, NT * NH], F32)
    gcx_sb = sb("gcx_sb", [128, NT * NH], F32)
    wsg_sb = sb("wsg_sb", [128, NT * NH], F32)
    expell_sb = sb("expell_sb", [128, NT * NH], F32)
    cdb_sb = sb("cdb_sb", [128, NT * NH], F32)
    H_sb = sb("H_sb", [128, 1024], F32)
    Hbf_sb = sb("Hbf_sb", [128, 1024], BF16)
    Hs_sb = sb("Hs_sb", [128, 1024], F32)
    sidx_sb = sb("sidx_sb", [128, NT], U32)
    goidx_sb = sb("goidx_sb", [128, NTH], U32)
    gpidx_sb = sb("gpidx_sb", [128, NTH], U32)
    fidx_sb = sb("fidx_sb", [128, 1], U32)
    nonce_sb = sb("nonce_sb", [1, 1], I32)
    nonce_t_sb = sb("nonce_t_sb", [128, 4], I32)
    flagrd_sb = sb("flagrd_sb", [2, 4], I32)

    S = {}
    names = ["fence", "s_ld", "s_t4",
             "s_cast_wd", "s_cast_win",
             "s_prep_wd", "s_prep_wg", "s_prep_wu",
             "s_sq", "s_var", "s_rstd", "s_rstd2", "s_xn",
             "s_mm", "s_ev", "s_evx", "s_sp", "s_bnx", "s_bn", "s_tp", "s_bte",
             "s_qkm", "s_qke",
             "s_v2", "s_a2", "s_tm", "s_vb", "s_tg", "s_sg", "s_vg",
             "s_vp", "s_thm", "s_vc", "s_rrv", "s_sc3", "s_ropev", "s_ropeg",
             "s_fx", "s_fmm", "s_fev", "s_fg", "s_fge", "s_fdm",
             "s_smm", "s_sw", "s_ymm", "s_wmm",
             "s_bz", "s_xwg", "s_dxp", "s_t2", "s_hsg", "s_bev1", "s_bev",
             "s_hup",
             "s_yt", "s_om", "s_se", "s_fl", "s_bar",
             "s_fin", "s_wo"]
    quad_names = ["s_u", "s_xnw"]
    pair_names = ["s_xt", "s_xzw", "s_kqw", "s_kq", "s_bx",
                  "s_fw", "s_fgw", "s_ygw", "s_sc", "s_gat", "s_uh", "s_gh",
                  "s_out"]
    for n in names:
        S[n] = st.enter_context(nc.semaphore(n))
    for n in pair_names:
        S[n] = [st.enter_context(nc.semaphore(n + "0")),
                st.enter_context(nc.semaphore(n + "1"))]
    for n in quad_names:
        S[n] = [st.enter_context(nc.semaphore(n + str(q))) for q in range(4)]
    S["s_cast_wg"] = S["s_cast_wd"]
    S["s_cast_wu"] = S["s_cast_wd"]
    S["s_cast_misc"] = S["s_cast_wd"]
    S["s_cast_bbcb"] = S["s_cast_win"]

    for n in ["s_vself", "s_aself", "s_gself"]:
        S[n] = st.enter_context(nc.semaphore(n))
    VC = {"v": 0, "a": 0, "g": 0}

    def vch(instr):
        VC["v"] += 1
        instr.then_inc(S["s_vself"], 1)
        return instr

    def vsy(e):
        e.wait_ge(S["s_vself"], VC["v"])

    def ach(instr):
        VC["a"] += 1
        instr.then_inc(S["s_aself"], 1)
        return instr

    def asy(e):
        e.wait_ge(S["s_aself"], VC["a"])

    def gch(instr):
        VC["g"] += 1
        instr.then_inc(S["s_gself"], 1)
        return instr

    def gsy(e):
        e.wait_ge(S["s_gself"], VC["g"])

    def vinc(e, sem, k=1):
        vsy(e)
        e.sem_inc(sem, k)

    def ainc(e, sem, k=1):
        asy(e)
        e.sem_inc(sem, k)

    def ginc(e, sem, k=1):
        gsy(e)
        e.sem_inc(sem, k)

    FN = [0]

    def fence(e):
        e.sem_inc(S["fence"], 1)
        e.wait_ge(S["fence"], 5 * FN[0])

    with nc.psum_tensor("ps_t4", [1, NH], F32) as ps_t4:
        with nc.Block("startup") as blk:
            @blk.sync
            def _(e):
                for dst, src in [(bnw_sb, bnw_bc), (cnw_sb, cnw_bc), (dtb_sb, dtb_bc),
                                 (dbc_sb, d_bc), (ident_sb, ident_in), (cum_sb, cum_in),
                                 (su_sb, su_in), (strict_sb, strict_in), (sd_sb, sd_in),
                                 (oneh_sb, oneh_in), (onesc_sb, onesc_in),
                                 (aones_sb, aones_in), (sidx_sb, scat_in),
                                 (goidx_sb, gown_in), (gpidx_sb, gpar_in),
                                 (fidx_sb, fscat_in), (nonce_sb, nonce_in),
                                 (nonce_t_sb, noncet_in), (t4p_sb, bbt_in)]:
                    e.dma_start(out=dst[:], in_=src).then_inc(S["s_ld"], 16)
                fence(e)

            @blk.gpsimd
            def _(e):
                e.dma_start(out=bbt_bf[:], in_=bbt_in).then_inc(S["s_cast_misc"], 16)
                e.dma_start(out=cbt_bf[:], in_=cbt_in).then_inc(S["s_cast_misc"], 16)
                e.dma_start(out=aones_bf[:], in_=aones_in).then_inc(S["s_cast_misc"], 16)
                e.wait_ge(S["s_cast_misc"], 16 * 3)
                for (srcw, dstw, sw) in [(wd_i, wd_bf, "s_cast_wd"),
                                         (wg_i, wg_bf, "s_cast_wg"),
                                         (wu_i, wu_bf, "s_cast_wu")]:
                    nrow = srcw.shape[0]
                    q = nrow // 4
                    for j in range(4):
                        e.dma_start(out=dstw[j * q:(j + 1) * q, :],
                                    in_=srcw[j * q:(j + 1) * q, :]).then_inc(S[sw], 16)
                e.memset(mhalf[:], -0.5)
                e.memset(halfpi_sb[:], HALF_PI)
                e.memset(Hs_sb[:], 0.0)
                fence(e)

            @blk.vector
            def _(e):
                e.wait_ge(S["s_ld"], 16 * 19)
                e.wait_ge(S["s_cast_misc"], 16 * 3)
                e.scalar_tensor_tensor(out=t4p_sb[:], in0=t4p_sb[:], scalar=1.0,
                                       in1=cbt_bf[:], op0=ALU.mult,
                                       op1=ALU.mult).then_inc(S["s_t4"], 1)
                e.memset(H_sb[:], 0.0)
                e.memset(Hbf_sb[:], 0.0)
                e.wait_ge(S["s_t4"], 2)
                vch(e.tensor_copy(out=t4row_f[:], in_=ps_t4[:]))
                vsy(e)
                e.tensor_copy(out=t4row_bf[:], in_=t4row_f[:]).then_inc(S["s_t4"], 1)
                fence(e)

            @blk.tensor
            def _(e):
                e.wait_ge(S["s_t4"], 1)
                e.matmul(out=ps_t4[:], lhsT=onesc_sb[:], rhs=t4p_sb[:],
                         start=True, stop=True).then_inc(S["s_t4"], 1)
                fence(e)

            @blk.scalar
            def _(e):
                fence(e)
        FN[0] += 1

    with nc.sbuf_tensor("bn_sb", [128, NT * 256], BF16) as bn_sb, \
         nc.sbuf_tensor("smalls_sb", [128, NT * 80], F32) as smalls_sb:

      if "AI" in PHASES:
        with contextlib.ExitStack() as sAI:
            def a(name, shape, dt):
                return sAI.enter_context(nc.sbuf_tensor(name, shape, dt))

            win_sb = a("win_sb", [128, 8 * CIN], BF16)
            u_ring = a("u_ring", [128, 4 * D], F32)
            sq_scr = a("sq_scr", [128, D], F32)
            cp_scr = a("cp_scr", [128, D], F32)
            ssum_c = a("ssum_c", [128, 4], F32)
            ssq_c = a("ssq_c", [128, 4], F32)
            mean_c = a("mean_c", [128, 4], F32)
            m2_c = a("m2_c", [128, 4], F32)
            var_c = a("var_c", [128, 4], F32)
            rstd_c = a("rstd_c", [128, 4], F32)
            lnv_c = a("lnv_c", [128, 4], F32)
            lnv2_c = a("lnv2_c", [128, 2], F32)
            xn_ring = a("xn_ring", [128, 4 * D], BF16)
            xnt_ring = a("xnt_ring", [128, 2 * 8 * 512], BF16)
            x_ev = a("x_ev", [128, 2 * 1024], BF16)
            z_ev = a("z_ev", [128, 2 * 1024], BF16)
            bc_ev = a("bc_ev", [128, 256], F32)
            rstdbc = a("rstdbc", [128, 2], F32)
            tb_scr = a("tb_scr", [128, 256], F32)
            bt_sb = a("bt_sb", [128, 1024], BF16)
            junk_sb = a("junk_sb", [128, 128], F32)
            t1_col = a("t1_col", [128, 4], F32)
            ps_ip = [sAI.enter_context(nc.psum_tensor(f"ps_ip{i}", [128, 512], F32))
                     for i in range(4)]
            ps_ip.append(sAI.enter_context(nc.psum_tensor("ps_ip4", [128, 336], F32)))
            ps_bt = sAI.enter_context(nc.psum_tensor("ps_bt", [128, 1024], BF16))
            ps_qk = sAI.enter_context(nc.psum_tensor("ps_qk", [128, 4 * NH], F32))

            with nc.Block("AI") as blk:
                @blk.gpsimd
                def _(e):
                    for dc in range(8):
                        e.dma_start(out=win_sb[:, dc * CIN:(dc + 1) * CIN],
                                    in_=w_in[dc * 128:(dc + 1) * 128, :]
                                    ).then_inc(S["s_cast_win"], 16)
                    def xn_store(m):
                        e.wait_ge(S["s_xn"], m + 1)
                        dst = xn_dram if m < NT else xnh_dram
                        mm_ = m if m < NT else m - NT
                        e.dma_start(out=dst[mm_ * 128:(mm_ + 1) * 128, :],
                                    in_=xn_ring[:, (m % 4) * D:(m % 4 + 1) * D]
                                    ).then_inc(S["s_xnw"][m % 4], 16)

                    def xz_store(m):
                        e.wait_ge(S["s_evx"], m + 1)
                        e.dma_start(out=x_dram[m * 128:(m + 1) * 128, :],
                                    in_=x_ev[:, (m % 2) * 1024:(m % 2 + 1) * 1024]
                                    ).then_inc(S["s_xzw"][m % 2], 16)
                        e.wait_ge(S["s_ev"], m + 1)
                        e.dma_start(out=z_dram[m * 128:(m + 1) * 128, :],
                                    in_=z_ev[:, (m % 2) * 1024:(m % 2 + 1) * 1024]
                                    ).then_inc(S["s_xzw"][m % 2], 16)

                    for step in range(NT + 4):
                        if step >= 4:
                            xz_store(step - 4)
                        if step < NT:
                            xn_store(step)
                    for j in range(NTH):
                        xn_store(NT + j)
                    for q_ in range(4):
                        e.wait_ge(S["s_xnw"][q_], 16 * 12)
                    e.wait_ge(S["s_xzw"][0], 32 * 16)
                    e.wait_ge(S["s_xzw"][1], 32 * 16)
                    e.wait_ge(S["s_cast_win"], 16 * 8)
                    fence(e)

                @blk.sync
                def _(e):
                    for m in range(NT):
                        if m >= 4:
                            e.wait_ge(S["s_xn"], m - 3)
                        e.dma_start(out=u_ring[:, (m % 4) * D:(m % 4 + 1) * D],
                                    in_=u_full[m * 128:(m + 1) * 128, :]).then_inc(S["s_u"][m % 4], 16)
                        if m % 4 == 3:
                            s4 = m // 4
                            for q_ in range(4):
                                e.wait_ge(S["s_xnw"][q_], 16 * (s4 + 1))
                            if s4 >= 2:
                                e.wait_ge(S["s_mm"], 4 * (s4 - 1))
                            for dc in range(8):
                                base = ((s4 % 2) * 8 + dc) * 512
                                e.dma_start(
                                    out=xnt_ring[:, base:base + 512],
                                    in_=xn_dram[s4 * 512:(s4 + 1) * 512,
                                                dc * 128:(dc + 1) * 128],
                                    transpose=True).then_inc(S["s_xt"][s4 % 2], 16)
                    for j in range(NTH):
                        m = NT + j
                        e.wait_ge(S["s_xn"], m - 3)
                        e.dma_start(out=u_ring[:, (m % 4) * D:(m % 4 + 1) * D],
                                    in_=u_half[j * 128:(j + 1) * 128, :]).then_inc(S["s_u"][m % 4], 16)
                    for q_ in range(4):
                        e.wait_ge(S["s_u"][q_], 16 * 12)
                    e.wait_ge(S["s_xt"][0], 128 * 4)
                    e.wait_ge(S["s_xt"][1], 128 * 4)
                    fence(e)

                @blk.scalar
                def _(e):
                    def sq(m):
                        e.wait_ge(S["s_u"][m % 4], 16 * (m // 4 + 1))
                        if m >= 4:
                            e.wait_ge(S["s_var"], m - 3)
                        asy(e)
                        ach(e.activation(out=sq_scr[:],
                                         in_=u_ring[:, (m % 4) * D:(m % 4 + 1) * D],
                                         func=AF.Square,
                                         accum_out=ssq_c[:, m % 4:m % 4 + 1]))
                        ainc(e, S["s_sq"], 1)

                    def lnexp_stats(m):
                        r = m % 4
                        e.wait_ge(S["s_var"], m + 1)
                        asy(e)
                        ach(e.activation(out=lnv_c[:, r:r + 1],
                                         in_=var_c[:, r:r + 1], func=AF.Ln))
                        asy(e)
                        ach(e.activation(out=rstd_c[:, r:r + 1],
                                         in_=lnv_c[:, r:r + 1], func=AF.Exp,
                                         scale=-0.5))
                        ainc(e, S["s_rstd"], 1)

                    def act_bn(m):
                        e.wait_ge(S["s_mm"], m + 1)
                        if m >= 2:
                            e.wait_ge(S["s_xzw"][m % 2], 32 * (m // 2))
                        e.copy(out=x_ev[:, (m % 2) * 1024:(m % 2) * 1024 + 512],
                               in_=ps_ip[0][:])
                        asy(e)
                        ach(e.copy(out=x_ev[:, (m % 2) * 1024 + 512:(m % 2 + 1) * 1024],
                                   in_=ps_ip[1][:]))
                        ainc(e, S["s_evx"], 1)
                        e.copy(out=smalls_sb[:, m * 80:(m + 1) * 80],
                               in_=ps_ip[4][:, 256:336]).then_inc(S["s_sp"], 1)
                        e.wait_ge(S["s_bnx"], m + 1)
                        asy(e)
                        ach(e.activation(out=lnv2_c[:], in_=rstdbc[:, 0:2],
                                         func=AF.Ln))
                        asy(e)
                        ach(e.activation(out=rstdbc[:, 0:2], in_=lnv2_c[:],
                                         func=AF.Exp, scale=-0.5))
                        ainc(e, S["s_rstd2"], 1)

                    for step in range(NT + 4):
                        if 1 <= step <= NT:
                            lnexp_stats(step - 1)
                        if step < NT:
                            sq(step)
                        if step >= 4:
                            act_bn(step - 4)
                    for j in range(NTH):
                        m = NT + j
                        sq(m)
                        if j >= 1:
                            lnexp_stats(m - 1)
                    lnexp_stats(NT + NTH - 1)
                    fence(e)

                @blk.vector
                def _(e):
                    def stats_xn(m):
                        e.wait_ge(S["s_u"][m % 4], 16 * (m // 4 + 1))
                        if m >= 4:
                            e.wait_ge(S["s_xnw"][m % 4], 16 * (m // 4))
                        ur = u_ring[:, (m % 4) * D:(m % 4 + 1) * D]
                        r = m % 4
                        vsy(e)
                        vch(e.tensor_scalar(out=cp_scr[:], in0=ur, scalar1=1.0,
                                            scalar2=None, op0=ALU.mult, op1=ALU.add,
                                            accum_out=ssum_c[:, r:r + 1]))
                        e.wait_ge(S["s_sq"], m + 1)
                        vsy(e)
                        vch(e.tensor_scalar(out=mean_c[:, r:r + 1], in0=ssum_c[:, r:r + 1],
                                            scalar1=1.0 / D, scalar2=None, op0=ALU.mult))
                        vsy(e)
                        vch(e.tensor_tensor(out=m2_c[:, r:r + 1], in0=mean_c[:, r:r + 1],
                                            in1=mean_c[:, r:r + 1], op=ALU.mult))
                        vch(e.tensor_scalar(out=var_c[:, r:r + 1], in0=ssq_c[:, r:r + 1],
                                            scalar1=1.0 / D, scalar2=EPS,
                                            op0=ALU.mult, op1=ALU.add))
                        vsy(e)
                        e.tensor_tensor(out=var_c[:, r:r + 1], in0=var_c[:, r:r + 1],
                                        in1=m2_c[:, r:r + 1],
                                        op=ALU.subtract).then_inc(S["s_var"], 1)
                        e.wait_ge(S["s_rstd"], m + 1)
                        vch(e.tensor_scalar(out=xn_ring[:, r * D:(r + 1) * D],
                                            in0=ur, scalar1=mean_c[:, r:r + 1],
                                            scalar2=rstd_c[:, r:r + 1],
                                            op0=ALU.subtract, op1=ALU.mult))
                        vinc(e, S["s_xn"], 1)

                    def bn_item(m):
                        s4 = m // 4
                        jj = m % 4
                        r = m % 2
                        e.wait_ge(S["s_mm"], m + 1)
                        if m >= 2:
                            e.wait_ge(S["s_xzw"][m % 2], 32 * (m // 2))
                        e.tensor_copy(out=z_ev[:, r * 1024:r * 1024 + 512], in_=ps_ip[2][:])
                        e.tensor_copy(out=z_ev[:, r * 1024 + 512:r * 1024 + 1024], in_=ps_ip[3][:])
                        vch(e.tensor_copy(out=bc_ev[:], in_=ps_ip[4][:, 0:256]))
                        vinc(e, S["s_ev"], 1)
                        vsy(e)
                        vch(e.scalar_tensor_tensor(out=junk_sb[:], in0=bc_ev[:, 0:128],
                                                   scalar=1.0 / DS, in1=bc_ev[:, 0:128],
                                                   op0=ALU.mult, op1=ALU.mult,
                                                   accum_out=rstdbc[:, 0:1]))
                        vsy(e)
                        vch(e.scalar_tensor_tensor(out=junk_sb[:], in0=bc_ev[:, 128:256],
                                                   scalar=1.0 / DS, in1=bc_ev[:, 128:256],
                                                   op0=ALU.mult, op1=ALU.mult,
                                                   accum_out=rstdbc[:, 1:2]))
                        vsy(e)
                        e.tensor_scalar(out=rstdbc[:, 0:2], in0=rstdbc[:, 0:2],
                                        scalar1=EPS, scalar2=None,
                                        op0=ALU.add).then_inc(S["s_bnx"], 1)
                        e.wait_ge(S["s_rstd2"], m + 1)
                        vch(e.tensor_scalar(out=tb_scr[:, 0:128], in0=bc_ev[:, 0:128],
                                            scalar1=rstdbc[:, 0:1], scalar2=None,
                                            op0=ALU.mult))
                        vch(e.tensor_scalar(out=tb_scr[:, 128:256], in0=bc_ev[:, 128:256],
                                            scalar1=rstdbc[:, 1:2], scalar2=None,
                                            op0=ALU.mult))
                        vsy(e)
                        vch(e.tensor_tensor(out=bn_sb[:, m * 256:m * 256 + 128],
                                            in0=tb_scr[:, 0:128], in1=bnw_sb[:],
                                            op=ALU.mult))
                        vch(e.tensor_tensor(out=bn_sb[:, m * 256 + 128:(m + 1) * 256],
                                            in0=tb_scr[:, 128:256], in1=cnw_sb[:],
                                            op=ALU.mult))
                        vinc(e, S["s_bn"], 1)
                        vsy(e)
                        vch(e.scalar_tensor_tensor(out=junk_sb[:],
                                                   in0=bn_sb[:, m * 256:m * 256 + 128],
                                                   scalar=1.0,
                                                   in1=bn_sb[:, m * 256 + 128:(m + 1) * 256],
                                                   op0=ALU.mult, op1=ALU.mult,
                                                   accum_out=t1_col[:, jj:jj + 1]))
                        if jj == 3:
                            e.wait_ge(S["s_tp"], s4 + 1)
                            vch(e.tensor_copy(out=bt_sb[:], in_=ps_bt[:]))
                            vinc(e, S["s_bte"], 1)
                            e.wait_ge(S["s_qkm"], s4 + 1)
                            vsy(e)
                            for j2 in range(4):
                                m2 = 4 * s4 + j2
                                ts = e.tensor_scalar(out=qk_sb[:, m2 * NH:(m2 + 1) * NH],
                                                     in0=ps_qk[:, j2 * NH:(j2 + 1) * NH],
                                                     scalar1=t1_col[:, j2:j2 + 1],
                                                     scalar2=None, op0=ALU.add)
                                if j2 == 3:
                                    ts.then_inc(S["s_qke"], 1)

                    for step in range(NT + 4):
                        if step >= 4:
                            bn_item(step - 4)
                        if step < NT:
                            stats_xn(step)
                    for j in range(NTH):
                        stats_xn(NT + j)
                    fence(e)

                @blk.tensor
                def _(e):
                    e.wait_ge(S["s_cast_win"], 16 * 8)
                    e.wait_ge(S["s_cast_misc"], 16 * 3)
                    for m in range(NT):
                        s4 = m // 4
                        e.wait_ge(S["s_xt"][s4 % 2], 128 * (s4 // 2 + 1))
                        if m > 0:
                            e.wait_ge(S["s_ev"], m)
                            e.wait_ge(S["s_evx"], m)
                            e.wait_ge(S["s_sp"], m)
                        base = ((s4 % 2) * 8)
                        for dc in range(8):
                            lt = xnt_ring[:, (base + dc) * 512 + (m % 4) * 128:
                                          (base + dc) * 512 + (m % 4 + 1) * 128]
                            for ci, (c0, c1) in enumerate(CBLK):
                                mm = e.matmul(out=ps_ip[ci][:, 0:(c1 - c0)], lhsT=lt,
                                              rhs=win_sb[:, dc * CIN + c0:dc * CIN + c1],
                                              start=(dc == 0), stop=(dc == 7))
                                if dc == 7 and ci == 4:
                                    mm.then_inc(S["s_mm"], 1)
                        if m % 4 == 3:
                            e.wait_ge(S["s_bn"], m + 1)
                            if s4 > 0:
                                e.wait_ge(S["s_bte"], s4)
                            for jj in range(4):
                                mj = 4 * s4 + jj
                                e.transpose(out=ps_bt[:, jj * 256:jj * 256 + 128],
                                            in_=bn_sb[:, mj * 256:mj * 256 + 128],
                                            identity=ident_sb[:])
                                tp = e.transpose(out=ps_bt[:, jj * 256 + 128:(jj + 1) * 256],
                                                 in_=bn_sb[:, mj * 256 + 128:(mj + 1) * 256],
                                                 identity=ident_sb[:])
                                if jj == 3:
                                    tp.then_inc(S["s_tp"], 1)
                            e.wait_ge(S["s_bte"], s4 + 1)
                            if s4 > 0:
                                e.wait_ge(S["s_qke"], s4)
                            e.wait_ge(S["s_t4"], 3)
                            for jj in range(4):
                                e.matmul(out=ps_qk[:, jj * NH:(jj + 1) * NH],
                                         lhsT=bt_sb[:, jj * 256:jj * 256 + 128],
                                         rhs=cbt_bf[:], start=True, stop=False)
                                e.matmul(out=ps_qk[:, jj * NH:(jj + 1) * NH],
                                         lhsT=bt_sb[:, jj * 256 + 128:(jj + 1) * 256],
                                         rhs=bbt_bf[:], start=False, stop=False)
                                mmq = e.matmul(out=ps_qk[:, jj * NH:(jj + 1) * NH],
                                               lhsT=aones_bf[:], rhs=t4row_bf[:],
                                               start=False, stop=True)
                                if jj == 3:
                                    mmq.then_inc(S["s_qkm"], 1)
                    fence(e)
        FN[0] += 1

      if "AII" in PHASES:
        with contextlib.ExitStack() as sA2:
            def a(name, shape, dt):
                return sA2.enter_context(nc.sbuf_tensor(name, shape, dt))

            dt_sb = a("dt_sb", [128, NT * NH], F32)
            adt_sb = a("adt_sb", [128, NT * NH], F32)
            beta_sb = a("beta_sb", [128, NT * NH], F32)
            betan_sb = a("betan_sb", [128, NT * NH], F32)
            gc_sb = a("gc_sb", [128, NT * NH], F32)
            g1_sb = a("g1_sb", [128, NT * NH], F32)
            dtin_sb = a("dtin_sb", [128, NT * NH], F32)
            spa_sb = a("spa_sb", [128, NT * NH], F32)
            ax_scr = a("ax_scr", [128, NT * NH], F32)
            ex_scr = a("ex_scr", [128, NT * NH], F32)
            ln_scr = a("ln_scr", [128, NT * NH], F32)
            mx_scr = a("mx_scr", [128, NT * NH], F32)
            eneg_scr = a("eneg_scr", [128, NH], F32)
            dec_scr = a("dec_scr", [128, NH], F32)
            sm3 = smalls_sb[:].rearrange("p (m c) -> p m c", c=80)
            dt3 = dt_sb[:].rearrange("p (m h) -> p m h", h=NH)

            with contextlib.ExitStack() as sG:
                ps_bn = [sG.enter_context(nc.psum_tensor(f"ps_bn{i}", [128, NH], F32))
                         for i in range(2)]
                ps_e = [sG.enter_context(nc.psum_tensor(f"ps_e{i}", [128, NH], F32))
                        for i in range(2)]
                ps_d = [sG.enter_context(nc.psum_tensor(f"ps_d{i}", [128, NH], F32))
                        for i in range(2)]

                with nc.Block("AII_gates") as blk:
                    @blk.vector
                    def _(e):
                        e.wait_ge(S["s_sp"], NT)
                        e.wait_ge(S["s_qke"], NT // 4)
                        e.tensor_tensor(out=dtin_sb[:].rearrange("p (m h) -> p m h", h=NH),
                                        in0=sm3[:, :, 0:16],
                                        in1=dtb_sb[:].unsqueeze(1).broadcast_to([128, NT, NH]),
                                        op=ALU.add).then_inc(S["s_v2"], 1)
                        e.wait_ge(S["s_a2"], 3)
                        vch(e.tensor_scalar_max(out=mx_scr[:], in0=dtin_sb[:], scalar1=0.0))
                        vsy(e)
                        vch(e.tensor_tensor(out=dt_sb[:], in0=mx_scr[:], in1=ln_scr[:],
                                            op=ALU.add))
                        vinc(e, S["s_v2"], 1)
                        e.wait_ge(S["s_a2"], 6)
                        vsy(e)
                        vch(e.tensor_tensor(out=mx_scr[:].rearrange("p (m h) -> p m h", h=NH),
                                            in0=sm3[:, :, 16:32], in1=sm3[:, :, 16:32],
                                            op=ALU.max))
                        vsy(e)
                        vch(e.tensor_scalar_max(out=mx_scr[:], in0=mx_scr[:], scalar1=0.0))
                        vsy(e)
                        vch(e.tensor_tensor(out=spa_sb[:], in0=mx_scr[:], in1=ln_scr[:],
                                            op=ALU.add))
                        vsy(e)
                        vch(e.tensor_scalar_max(out=spa_sb[:], in0=spa_sb[:], scalar1=1e-4))
                        vsy(e)
                        vch(e.scalar_tensor_tensor(out=adt_sb[:], in0=spa_sb[:], scalar=-1.0,
                                                   in1=dt_sb[:], op0=ALU.mult,
                                                   op1=ALU.mult))
                        vinc(e, S["s_v2"], 1)
                        e.wait_ge(S["s_a2"], 7)
                        vch(e.tensor_scalar(out=ex_scr[:], in0=ex_scr[:], scalar1=1.0,
                                            scalar2=None, op0=ALU.add))
                        vsy(e)
                        vch(e.reciprocal(out=beta_sb[:], in_=ex_scr[:]))
                        vinc(e, S["s_v2"], 1)
                        for m in range(NT):
                            e.wait_ge(S["s_tm"], m + 1)
                            e.tensor_copy(out=betan_sb[:, m * NH:(m + 1) * NH],
                                          in_=ps_bn[m % 2][:]).then_inc(S["s_vb"], 1)
                        vsy(e)
                        e.wait_ge(S["s_vb"], NT)
                        vch(e.scalar_tensor_tensor(out=mx_scr[:], in0=beta_sb[:], scalar=1.0,
                                                   in1=betan_sb[:], op0=ALU.add,
                                                   op1=ALU.subtract))
                        vsy(e)
                        vch(e.tensor_tensor(out=gc_sb[:], in0=mx_scr[:], in1=dt_sb[:],
                                            op=ALU.mult))
                        vch(e.tensor_tensor(out=g1_sb[:], in0=beta_sb[:], in1=dt_sb[:],
                                            op=ALU.mult))
                        vsy(e)
                        vch(e.tensor_tensor(out=mx_scr[:], in0=g1_sb[:], in1=qk_sb[:],
                                            op=ALU.mult))
                        vsy(e)
                        vch(e.tensor_tensor(out=coef_sb[:].rearrange("p (m h) -> p m h", h=NH),
                                            in0=mx_scr[:].rearrange("p (m h) -> p m h", h=NH),
                                            in1=dbc_sb[:].unsqueeze(1).broadcast_to([128, NT, NH]),
                                            op=ALU.add))
                        vinc(e, S["s_v2"], 1)
                        vsy(e)
                        for m in range(NT):
                            e.wait_ge(S["s_sg"], m + 1)
                            e.tensor_tensor(out=gcx_sb[:, m * NH:(m + 1) * NH],
                                            in0=gc_sb[:, m * NH:(m + 1) * NH],
                                            in1=eneg_scr[:], op=ALU.mult)
                            e.tensor_tensor(out=wsg_sb[:, m * NH:(m + 1) * NH],
                                            in0=gc_sb[:, m * NH:(m + 1) * NH],
                                            in1=dec_scr[:], op=ALU.mult)
                            e.tensor_tensor(out=cdb_sb[:, m * NH:(m + 1) * NH],
                                            in0=dec_scr[:],
                                            in1=expell_sb[:, m * NH:(m + 1) * NH],
                                            op=ALU.mult).then_inc(S["s_vg"], 1)
                        fence(e)

                    @blk.scalar
                    def _(e):
                        e.wait_ge(S["s_v2"], 1)
                        ach(e.activation(out=ax_scr[:], in_=dtin_sb[:],
                                         func=AF.Abs))
                        ainc(e, S["s_a2"], 1)
                        asy(e)
                        ach(e.activation(out=ex_scr[:], in_=ax_scr[:], func=AF.Exp,
                                         scale=-1.0))
                        ainc(e, S["s_a2"], 1)
                        asy(e)
                        ach(e.activation(out=ln_scr[:], in_=ex_scr[:], func=AF.Ln,
                                         bias=1.0))
                        ainc(e, S["s_a2"], 1)
                        e.wait_ge(S["s_v2"], 2)
                        ach(e.activation(out=ax_scr[:].rearrange("p (m h) -> p m h", h=NH),
                                         in_=sm3[:, :, 16:32],
                                         func=AF.Abs))
                        ainc(e, S["s_a2"], 1)
                        asy(e)
                        ach(e.activation(out=ex_scr[:], in_=ax_scr[:], func=AF.Exp,
                                         scale=-1.0))
                        ainc(e, S["s_a2"], 1)
                        asy(e)
                        ach(e.activation(out=ln_scr[:], in_=ex_scr[:], func=AF.Ln,
                                         bias=1.0))
                        ainc(e, S["s_a2"], 1)
                        e.wait_ge(S["s_v2"], 3)
                        ach(e.activation(out=ex_scr[:].rearrange("p (m h) -> p m h", h=NH),
                                         in_=sm3[:, :, 32:48], func=AF.Exp,
                                         scale=-1.0))
                        ainc(e, S["s_a2"], 1)
                        for m in range(NT):
                            e.wait_ge(S["s_tg"], m + 1)
                            if m >= 1:
                                e.wait_ge(S["s_vg"], m)
                            e.activation(out=expell_sb[:, m * NH:(m + 1) * NH],
                                         in_=ps_e[m % 2][:], func=AF.Exp)
                            e.activation(out=eneg_scr[:], in_=ps_e[m % 2][:],
                                         func=AF.Exp, scale=-1.0)
                            e.activation(out=dec_scr[:], in_=ps_d[m % 2][:],
                                         func=AF.Exp).then_inc(S["s_sg"], 1)
                        fence(e)

                    @blk.tensor
                    def _(e):
                        e.wait_ge(S["s_v2"], 4)
                        for m in range(NT):
                            if m >= 2:
                                e.wait_ge(S["s_vb"], m - 1)
                            if m < NT - 1:
                                e.matmul(out=ps_bn[m % 2][:], lhsT=sd_sb[:],
                                         rhs=beta_sb[:, m * NH:(m + 1) * NH],
                                         start=True, stop=False)
                                e.matmul(out=ps_bn[m % 2][:], lhsT=oneh_sb[:],
                                         rhs=beta_sb[0:1, (m + 1) * NH:(m + 2) * NH],
                                         start=False, stop=True).then_inc(S["s_tm"], 1)
                            else:
                                e.matmul(out=ps_bn[m % 2][:], lhsT=sd_sb[:],
                                         rhs=beta_sb[:, m * NH:(m + 1) * NH],
                                         start=True, stop=True).then_inc(S["s_tm"], 1)
                        e.wait_ge(S["s_v2"], 5)
                        for m in range(NT):
                            if m >= 2:
                                e.wait_ge(S["s_vg"], m - 1)
                                e.wait_ge(S["s_sg"], m - 1)
                            e.matmul(out=ps_e[m % 2][:], lhsT=cum_sb[:],
                                     rhs=adt_sb[:, m * NH:(m + 1) * NH],
                                     start=True, stop=True)
                            e.matmul(out=ps_d[m % 2][:], lhsT=su_sb[:],
                                     rhs=adt_sb[:, m * NH:(m + 1) * NH],
                                     start=True, stop=True).then_inc(S["s_tg"], 1)
                        fence(e)

                    @blk.sync
                    def _(e):
                        fence(e)

                    @blk.gpsimd
                    def _(e):
                        fence(e)
                FN[0] += 1

            with contextlib.ExitStack() as sR:
                def r_(name, shape, dt):
                    return sR.enter_context(nc.sbuf_tensor(name, shape, dt))
                carry_sb = r_("carry_sb", [1, 512], F32)
                prod_sb = r_("prod_sb", [128, 2 * 512], F32)
                rs1_sb = r_("rs1_sb", [128, 512], F32)
                rsi_sb = r_("rsi_sb", [128, 512], F32)
                red_s = r_("red_s", [128, 512], F32)
                rci_sb = r_("rci_sb", [128, 512], F32)
                red_c = r_("red_c", [128, 512], F32)
                c_sb = r_("c_sb", [128, 2 * 512], BF16)
                s_sb = r_("s_sb", [128, 2 * 512], BF16)
                x1b_sb = r_("x1b_sb", [128, 512], BF16)
                x2b_sb = r_("x2b_sb", [128, 512], BF16)
                m1_sb = r_("m1_sb", [128, 512], BF16)
                m2_sb = r_("m2_sb", [128, 512], BF16)
                x1q_sb = r_("x1q_sb", [128, 512], BF16)
                x2q_sb = r_("x2q_sb", [128, 512], BF16)
                m3_sb = r_("m3_sb", [128, 512], BF16)
                m4_sb = r_("m4_sb", [128, 512], BF16)
                bb_sb = r_("bb_sb", [128, NH * DS], BF16)
                cb_sb = r_("cb_sb", [128, NH * DS], BF16)
                k_tile = r_("k_tile", [128, 2 * NH * DS], BF16)
                stage_sb = r_("stage_sb", [128, HID], BF16)
                q_tile = r_("q_tile", [128, 2 * NH * DS], BF16)
                ps_thp = [sR.enter_context(nc.psum_tensor(f"ps_th{i}", [128, 512], F32))
                          for i in range(2)]
                ps_carry = sR.enter_context(nc.psum_tensor("ps_carry", [1, 512], F32))

                with nc.Block("AII_theta") as blk:
                    @blk.vector
                    def _(e):
                        e.memset(carry_sb[:], 0.0).then_inc(S["s_vc"], 1)

                        def ropeK(m):
                            r = m % 2
                            e.wait_ge(S["s_sc3"], m + 1)
                            if m == 0:
                                e.wait_ge(S["s_cast_bbcb"], 16 * 10)
                            if m >= 2:
                                e.wait_ge(S["s_kqw"][m % 2], 32 * (m // 2))
                            kv = k_tile[:, r * 2048:(r + 1) * 2048].rearrange(
                                "p (h n) -> p h n", n=DS)
                            bb3 = bb_sb[:].rearrange("p (h n) -> p h n", n=DS)
                            bn1 = bn_sb[:, m * 256:m * 256 + 32].unsqueeze(1).broadcast_to([128, NH, 32])
                            bn2 = bn_sb[:, m * 256 + 32:m * 256 + 64].unsqueeze(1).broadcast_to([128, NH, 32])
                            bnr = bn_sb[:, m * 256 + 64:m * 256 + 128].unsqueeze(1).broadcast_to([128, NH, 64])
                            cs = c_sb[:, (m % 2) * 512:(m % 2 + 1) * 512]
                            ss = s_sb[:, (m % 2) * 512:(m % 2 + 1) * 512]
                            x1b3 = x1b_sb[:].rearrange("p (h a) -> p h a", a=32)
                            x2b3 = x2b_sb[:].rearrange("p (h a) -> p h a", a=32)
                            m13 = m1_sb[:].rearrange("p (h a) -> p h a", a=32)
                            m23 = m2_sb[:].rearrange("p (h a) -> p h a", a=32)
                            vch(e.tensor_tensor(out=x1b3, in0=bn1, in1=bb3[:, :, 0:32], op=ALU.add))
                            vch(e.tensor_tensor(out=x2b3, in0=bn2, in1=bb3[:, :, 32:64], op=ALU.add))
                            e.tensor_tensor(out=kv[:, :, 64:128], in0=bnr,
                                            in1=bb3[:, :, 64:128], op=ALU.add)
                            vsy(e)
                            vch(e.tensor_tensor(out=m1_sb[:], in0=x1b_sb[:], in1=cs, op=ALU.mult))
                            vch(e.tensor_tensor(out=m2_sb[:], in0=x2b_sb[:], in1=ss, op=ALU.mult))
                            vsy(e)
                            vch(e.tensor_tensor(out=kv[:, :, 0:32], in0=m13, in1=m23, op=ALU.subtract))
                            vsy(e)
                            vch(e.tensor_tensor(out=m1_sb[:], in0=x1b_sb[:], in1=ss, op=ALU.mult))
                            vch(e.tensor_tensor(out=m2_sb[:], in0=x2b_sb[:], in1=cs, op=ALU.mult))
                            vsy(e)
                            vch(e.tensor_tensor(out=kv[:, :, 32:64], in0=m13, in1=m23,
                                                op=ALU.add))
                            vinc(e, S["s_ropev"], 1)

                        for m in range(NT):
                            r = m % 2
                            if m >= 2:
                                e.wait_ge(S["s_thm"], m - 1)
                            e.tensor_tensor(
                                out=prod_sb[:, r * 512:(r + 1) * 512].rearrange(
                                    "p (h a) -> p h a", a=32),
                                in0=sm3[:, m, 48:80].unsqueeze(1).broadcast_to([128, NH, 32]),
                                in1=dt3[:, m, :].unsqueeze(2).broadcast_to([128, NH, 32]),
                                op=ALU.mult).then_inc(S["s_vp"], 1)
                            if m >= 1:
                                ropeK(m - 1)
                            e.wait_ge(S["s_thm"], m + 1)
                            vch(e.tensor_tensor(out=carry_sb[:], in0=carry_sb[:],
                                                in1=ps_carry[:], op=ALU.add))
                            vinc(e, S["s_vc"], 1)
                            pt = ps_thp[m % 2]
                            vch(e.tensor_scalar(out=rs1_sb[:], in0=pt[:],
                                                scalar1=1.0 / TWO_PI, scalar2=MAGIC,
                                                op0=ALU.mult, op1=ALU.add))
                            vsy(e)
                            vch(e.tensor_scalar(out=rsi_sb[:], in0=rs1_sb[:],
                                                scalar1=-MAGIC, scalar2=None,
                                                op0=ALU.add))
                            vsy(e)
                            vch(e.scalar_tensor_tensor(out=red_s[:], in0=rsi_sb[:],
                                                       scalar=-TWO_PI, in1=pt[:],
                                                       op0=ALU.mult, op1=ALU.add))
                            vch(e.tensor_scalar(out=rs1_sb[:], in0=pt[:],
                                                scalar1=1.0 / TWO_PI, scalar2=0.25,
                                                op0=ALU.mult, op1=ALU.add))
                            vsy(e)
                            vch(e.tensor_scalar(out=rs1_sb[:], in0=rs1_sb[:],
                                                scalar1=MAGIC, scalar2=None,
                                                op0=ALU.add))
                            vsy(e)
                            vch(e.tensor_scalar(out=rci_sb[:], in0=rs1_sb[:],
                                                scalar1=-MAGIC, scalar2=None,
                                                op0=ALU.add))
                            vsy(e)
                            vch(e.scalar_tensor_tensor(out=red_c[:], in0=rci_sb[:],
                                                       scalar=-TWO_PI, in1=pt[:],
                                                       op0=ALU.mult, op1=ALU.add))
                            vinc(e, S["s_rrv"], 1)
                        ropeK(NT - 1)
                        fence(e)

                    @blk.tensor
                    def _(e):
                        for m in range(NT):
                            e.wait_ge(S["s_vp"], m + 1)
                            e.wait_ge(S["s_vc"], m + 1)
                            if m >= 2:
                                e.wait_ge(S["s_rrv"], m - 1)
                            r = m % 2
                            pt = ps_thp[m % 2]
                            e.matmul(out=pt[:], lhsT=cum_sb[:],
                                     rhs=prod_sb[:, r * 512:(r + 1) * 512],
                                     start=True, stop=False)
                            e.matmul(out=pt[:], lhsT=aones_sb[:], rhs=carry_sb[:],
                                     start=False, stop=True)
                            e.matmul(out=ps_carry[:], lhsT=onesc_sb[:],
                                     rhs=prod_sb[:, r * 512:(r + 1) * 512],
                                     start=True, stop=True).then_inc(S["s_thm"], 1)
                        fence(e)

                    @blk.scalar
                    def _(e):
                        for m in range(NT):
                            e.wait_ge(S["s_rrv"], m + 1)
                            if m >= 2:
                                e.wait_ge(S["s_ropev"], m - 1)
                                e.wait_ge(S["s_ropeg"], m - 1)
                            sl = (m % 2) * 512
                            asy(e)
                            ach(e.activation(out=s_sb[:, sl:sl + 512], in_=red_s[:],
                                             func=AF.Sin, scale=1.0 - 2.0 ** -20))
                            ach(e.activation(out=c_sb[:, sl:sl + 512], in_=red_c[:],
                                             func=AF.Sin, scale=1.0 - 2.0 ** -20,
                                             bias=halfpi_sb[:]))
                            ainc(e, S["s_sc3"], 1)
                        fence(e)

                    @blk.gpsimd
                    def _(e):
                        e.dma_start(out=bb_sb[:], in_=bb_bc).then_inc(S["s_cast_bbcb"], 16)
                        e.dma_start(out=cb_sb[:], in_=cb_bc).then_inc(S["s_cast_bbcb"], 16)
                        e.wait_ge(S["s_cast_bbcb"], 16 * 10)
                        for m in range(NT):
                            r = m % 2
                            e.wait_ge(S["s_sc3"], m + 1)
                            if m >= 2:
                                e.wait_ge(S["s_kqw"][m % 2], 32 * (m // 2))
                            qv = q_tile[:, r * 2048:(r + 1) * 2048].rearrange(
                                "p (h n) -> p h n", n=DS)
                            cb3 = cb_sb[:].rearrange("p (h n) -> p h n", n=DS)
                            cn1 = bn_sb[:, m * 256 + 128:m * 256 + 160].unsqueeze(1).broadcast_to([128, NH, 32])
                            cn2 = bn_sb[:, m * 256 + 160:m * 256 + 192].unsqueeze(1).broadcast_to([128, NH, 32])
                            cnr = bn_sb[:, m * 256 + 192:m * 256 + 256].unsqueeze(1).broadcast_to([128, NH, 64])
                            cs = c_sb[:, (m % 2) * 512:(m % 2 + 1) * 512]
                            ss = s_sb[:, (m % 2) * 512:(m % 2 + 1) * 512]
                            x1q3 = x1q_sb[:].rearrange("p (h a) -> p h a", a=32)
                            x2q3 = x2q_sb[:].rearrange("p (h a) -> p h a", a=32)
                            m33 = m3_sb[:].rearrange("p (h a) -> p h a", a=32)
                            m43 = m4_sb[:].rearrange("p (h a) -> p h a", a=32)
                            gch(e.tensor_tensor(out=x1q3, in0=cn1, in1=cb3[:, :, 0:32], op=ALU.add))
                            gch(e.tensor_tensor(out=x2q3, in0=cn2, in1=cb3[:, :, 32:64], op=ALU.add))
                            e.tensor_tensor(out=qv[:, :, 64:128], in0=cnr,
                                            in1=cb3[:, :, 64:128], op=ALU.add)
                            gsy(e)
                            gch(e.tensor_tensor(out=m3_sb[:], in0=x1q_sb[:], in1=cs, op=ALU.mult))
                            gch(e.tensor_tensor(out=m4_sb[:], in0=x2q_sb[:], in1=ss, op=ALU.mult))
                            gsy(e)
                            gch(e.tensor_tensor(out=qv[:, :, 0:32], in0=m33, in1=m43, op=ALU.subtract))
                            gsy(e)
                            gch(e.tensor_tensor(out=m3_sb[:], in0=x1q_sb[:], in1=ss, op=ALU.mult))
                            gch(e.tensor_tensor(out=m4_sb[:], in0=x2q_sb[:], in1=cs, op=ALU.mult))
                            gsy(e)
                            gch(e.tensor_tensor(out=qv[:, :, 32:64], in0=m33, in1=m43,
                                                op=ALU.add))
                            ginc(e, S["s_ropeg"], 1)
                            gsy(e)
                            e.dma_start(out=q_dram[m * 128:(m + 1) * 128, :],
                                        in_=q_tile[:, r * 2048:(r + 1) * 2048]
                                        ).then_inc(S["s_kqw"][m % 2], 16)
                            e.wait_ge(S["s_ropev"], m + 1)
                            e.dma_start(out=k_dram[m * 128:(m + 1) * 128, :],
                                        in_=k_tile[:, r * 2048:(r + 1) * 2048]
                                        ).then_inc(S["s_kqw"][m % 2], 16)
                        e.wait_ge(S["s_kqw"][0], 32 * 16)
                        e.wait_ge(S["s_kqw"][1], 32 * 16)
                        fence(e)

                    @blk.sync
                    def _(e):
                        e.wait_ge(S["s_cast_wd"], 16 * 15)
                        for hc in range(24):
                            sl3 = (hc % 3) * 1024
                            if hc >= 3:
                                e.wait_ge(S["s_prep_wd"], 16 * (2 * hc - 4))
                            if hc >= 1:
                                e.wait_ge(S["s_prep_wd"], 16 * (2 * hc))
                            e.dma_start(out=stage_sb[:, sl3:sl3 + 1024],
                                        in_=wd_bf[:, hc * 128:(hc + 1) * 128],
                                        transpose=True).then_inc(S["s_prep_wd"], 16)
                            e.wait_ge(S["s_prep_wd"], 16 * (2 * hc + 1))
                            e.dma_start(out=wdt_dram[hc * 128:(hc + 1) * 128, :],
                                        in_=stage_sb[:, sl3:sl3 + 1024]
                                        ).then_inc(S["s_prep_wd"], 16)
                        e.wait_ge(S["s_cast_wg"], 16 * 15)
                        e.wait_ge(S["s_prep_wd"], 16 * 48)
                        for dc in range(8):
                            if dc >= 1:
                                e.wait_ge(S["s_prep_wg"], 16 * (2 * dc))
                            e.dma_start(out=stage_sb[:], in_=wg_bf[:, dc * 128:(dc + 1) * 128],
                                        transpose=True).then_inc(S["s_prep_wg"], 16)
                            e.wait_ge(S["s_prep_wg"], 16 * (2 * dc + 1))
                            e.dma_start(out=wgt_dram[:, dc, :, :].transpose([1, 0, 2]),
                                        in_=stage_sb[:].rearrange("p (hc h) -> p hc h", h=128)
                                        ).then_inc(S["s_prep_wg"], 16)
                        e.wait_ge(S["s_cast_wu"], 16 * 15)
                        e.wait_ge(S["s_prep_wg"], 16 * 16)
                        for dc in range(8):
                            if dc >= 1:
                                e.wait_ge(S["s_prep_wu"], 16 * (2 * dc))
                            e.dma_start(out=stage_sb[:], in_=wu_bf[:, dc * 128:(dc + 1) * 128],
                                        transpose=True).then_inc(S["s_prep_wu"], 16)
                            e.wait_ge(S["s_prep_wu"], 16 * (2 * dc + 1))
                            e.dma_start(out=wut_dram[:, dc, :, :].transpose([1, 0, 2]),
                                        in_=stage_sb[:].rearrange("p (hc h) -> p hc h", h=128)
                                        ).then_inc(S["s_prep_wu"], 16)
                        fence(e)
                FN[0] += 1

    if "AV" in PHASES:
        with contextlib.ExitStack() as sF:
            def f_(name, shape, dt):
                return sF.enter_context(nc.sbuf_tensor(name, shape, dt))
            wdt_sb = f_("wdt_sb", [128, 24 * 1024], BF16)
            xnth_ring = f_("xnth_ring", [128, 8 * 1024], BF16)
            wgu_ring = f_("wgu_ring", [128, 4 * 1024], BF16)
            h_sb = f_("h_sb", [128, 24 * 1024], BF16)
            sg_scr = f_("sg_scr", [128, 1024], BF16)
            g_tile = f_("g_tile", [128, 2 * 1024], BF16)
            ps_g = sF.enter_context(nc.psum_tensor("ps_g", [128, 1024], F32))
            ps_u2 = sF.enter_context(nc.psum_tensor("ps_u2", [128, 1024], F32))
            ps_dn = sF.enter_context(nc.psum_tensor("ps_dn", [128, 2 * 1024], F32))

            with nc.Block("AV") as blk:
                @blk.sync
                def _(e):
                    e.wait_ge(S["s_prep_wg"], 16 * 16)
                    e.wait_ge(S["s_prep_wu"], 16 * 16)
                    for sl in range(2):
                        if sl > 0:
                            e.wait_ge(S["s_prep_wd"], 16 * 48)
                            for hc in range(24):
                                e.dma_start(out=wdt_sb[:, hc * 1024:(hc + 1) * 1024],
                                            in_=wdt_dram[hc * 128:(hc + 1) * 128, :].rearrange(
                                                "(o p) d -> p o d", p=128),
                                            ).then_inc(S["s_prep_wd"], 16)
                            e.wait_ge(S["s_fmm"], 48 * sl)
                        for dc in range(8):
                            e.dma_start(out=xnth_ring[:, dc * 1024:(dc + 1) * 1024],
                                        in_=xnh_dram[sl * 1024:(sl + 1) * 1024,
                                                     dc * 128:(dc + 1) * 128],
                                        transpose=True).then_inc(S["s_fx"], 16)
                        for hc in range(24):
                            g_i = sl * 24 + hc
                            rr = hc % 2
                            if g_i >= 2:
                                e.wait_ge(S["s_fmm"], 2 * g_i - 2)
                            e.dma_start(
                                out=wgu_ring[:, (rr * 2) * 1024:(rr * 2 + 1) * 1024
                                             ].rearrange("p (dc h) -> p dc h", h=128),
                                in_=wgt_dram[hc, :, :, :].transpose([1, 0, 2])
                                ).then_inc(S["s_fw"][g_i % 2], 16)
                            e.dma_start(
                                out=wgu_ring[:, (rr * 2 + 1) * 1024:(rr * 2 + 2) * 1024
                                             ].rearrange("p (dc h) -> p dc h", h=128),
                                in_=wut_dram[hc, :, :, :].transpose([1, 0, 2])
                                ).then_inc(S["s_fw"][g_i % 2], 16)
                    e.wait_ge(S["s_fx"], 16 * 16)
                    e.wait_ge(S["s_fw"][0], 32 * 12)
                    e.wait_ge(S["s_fw"][1], 32 * 12)
                    e.wait_ge(S["s_prep_wd"], 16 * 72)
                    fence(e)

                @blk.tensor
                def _(e):
                    e.wait_ge(S["s_prep_wg"], 16 * 16)
                    e.wait_ge(S["s_prep_wu"], 16 * 16)
                    for sl in range(2):
                        e.wait_ge(S["s_fx"], 16 * 8 * (sl + 1))
                        for hc in range(24):
                            g_i = sl * 24 + hc
                            rr = hc % 2
                            e.wait_ge(S["s_fw"][g_i % 2], 32 * (g_i // 2 + 1))
                            if g_i >= 1:
                                e.wait_ge(S["s_fg"], g_i)
                                e.wait_ge(S["s_fev"], g_i)
                            for dc in range(8):
                                lt = wgu_ring[:, (rr * 2) * 1024 + dc * 128:
                                              (rr * 2) * 1024 + (dc + 1) * 128]
                                e.matmul(out=ps_g[:, 0:512], lhsT=lt,
                                         rhs=xnth_ring[:, dc * 1024:dc * 1024 + 512],
                                         start=(dc == 0), stop=(dc == 7))
                                mm = e.matmul(out=ps_g[:, 512:1024], lhsT=lt,
                                              rhs=xnth_ring[:, dc * 1024 + 512:(dc + 1) * 1024],
                                              start=(dc == 0), stop=(dc == 7))
                                if dc == 7:
                                    mm.then_inc(S["s_fmm"], 1)
                            for dc in range(8):
                                lt = wgu_ring[:, (rr * 2 + 1) * 1024 + dc * 128:
                                              (rr * 2 + 1) * 1024 + (dc + 1) * 128]
                                e.matmul(out=ps_u2[:, 0:512], lhsT=lt,
                                         rhs=xnth_ring[:, dc * 1024:dc * 1024 + 512],
                                         start=(dc == 0), stop=(dc == 7))
                                mm = e.matmul(out=ps_u2[:, 512:1024], lhsT=lt,
                                              rhs=xnth_ring[:, dc * 1024 + 512:(dc + 1) * 1024],
                                              start=(dc == 0), stop=(dc == 7))
                                if dc == 7:
                                    mm.then_inc(S["s_fmm"], 1)
                        e.wait_ge(S["s_fev"], 24 * (sl + 1))
                        e.wait_ge(S["s_prep_wd"], 16 * 72)
                        for tt in range(8):
                            ti = sl * 8 + tt
                            rr = tt % 2
                            if ti >= 2:
                                e.wait_ge(S["s_fge"], ti - 1)
                            for hc in range(24):
                                lt = h_sb[:, hc * 1024 + tt * 128:hc * 1024 + (tt + 1) * 128]
                                e.matmul(out=ps_dn[:, rr * 1024:rr * 1024 + 512], lhsT=lt,
                                         rhs=wdt_sb[:, hc * 1024:hc * 1024 + 512],
                                         start=(hc == 0), stop=(hc == 23))
                                mm = e.matmul(out=ps_dn[:, rr * 1024 + 512:(rr + 1) * 1024],
                                              lhsT=lt,
                                              rhs=wdt_sb[:, hc * 1024 + 512:(hc + 1) * 1024],
                                              start=(hc == 0), stop=(hc == 23))
                                if hc == 23:
                                    mm.then_inc(S["s_fdm"], 1)
                    fence(e)

                @blk.scalar
                def _(e):
                    for sl in range(2):
                        for hc in range(24):
                            g_i = sl * 24 + hc
                            e.wait_ge(S["s_fmm"], 2 * g_i + 1)
                            if g_i >= 1:
                                e.wait_ge(S["s_fev"], g_i)
                            e.activation(out=sg_scr[:], in_=ps_g[:], func=AF.Silu,
                                         scale=DEQ).then_inc(S["s_fg"], 1)
                        for tt in range(8):
                            ti = sl * 8 + tt
                            rr = tt % 2
                            e.wait_ge(S["s_fdm"], ti + 1)
                            if ti >= 2:
                                e.wait_ge(S["s_fgw"][ti % 2], 16 * (ti // 2))
                            e.activation(out=g_tile[:, rr * 1024:(rr + 1) * 1024],
                                         in_=ps_dn[:, rr * 1024:(rr + 1) * 1024],
                                         func=AF.Silu, scale=DEQ).then_inc(S["s_fge"], 1)
                        fence(e) if sl == 1 else None

                @blk.vector
                def _(e):
                    for sl in range(2):
                        if sl > 0:
                            e.wait_ge(S["s_fdm"], 8 * sl)
                        for hc in range(24):
                            g_i = sl * 24 + hc
                            e.wait_ge(S["s_fg"], g_i + 1)
                            e.wait_ge(S["s_fmm"], 2 * g_i + 2)
                            e.scalar_tensor_tensor(out=h_sb[:, hc * 1024:(hc + 1) * 1024],
                                                   in0=ps_u2[:], scalar=DEQ,
                                                   in1=sg_scr[:], op0=ALU.mult,
                                                   op1=ALU.mult).then_inc(S["s_fev"], 1)
                    fence(e)

                @blk.gpsimd
                def _(e):
                    for ti in range(16):
                        rr = ti % 2
                        e.wait_ge(S["s_fge"], ti + 1)
                        e.dma_start(out=g_dram[ti * 128:(ti + 1) * 128, :],
                                    in_=g_tile[:, rr * 1024:(rr + 1) * 1024]
                                    ).then_inc(S["s_fgw"][ti % 2], 16)
                    e.wait_ge(S["s_fgw"][0], 16 * 8)
                    e.wait_ge(S["s_fgw"][1], 16 * 8)
                    fence(e)
            FN[0] += 1

    if "B" in PHASES:
        with contextlib.ExitStack() as sB:
            def b_(name, shape, dt):
                return sB.enter_context(nc.sbuf_tensor(name, shape, dt))
            ktq_sb = b_("ktq_sb", [128, 2 * 2 * NH * 512], BF16)
            ktok_sb = b_("ktok_sb", [128, 2 * NH * DS], BF16)
            xc_sb = b_("xc_sb", [128, 2 * 1024], BF16)
            zc_sb = b_("zc_sb", [128, 2 * 1024], BF16)
            xw_sb = b_("xw_sb", [128, 1024], BF16)
            siluz_sb = b_("siluz_sb", [128, 1024], BF16)
            sw_sb = b_("sw_sb", [128, 16 * 128], BF16)
            t1_sb = b_("t1_sb", [128, 1024], F32)
            dxp_sb = b_("dxp_sb", [128, 1024], F32)
            t2_sb = b_("t2_sb", [128, 1024], F32)
            yg_tile = b_("yg_tile", [128, 2 * 1024], BF16)
            ps_S = [sB.enter_context(nc.psum_tensor(f"ps_S{i}", [128, 512], F32))
                    for i in range(2)]
            ps_y = sB.enter_context(nc.psum_tensor("ps_y", [128, 1024], F32))
            ps_w = sB.enter_context(nc.psum_tensor("ps_w", [128, 1024], F32))

            def kt_slice(sr, kq, h, col, width=128):
                base = ((sr * 2 + kq) * NH + h) * 512
                return ktq_sb[:, base + col:base + col + width]

            with nc.Block("B") as blk:
                @blk.sync
                def _(e):
                    for h in range(NH):
                        for kq, dram in ((0, k_dram), (1, q_dram)):
                            e.dma_start(out=kt_slice(0, kq, h, 0, 512),
                                        in_=dram[0:512, h * DS:(h + 1) * DS],
                                        transpose=True).then_inc(S["s_kq"][0], 16)
                    for s in range(1, 8):
                        e.wait_ge(S["s_wmm"], 4 * (s - 1))
                        for h in range(NH):
                            for kq, dram in ((0, k_dram), (1, q_dram)):
                                e.dma_start(out=kt_slice(s % 2, kq, h, 0, 512),
                                            in_=dram[s * 512:(s + 1) * 512,
                                                     h * DS:(h + 1) * DS],
                                            transpose=True).then_inc(S["s_kq"][s % 2], 16)
                    e.wait_ge(S["s_kq"][0], 512 * 4)
                    e.wait_ge(S["s_kq"][1], 512 * 4)
                    fence(e)

                @blk.scalar
                def _(e):
                    for c in range(NT):
                        rc = c % 2
                        if c >= 2:
                            e.wait_ge(S["s_wmm"], c - 1)
                            e.wait_ge(S["s_dxp"], c - 1)
                            e.wait_ge(S["s_bz"], c - 1)
                        e.dma_start(out=ktok_sb[:, rc * 2048:(rc + 1) * 2048],
                                    in_=k_dram[c * 128:(c + 1) * 128, :]).then_inc(S["s_bx"][c % 2], 16)
                        e.dma_start(out=xc_sb[:, rc * 1024:(rc + 1) * 1024],
                                    in_=x_dram[c * 128:(c + 1) * 128, :]).then_inc(S["s_bx"][c % 2], 16)
                        e.dma_start(out=zc_sb[:, rc * 1024:(rc + 1) * 1024],
                                    in_=z_dram[c * 128:(c + 1) * 128, :]).then_inc(S["s_bx"][c % 2], 16)
                        e.wait_ge(S["s_bx"][c % 2], 48 * (c // 2 + 1))
                        if c >= 1:
                            e.wait_ge(S["s_bev"], c)
                        e.activation(out=siluz_sb[:], in_=zc_sb[:, rc * 1024:(rc + 1) * 1024],
                                     func=AF.Silu).then_inc(S["s_bz"], 1)
                    e.wait_ge(S["s_bx"][0], 48 * 16)
                    e.wait_ge(S["s_bx"][1], 48 * 16)
                    fence(e)

                @blk.tensor
                def _(e):
                    for c in range(NT):
                        s = c // 4
                        sr = s % 2
                        rc = c % 2
                        wc = (c % 4) * 128
                        e.wait_ge(S["s_kq"][s % 2], 512 * (s // 2 + 1))
                        e.wait_ge(S["s_bx"][c % 2], 48 * (c // 2 + 1))
                        for g4 in range(4):
                            gi = 4 * c + g4
                            if gi >= 2:
                                e.wait_ge(S["s_sw"], gi - 1)
                            for hh in range(4):
                                h = 4 * g4 + hh
                                mm = e.matmul(out=ps_S[g4 % 2][:, hh * 128:(hh + 1) * 128],
                                              lhsT=kt_slice(sr, 0, h, wc),
                                              rhs=kt_slice(sr, 1, h, wc),
                                              start=True, stop=True)
                                if hh == 3:
                                    mm.then_inc(S["s_smm"], 1)
                        if c >= 1:
                            e.wait_ge(S["s_bev1"], c)
                        e.wait_ge(S["s_hup"], c + 1)
                        for h in range(NH):
                            e.wait_ge(S["s_sw"], 4 * c + h // 4 + 1)
                            e.matmul(out=ps_y[:, h * 64:(h + 1) * 64],
                                     lhsT=sw_sb[:, h * 128:(h + 1) * 128],
                                     rhs=xc_sb[:, rc * 1024 + h * 64:rc * 1024 + (h + 1) * 64],
                                     start=True, stop=False)
                            mm = e.matmul(out=ps_y[:, h * 64:(h + 1) * 64],
                                          lhsT=kt_slice(sr, 1, h, wc),
                                          rhs=Hbf_sb[:, h * 64:(h + 1) * 64],
                                          start=False, stop=True)
                            if h == NH - 1:
                                mm.then_inc(S["s_ymm"], 1)
                        e.wait_ge(S["s_xwg"], c + 1)
                        for h in range(NH):
                            mm = e.matmul(out=ps_w[:, h * 64:(h + 1) * 64],
                                          lhsT=ktok_sb[:, rc * 2048 + h * DS:rc * 2048 + (h + 1) * DS],
                                          rhs=xw_sb[:, h * 64:(h + 1) * 64],
                                          start=True, stop=True)
                            if h == NH - 1:
                                mm.then_inc(S["s_wmm"], 1)
                    fence(e)

                @blk.vector
                def _(e):
                    e.sem_inc(S["s_hup"], 1)
                    for c in range(NT):
                        rc = c % 2
                        for g4 in range(4):
                            gi = 4 * c + g4
                            e.wait_ge(S["s_smm"], gi + 1)
                            for hh in range(4):
                                h = 4 * g4 + hh
                                stt = e.scalar_tensor_tensor(
                                    out=sw_sb[:, h * 128:(h + 1) * 128],
                                    in0=ps_S[g4 % 2][:, hh * 128:(hh + 1) * 128],
                                    scalar=gcx_sb[:, c * NH + h:c * NH + h + 1],
                                    in1=strict_sb[:], op0=ALU.mult, op1=ALU.mult)
                                if hh == 3:
                                    stt.then_inc(S["s_sw"], 1)
                        e.wait_ge(S["s_ymm"], c + 1)
                        if c >= 1:
                            e.wait_ge(S["s_t2"], c)
                        e.tensor_tensor(
                            out=t1_sb[:].rearrange("p (h q) -> p h q", q=64),
                            in0=ps_y[:].rearrange("p (h q) -> p h q", q=64),
                            in1=expell_sb[:, c * NH:(c + 1) * NH].unsqueeze(2
                                ).broadcast_to([128, NH, 64]),
                            op=ALU.mult).then_inc(S["s_bev1"], 1)
                        e.wait_ge(S["s_t2"], c + 1)
                        e.wait_ge(S["s_bz"], c + 1)
                        if c >= 2:
                            e.wait_ge(S["s_ygw"][c % 2], 16 * (c // 2))
                        e.tensor_tensor(out=yg_tile[:, rc * 1024:(rc + 1) * 1024],
                                        in0=t2_sb[:], in1=siluz_sb[:],
                                        op=ALU.mult).then_inc(S["s_bev"], 1)
                        e.wait_ge(S["s_wmm"], c + 1)
                        e.wait_ge(S["s_hsg"], c + 1)
                        vch(e.tensor_tensor(out=H_sb[:], in0=Hs_sb[:], in1=ps_w[:],
                                                op=ALU.add))
                        vsy(e)
                        e.tensor_copy(out=Hbf_sb[:], in_=H_sb[:]).then_inc(S["s_hup"], 1)
                    fence(e)

                @blk.gpsimd
                def _(e):
                    for c in range(NT):
                        rc = c % 2
                        e.wait_ge(S["s_bx"][c % 2], 48 * (c // 2 + 1))
                        if c >= 1:
                            e.wait_ge(S["s_wmm"], c)
                        xc3 = xc_sb[:, rc * 1024:(rc + 1) * 1024].rearrange(
                            "p (h q) -> p h q", q=64)
                        e.tensor_tensor(
                            out=xw_sb[:].rearrange("p (h q) -> p h q", q=64),
                            in0=xc3,
                            in1=wsg_sb[:, c * NH:(c + 1) * NH].unsqueeze(2
                                ).broadcast_to([128, NH, 64]),
                            op=ALU.mult).then_inc(S["s_xwg"], 1)
                        gch(e.tensor_tensor(
                            out=dxp_sb[:].rearrange("p (h q) -> p h q", q=64),
                            in0=xc3,
                            in1=coef_sb[:, c * NH:(c + 1) * NH].unsqueeze(2
                                ).broadcast_to([128, NH, 64]),
                            op=ALU.mult))
                        ginc(e, S["s_dxp"], 1)
                        e.wait_ge(S["s_hup"], c + 1)
                        e.tensor_tensor(
                            out=Hs_sb[:].rearrange("p (h q) -> p h q", q=64),
                            in0=H_sb[:].rearrange("p (h q) -> p h q", q=64),
                            in1=cdb_sb[:, c * NH:(c + 1) * NH].unsqueeze(2
                                ).broadcast_to([128, NH, 64]),
                            op=ALU.mult).then_inc(S["s_hsg"], 1)
                        e.wait_ge(S["s_bev1"], c + 1)
                        gsy(e)
                        e.tensor_tensor(out=t2_sb[:], in0=t1_sb[:], in1=dxp_sb[:],
                                        op=ALU.add).then_inc(S["s_t2"], 1)
                        e.wait_ge(S["s_bev"], c + 1)
                        e.dma_start(out=yg_dram[c * 128:(c + 1) * 128, :],
                                    in_=yg_tile[:, rc * 1024:(rc + 1) * 1024]
                                    ).then_inc(S["s_ygw"][c % 2], 16)
                    e.wait_ge(S["s_ygw"][0], 16 * 16)
                    e.wait_ge(S["s_ygw"][1], 16 * 16)
                    fence(e)
            FN[0] += 1

    if "C" in PHASES:
        with contextlib.ExitStack() as sC:
            def c_(name, shape, dt):
                return sC.enter_context(nc.sbuf_tensor(name, shape, dt))
            ygT_sb = c_("ygT_sb", [128, 8 * 4096], BF16)
            wout_sb = c_("wout_sb", [128, 8 * 1024], BF16)
            s_ring = c_("s_ring", [128, 3 * 1024], F32)
            so_ring = c_("so_ring", [128, 2 * 1024], F32)
            sp_ring = c_("sp_ring", [128, 2 * 1024], F32)
            uh_ring = c_("uh_ring", [128, 2 * 1024], F32)
            gh_ring = c_("gh_ring", [128, 2 * 1024], BF16)
            stot_sb = c_("stot_sb", [128, 1024], F32)
            out_ring = c_("out_ring", [128, 2 * 1024], F32)
            ps_s = sC.enter_context(nc.psum_tensor("ps_s", [128, 2 * 1024], F32))

            with nc.Block("C") as blk:
                @blk.sync
                def _(e):
                    for pc in range(8):
                        e.dma_start(out=ygT_sb[:, pc * 4096:(pc + 1) * 4096],
                                    in_=yg_dram[:, pc * 128:(pc + 1) * 128],
                                    transpose=True).then_inc(S["s_yt"], 16)
                    for ft in range(NTH):
                        rf = ft % 2
                        if ft >= 2:
                            e.wait_ge(S["s_fin"], ft - 1)
                        e.dma_start(out=uh_ring[:, rf * 1024:(rf + 1) * 1024],
                                    in_=u_half[ft * 128:(ft + 1) * 128, :]
                                    ).then_inc(S["s_uh"][ft % 2], 16)
                        e.dma_start(out=gh_ring[:, rf * 1024:(rf + 1) * 1024],
                                    in_=g_dram[ft * 128:(ft + 1) * 128, :]
                                    ).then_inc(S["s_gh"][ft % 2], 16)
                        e.wait_ge(S["s_fin"], ft + 1)
                        e.dma_start(out=y_out[ft * 128:(ft + 1) * 128, :],
                                    in_=out_ring[:, rf * 1024:(rf + 1) * 1024]
                                    ).then_inc(S["s_out"][ft % 2], 16)
                    e.wait_ge(S["s_out"][0], 16 * (NTH // 2))
                    e.wait_ge(S["s_out"][1], 16 * (NTH // 2))
                    fence(e)

                @blk.gpsimd
                def _(e):
                    for pc in range(8):
                        e.dma_start(out=wout_sb[:, pc * 1024:(pc + 1) * 1024],
                                    in_=w_out[pc * 128:(pc + 1) * 128, :]
                                    ).then_inc(S["s_wo"], 16)
                    for tt in range(NT):
                        rt = tt % 3
                        e.wait_ge(S["s_se"], tt + 1)
                        if tt >= 2:
                            e.wait_ge(S["s_sc"][tt % 2], 16 * ((tt - 2) // 2 + 1))
                        e.indirect_dma_start(
                            out=xchg,
                            out_offset=IndirectOffsetOnAxis(ap=sidx_sb[:, tt:tt + 1], axis=0),
                            in_=s_ring[:, rt * 1024:(rt + 1) * 1024],
                            in_offset=None).then_inc(S["s_sc"][tt % 2], 16)
                    e.wait_ge(S["s_sc"][0], 16 * (NT // 2))
                    e.wait_ge(S["s_sc"][1], 16 * (NT // 2))
                    e.indirect_dma_start(
                        out=flags,
                        out_offset=IndirectOffsetOnAxis(ap=fidx_sb[:], axis=0),
                        in_=nonce_t_sb[:], in_offset=None).then_inc(S["s_fl"], 16)
                    e.wait_ge(S["s_fl"], 16)
                    nonce_r = e.alloc_register("nonce_r")
                    e.reg_load(nonce_r, nonce_sb[0:1, 0:1])
                    tmp_r = e.alloc_register("tmp_r")
                    nm_r = e.alloc_register("nm_r")
                    cnt_r = e.alloc_register("cnt_r")
                    cond_r = e.alloc_register("cond_r")
                    c1_r = e.alloc_register("c1_r")
                    c2_r = e.alloc_register("c2_r")
                    e.reg_mov(cnt_r, 0)
                    e.reg_mov(cond_r, 1)
                    flags_s = flags.rearrange("(k f) c -> k f c", f=128)[:, 0, :]
                    with e.While(cond_r):
                        e.dma_start(out=flagrd_sb[:], in_=flags_s).then_inc(S["s_fl"], 16)
                        e.reg_mov(nm_r, 0)
                        for kk in range(2):
                            e.reg_load(tmp_r, flagrd_sb[kk:kk + 1, 0:1])
                            e.reg_alu(tmp_r, tmp_r, nonce_r, ALU.is_equal)
                            e.reg_alu(nm_r, nm_r, tmp_r, ALU.add)
                        e.reg_add(cnt_r, cnt_r, 1)
                        e.reg_alu(c1_r, nm_r, 2, ALU.is_lt)
                        e.reg_alu(c2_r, cnt_r, POLL_CAP, ALU.is_lt)
                        e.reg_alu(cond_r, c1_r, c2_r, ALU.logical_and)
                    e.sem_inc(S["s_bar"], 1)
                    for ft in range(NTH):
                        rf = ft % 2
                        if ft >= 2:
                            e.wait_ge(S["s_fin"], ft - 1)
                        e.indirect_dma_start(
                            out=so_ring[:, rf * 1024:(rf + 1) * 1024], out_offset=None,
                            in_=xchg,
                            in_offset=IndirectOffsetOnAxis(ap=goidx_sb[:, ft:ft + 1], axis=0)
                            ).then_inc(S["s_gat"][ft % 2], 16)
                        e.indirect_dma_start(
                            out=sp_ring[:, rf * 1024:(rf + 1) * 1024], out_offset=None,
                            in_=xchg,
                            in_offset=IndirectOffsetOnAxis(ap=gpidx_sb[:, ft:ft + 1], axis=0)
                            ).then_inc(S["s_gat"][ft % 2], 16)
                    e.wait_ge(S["s_out"][0], 16 * (NTH // 2))
                    e.wait_ge(S["s_out"][1], 16 * (NTH // 2))
                    fence(e)

                @blk.tensor
                def _(e):
                    e.wait_ge(S["s_yt"], 16 * 8)
                    e.wait_ge(S["s_wo"], 16 * 8)
                    for tt in range(NT):
                        if tt >= 2:
                            e.wait_ge(S["s_se"], tt - 1)
                        ob = (tt % 2) * 1024
                        for pc in range(8):
                            lt = ygT_sb[:, pc * 4096 + tt * 128:pc * 4096 + (tt + 1) * 128]
                            e.matmul(out=ps_s[:, ob:ob + 512], lhsT=lt,
                                     rhs=wout_sb[:, pc * 1024:pc * 1024 + 512],
                                     start=(pc == 0), stop=(pc == 7))
                            mm = e.matmul(out=ps_s[:, ob + 512:ob + 1024], lhsT=lt,
                                          rhs=wout_sb[:, pc * 1024 + 512:(pc + 1) * 1024],
                                          start=(pc == 0), stop=(pc == 7))
                            if pc == 7:
                                mm.then_inc(S["s_om"], 1)
                    e.wait_ge(S["s_out"][0], 16 * (NTH // 2))
                    e.wait_ge(S["s_out"][1], 16 * (NTH // 2))
                    fence(e)

                @blk.vector
                def _(e):
                    for tt in range(NT):
                        rt = tt % 3
                        e.wait_ge(S["s_om"], tt + 1)
                        if tt >= 3:
                            e.wait_ge(S["s_sc"][(tt - 3) % 2], 16 * ((tt - 3) // 2 + 1))
                        e.tensor_copy(out=s_ring[:, rt * 1024:(rt + 1) * 1024],
                                      in_=ps_s[:, (tt % 2) * 1024:(tt % 2 + 1) * 1024]
                                      ).then_inc(S["s_se"], 1)
                    for ft in range(NTH):
                        rf = ft % 2
                        e.wait_ge(S["s_gat"][ft % 2], 32 * (ft // 2 + 1))
                        e.wait_ge(S["s_uh"][ft % 2], 16 * (ft // 2 + 1))
                        e.wait_ge(S["s_gh"][ft % 2], 16 * (ft // 2 + 1))
                        if ft >= 2:
                            e.wait_ge(S["s_out"][ft % 2], 16 * (ft // 2))
                        vch(e.tensor_tensor(out=stot_sb[:],
                                                in0=so_ring[:, rf * 1024:(rf + 1) * 1024],
                                                in1=sp_ring[:, rf * 1024:(rf + 1) * 1024],
                                                op=ALU.add))
                        vsy(e)
                        vch(e.tensor_tensor(out=stot_sb[:], in0=stot_sb[:],
                                            in1=gh_ring[:, rf * 1024:(rf + 1) * 1024],
                                            op=ALU.mult))
                        vsy(e)
                        vch(e.tensor_tensor(out=out_ring[:, rf * 1024:(rf + 1) * 1024],
                                            in0=uh_ring[:, rf * 1024:(rf + 1) * 1024],
                                            in1=stot_sb[:], op=ALU.add))
                        vinc(e, S["s_fin"], 1)
                    e.wait_ge(S["s_out"][0], 16 * (NTH // 2))
                    e.wait_ge(S["s_out"][1], 16 * (NTH // 2))
                    fence(e)

                @blk.scalar
                def _(e):
                    e.wait_ge(S["s_out"][0], 16 * (NTH // 2))
                    e.wait_ge(S["s_out"][1], 16 * (NTH // 2))
                    fence(e)
        FN[0] += 1
    else:
        with nc.Block("stub") as blk:
            @blk.sync
            def _(e):
                e.dma_start(out=y_out[:], in_=u_half[:]).then_inc(S["s_out"][0], 16)
                e.wait_ge(S["s_out"][0], 16)
                fence(e)
            for en in ("gpsimd", "scalar", "vector", "tensor"):
                getattr(blk, en)(fence)
        FN[0] += 1

    st.close()
    return nc



_NC_CACHE = [None]


def kernel(**inputs):
    u = np.asarray(inputs["L0_attn_u"])
    W_in = np.asarray(inputs["L0_attn_W_in"])
    W_out = np.asarray(inputs["L0_attn_W_out"])
    dt_bias = np.asarray(inputs["L0_attn_dt_bias"])
    B_bias = np.asarray(inputs["L0_attn_B_bias"])
    C_bias = np.asarray(inputs["L0_attn_C_bias"])
    B_norm_w = np.asarray(inputs["L0_attn_B_norm_w"])
    C_norm_w = np.asarray(inputs["L0_attn_C_norm_w"])
    Dv = np.asarray(inputs["L0_attn_D"])
    w_gate = np.asarray(inputs["L0_ffn_w_gate"])
    w_up = np.asarray(inputs["L0_ffn_w_up"])
    w_down = np.asarray(inputs["L0_ffn_w_down"])

    if _NC_CACHE[0] is None:
        _NC_CACHE[0] = build_nc()
    nc = _NC_CACHE[0]

    f32 = np.float32
    consts = dict(
        ident_in=np.eye(128, dtype=ml_dtypes.bfloat16),
        cum_in=np.triu(np.ones((128, 128), f32)),
        su_in=np.tril(np.ones((128, 128), f32), -1),
        strict_in=np.triu(np.ones((128, 128)), 1).astype(ml_dtypes.bfloat16),
        sd_in=np.eye(128, k=-1, dtype=f32),
        oneh_in=np.eye(1, 128, k=127, dtype=f32),
        onesc_in=np.ones((128, 1), f32),
        aones_in=np.ones((1, 128), f32),
    )
    nonce_val = int(np.random.randint(1, 2 ** 31 - 1))
    consts["nonce_in"] = np.array([[nonce_val]], np.int32)
    consts["noncet_in"] = np.full((128, 4), nonce_val, np.int32)

    in_maps = []
    for k in range(8):
        b, g = k // 2, k % 2
        me = k & 1
        zc = W_in[:, g * 1024:(g + 1) * 1024]
        xc = W_in[:, 2048 + g * 1024:2048 + (g + 1) * 1024]
        Bc = W_in[:, 4096:4224]
        Cc = W_in[:, 4224:4352]
        dtc = W_in[:, 4352 + g * NH:4352 + (g + 1) * NH]
        Ac = W_in[:, 4384 + g * NH:4384 + (g + 1) * NH]
        trc = W_in[:, 4416 + g * NH:4416 + (g + 1) * NH]
        angc = W_in[:, 4448:4480]
        w_in_k = np.ascontiguousarray(
            np.concatenate([xc, zc, Bc, Cc, dtc, Ac, trc, angc], axis=1))
        bb = np.ascontiguousarray(B_bias[g * NH:(g + 1) * NH])
        cb = np.ascontiguousarray(C_bias[g * NH:(g + 1) * NH])
        scat = np.zeros((128, NT), np.uint32)
        gown = np.zeros((128, NTH), np.uint32)
        gpar = np.zeros((128, NTH), np.uint32)
        p_ = np.arange(128, dtype=np.uint32)
        for tt in range(NT):
            scat[:, tt] = me * T + tt * 128 + p_
        for ft in range(NTH):
            gown[:, ft] = me * T + g * TH + ft * 128 + p_
            gpar[:, ft] = (1 - me) * T + g * TH + ft * 128 + p_
        im = dict(
            u_full=np.ascontiguousarray(u[b]),
            u_half=np.ascontiguousarray(u[b, g * TH:(g + 1) * TH]),
            w_in=w_in_k,
            w_out=np.ascontiguousarray(W_out[g * 1024:(g + 1) * 1024]),
            wg_i=w_gate, wu_i=w_up, wd_i=w_down,
            bb_bc=np.ascontiguousarray(
                np.broadcast_to(bb.reshape(1, NH * DS), (128, NH * DS))),
            cb_bc=np.ascontiguousarray(
                np.broadcast_to(cb.reshape(1, NH * DS), (128, NH * DS))),
            bbt_in=np.ascontiguousarray(bb.T),
            cbt_in=np.ascontiguousarray(cb.T),
            bnw_bc=np.ascontiguousarray(np.broadcast_to(B_norm_w.reshape(1, DS), (128, DS))),
            cnw_bc=np.ascontiguousarray(np.broadcast_to(C_norm_w.reshape(1, DS), (128, DS))),
            dtb_bc=np.ascontiguousarray(
                np.broadcast_to(dt_bias[g * NH:(g + 1) * NH].reshape(1, NH), (128, NH))),
            d_bc=np.ascontiguousarray(
                np.broadcast_to(Dv[g * NH:(g + 1) * NH].reshape(1, NH), (128, NH))),
            scat_in=scat, gown_in=gown, gpar_in=gpar,
            fscat_in=(me * 128 + p_).reshape(128, 1).astype(np.uint32),
            **consts,
        )
        in_maps.append(im)

    res = run_bass_kernel_spmd(nc, in_maps, core_ids=list(range(8)))
    kernel._last_results = res.results
    out = np.zeros((B, T, D), np.float32)
    for k in range(8):
        b, g = k // 2, k % 2
        out[b, g * TH:(g + 1) * TH] = res.results[k]["y_out"]
    return out
